# version 04
# baseline (speedup 1.0000x reference)
import numpy as np
import ml_dtypes
from contextlib import ExitStack

import concourse.bass as bass
import concourse.mybir as mybir
from concourse.bass_utils import run_bass_kernel_spmd

F32 = mybir.dt.float32
BF16 = mybir.dt.bfloat16
AF = mybir.ActivationFunctionType
ALU = mybir.AluOpType
bf16 = ml_dtypes.bfloat16

B, T, C = 4, 4096, 2048
H, HKV, HD = 16, 4, 128
HID = 8192
NOWN = 2048
NTL = 16
NKT = 32
EPS = 1e-5
SCALE = 0.25
RSQN = float(1.0 / np.sqrt(HID))
NEG = -1e9


def _rope_tables():
    theta = 1.0 / 10000.0 ** (np.arange(0, HD, 2, dtype=np.float64) / HD)
    freqs = np.arange(T, dtype=np.float64)[:, None] * theta[None, :]
    cos = np.cos(freqs).astype(np.float32)
    sin = np.sin(freqs).astype(np.float32)
    return np.tile(cos, (1, 4)).copy(), np.tile(sin, (1, 4)).copy()


def _h128():
    i = np.arange(128)
    a = i[:, None] & i[None, :]
    pc = np.zeros((128, 128), np.int64)
    for b in range(7):
        pc += (a >> b) & 1
    return np.where(pc % 2 == 0, np.float32(1.0), np.float32(-1.0)).astype(bf16)


def _masks(p):
    m = np.zeros((NTL, 2, 128, 512), np.float32)
    tq = np.arange(128)
    tk = np.arange(128)
    for i in range(NTL):
        tg = 2 * i + p
        for r in range(2):
            c = 2 * i + r
            valid = (tg * 128 + tq[None, :]) >= (c * 128 + tk[:, None])
            blk = np.where(valid, np.float32(0.0), np.float32(NEG))
            m[i, r] = np.tile(blk, (1, 4))
    return m.astype(bf16)


def _own_rows(p):
    idx = []
    for i in range(NTL):
        tg = 2 * i + p
        idx.extend(range(tg * 128, (tg + 1) * 128))
    return np.array(idx)


class St:
    pass


def W(eng, st, sem, val=None):
    if val is None:
        if not isinstance(sem, tuple):
            return
        sem, val = sem
    if isinstance(val, tuple):
        sem, val = val
    if isinstance(val, int) and val > 0:
        eng.wait_ge(st.sem[sem], val)


def INC(st, instr, sem, k=1):
    instr.then_inc(st.sem[sem], k)
    st.cnt[sem] += k
    return st.cnt[sem]


DMA_SEMS = ("pre", "x0", "x1", "m0", "m1", "wa0", "wa1", "wp", "tbs",
            "wd0", "wd1", "t0", "t1", "tw0", "tw1",
            "gwk", "gw1", "gw3", "gwo", "gwq", "gbq", "gbo", "gw2", "gout",
            "gx0", "gx1", "gq0", "gq1", "gav0", "gav1", "gh0", "gh1")


def DMA(st, eng, key, out, in_, first=True, **kw):
    if first and st.cnt[key] > 0:
        eng.wait_ge(st.sem[key], st.cnt[key])
    return INC(st, eng.dma_start(out=out, in_=in_, **kw), key, 16)


def FR(st, sem):
    return (sem, st.cnt[sem])


def _barrier(st):
    blk = st.block
    tgt = dict(st.cnt)
    for en in ("sync", "tensor", "vector", "scalar", "gpsimd"):
        def mk(e, tgt=tgt):
            for s, v in tgt.items():
                if v > 0:
                    e.wait_ge(st.sem[s], v)
        getattr(blk, en)(mk)


def _ln_part1_v(st, v, xsrc):
    sb = st.sb
    v.drain()
    for b4 in range(4):
        v.bn_stats(sb["bn6"][:, b4 * 6:(b4 + 1) * 6], xsrc[:, b4 * 512:(b4 + 1) * 512])
    v.drain()
    v.bn_aggr(sb["mv"][:], sb["bn6"][:])
    v.drain()
    v.tensor_scalar_add(sb["vt"][:], sb["mv"][:, 1:2], EPS)
    v.drain()
    st.ln_vrec = INC(st, v.reciprocal(sb["vt"][:], sb["vt"][:]), "vv")


def _ln_part2_s(st, s):
    sb = st.sb
    W(s, st, "vv", st.ln_vrec)
    st.ln_esq = INC(st, s.activation(sb["rstd"][:], sb["vt"][:], AF.Sqrt), "ex")


def _ln_part3_v(st, v, xsrc, dst):
    sb = st.sb
    W(v, st, "ex", st.ln_esq)
    return INC(st, v.tensor_scalar(dst, xsrc, sb["mv"][:, 0:1], sb["rstd"][:],
                                   ALU.subtract, ALU.mult), "vv")


def _rope6(st, v, psrc, cos_t, sin_t, dst_bf):
    sb = st.sb
    pv = psrc.rearrange("p (h i two) -> p h i two", i=64, two=2)
    dv = dst_bf.rearrange("p (h i two) -> p h i two", i=64, two=2)
    a, b = pv[:, :, :, 0], pv[:, :, :, 1]
    ct = cos_t.rearrange("p (h i) -> p h i", i=64)
    sn = sin_t.rearrange("p (h i) -> p h i", i=64)
    r1 = sb["rt1"][:].rearrange("p (h i) -> p h i", i=64)
    r2 = sb["rt2"][:].rearrange("p (h i) -> p h i", i=64)
    v.drain()
    v.tensor_mul(r1, a, ct)
    v.tensor_mul(r2, b, sn)
    v.drain()
    v.tensor_sub(dv[:, :, :, 0], r1, r2)
    v.drain()
    v.tensor_mul(r1, a, sn)
    v.tensor_mul(r2, b, ct)
    v.drain()
    return v.tensor_add(dv[:, :, :, 1], r1, r2)


class TspPipe:

    NSTEP = 64

    def __init__(self, st):
        self.st = st
        self.i = 0
        self.c_tsp = [0] * self.NSTEP
        self.c_wr = [0] * self.NSTEP

    def _src_dst(self, i):
        d = self.st.d
        w = 0 if i < 32 else 1
        dc = (i % 32) // 2
        half = i % 2
        src = (d["w1b"] if w == 0 else d["w3b"])
        dst = (d["w1t"] if w == 0 else d["w3t"])
        return src, dst, dc, half

    def _emit_write(self, s, j):
        st = self.st
        src, dst, dc, half = self._src_dst(j)
        W(s, st, self.c_tsp[j])
        key = f"tw{j % 2}"
        out = dst[half * 32:(half + 1) * 32, :, dc * 128:(dc + 1) * 128]
        out = out.rearrange("hc p h -> p hc h")
        DMA(st, s, key, out,
            st.sb[f"tsp{j % 2}"][:, :].rearrange("p (hc h) -> p hc h", h=128))
        self.c_wr[j] = (key, st.cnt[key])

    def step(self):
        st = self.st
        if self.i >= self.NSTEP:
            return
        i = self.i
        self.i += 1
        blk = st.block

        @blk.scalar
        def _(s, i=i):
            src, dst, dc, half = self._src_dst(i)
            if i == 0:
                W(s, st, st.c_w1b)
            if i == 32:
                W(s, st, st.c_w3b)
            if i >= 2:
                W(s, st, self.c_wr[i - 2])
            key = f"t{i % 2}"
            DMA(st, s, key, st.sb[f"tsp{i % 2}"][:],
                src[half * 4096:(half + 1) * 4096, dc * 128:(dc + 1) * 128],
                transpose=True)
            self.c_tsp[i] = (key, st.cnt[key])
            if i >= 1:
                self._emit_write(s, i - 1)

    def finish(self):
        st = self.st
        blk = st.block
        while self.i < self.NSTEP:
            self.step()

        @blk.scalar
        def _(s):
            self._emit_write(s, self.NSTEP - 1)
        st.c_wt_done = [("tw0", st.cnt["tw0"]), ("tw1", st.cnt["tw1"])]


def _emit_preload(st):
    d, sb = st.d, st.sb
    blk = st.block

    @blk.sync
    def _(y):
        for k2, nm in enumerate(("identb", "identf", "h128b", "onesrow", "ones512",
                                 "onescol", "onesrowf")):
            src = "h128" if nm == "h128b" else nm
            DMA(st, y, "pre", sb[nm][:], d[src], first=(k2 == 0))
    st.c_preload = ("pre", st.cnt["pre"])


def _emit_gp_weights(st):
    d, sb = st.d, st.sb
    blk = st.block

    @blk.gpsimd
    def _(g):
        DMA(st, g, "gwk",
            sb["wkv"][:, :].rearrange("p (dc n) -> p dc n", n=1024)[:, :, 0:512],
            d["wk"].rearrange("(dc p) n -> p dc n", p=128))
        DMA(st, g, "gwk",
            sb["wkv"][:, :].rearrange("p (dc n) -> p dc n", n=1024)[:, :, 512:1024],
            d["wv"].rearrange("(dc p) n -> p dc n", p=128), first=False)
        DMA(st, g, "gwk", sb["bkv"][:, 0:512], d["bk"], first=False)
        DMA(st, g, "gwk", sb["bkv"][:, 512:1024], d["bv"], first=False)
        st.c_wkv = ("gwk", st.cnt["gwk"])
        st.c_w1b = ("gw1", DMA(st, g, "gw1", d["w1b"], d["w1"]))
        st.c_w3b = ("gw3", DMA(st, g, "gw3", d["w3b"], d["w3"]))
        for dc in range(16):
            DMA(st, g, "gwo", d["wob"][dc],
                d["wo"].rearrange("(hc p) (dc e) -> dc p hc e", p=128, e=128)[dc],
                first=(dc == 0))
        st.c_wob = ("gwo", st.cnt["gwo"])


def _emit_ph0(st):
    d, sb, ps = st.d, st.sb, st.ps
    blk = st.block

    ld_x = [0] * NKT
    v_xn = [0] * NKT
    m_tp4 = [[0] * 4 for _ in range(NKT)]
    e_xnf = [0] * NKT
    m_kv = [0] * NKT
    v_rope = [0] * NKT
    e_v = [0] * NKT
    m_ktp = [0] * NKT
    tp_free = [0, 0]

    for kt in range(NKT):
        i = kt % 2

        @blk.sync
        def _(y, kt=kt, i=i):
            if kt >= 2:
                W(y, st, "vv", v_rope[kt - 2])
            DMA(st, y, f"x{i}", sb[f"x{i}"][:], d["x"][kt * 128:(kt + 1) * 128, :])
            DMA(st, y, f"x{i}", sb[f"ckv{i}"][:],
                d["cos4kv"][kt * 128:(kt + 1) * 128, :], first=False)
            DMA(st, y, f"x{i}", sb[f"skv{i}"][:],
                d["sin4kv"][kt * 128:(kt + 1) * 128, :], first=False)
            ld_x[kt] = (f"x{i}", st.cnt[f"x{i}"])

        @blk.vector
        def _(v, kt=kt, i=i):
            W(v, st, ld_x[kt])
            _ln_part1_v(st, v, sb[f"x{i}"][:])

        @blk.scalar
        def _(s, kt=kt):
            _ln_part2_s(st, s)

        @blk.vector
        def _(v, kt=kt, i=i):
            if kt >= 2:
                W(v, st, "mm", m_tp4[kt - 2][3])
            v_xn[kt] = _ln_part3_v(st, v, sb[f"x{i}"][:], sb[f"xntm{i}"][:])

        for g4 in range(4):
            @blk.tensor
            def _(t, kt=kt, i=i, g4=g4):
                if kt == 0 and g4 == 0:
                    W(t, st, st.c_preload)
                if g4 == 0:
                    W(t, st, "vv", v_xn[kt])
                b = g4 % 2
                W(t, st, "mm", tp_free[b])
                last = None
                for q in range(4):
                    dcx = 4 * g4 + q
                    last = t.transpose(ps[f"tp{b}"][:, q * 128:(q + 1) * 128],
                                       sb[f"xntm{i}"][:, dcx * 128:(dcx + 1) * 128],
                                       sb["identb"][:])
                m_tp4[kt][g4] = INC(st, last, "mm")

            @blk.vector
            def _(v, kt=kt, i=i, g4=g4):
                if g4 == 0 and kt >= 2:
                    W(v, st, "mm", m_kv[kt - 2])
                b = g4 % 2
                W(v, st, "mm", m_tp4[kt][g4])
                tp_free[b] = ("vv", INC(st, v.tensor_copy(
                    sb[f"xnf{i}"][:, g4 * 512:(g4 + 1) * 512], ps[f"tp{b}"][:]), "vv"))
                if g4 == 3:
                    e_xnf[kt] = ("vv", st.cnt["vv"])

        @blk.tensor
        def _(t, kt=kt, i=i):
            W(t, st, e_xnf[kt])
            if kt == 0:
                W(t, st, st.c_wkv)
                W(t, st, st.c_preload)
            if kt >= 2:
                W(t, st, "vv", v_rope[kt - 2])
                W(t, st, e_v[kt - 2])
            t.matmul(ps[f"pk{i}"][:], sb["onesrow"][:], sb["bkv"][:, 0:512],
                     start=True, stop=False)
            t.matmul(ps[f"pv{i}"][:], sb["onesrow"][:], sb["bkv"][:, 512:1024],
                     start=True, stop=False)
            for dc in range(16):
                lt = sb[f"xnf{i}"][:, dc * 128:(dc + 1) * 128]
                t.matmul(ps[f"pk{i}"][:], lt, sb["wkv"][:, dc * 1024:dc * 1024 + 512],
                         start=False, stop=(dc == 15))
                last = t.matmul(ps[f"pv{i}"][:], lt,
                                sb["wkv"][:, dc * 1024 + 512:(dc + 1) * 1024],
                                start=False, stop=(dc == 15))
            m_kv[kt] = INC(st, last, "mm")

        @blk.vector
        def _(v, kt=kt, i=i):
            W(v, st, "mm", m_kv[kt])
            if kt >= 2:
                W(v, st, "mm", m_ktp[kt - 2])
            ins = _rope6(st, v, ps[f"pk{i}"][:], sb[f"ckv{i}"][:], sb[f"skv{i}"][:],
                         sb[f"ktm{i}"][:])
            v_rope[kt] = INC(st, ins, "vv")

        @blk.vector
        def _(v, kt=kt, i=i):
            W(v, st, "mm", m_kv[kt])
            e_v[kt] = ("vv", INC(st, v.tensor_copy(sb["v_tm"][:, kt * 512:(kt + 1) * 512],
                                                   ps[f"pv{i}"][:]), "vv"))

        @blk.tensor
        def _(t, kt=kt, i=i):
            W(t, st, "vv", v_rope[kt])
            b = kt % 2
            W(t, st, "mm", tp_free[b])
            last = None
            for j in range(4):
                last = t.transpose(ps[f"tp{b}"][:, j * 128:(j + 1) * 128],
                                   sb[f"ktm{i}"][:, j * 128:(j + 1) * 128],
                                   sb["identb"][:])
            m_ktp[kt] = INC(st, last, "mm")

        @blk.vector
        def _(v, kt=kt):
            b = kt % 2
            W(v, st, "mm", m_ktp[kt])
            last = None
            for j in range(4):
                last = v.tensor_copy(sb["k_fm"][:, j * T + kt * 128: j * T + (kt + 1) * 128],
                                     ps[f"tp{b}"][:, j * 128:(j + 1) * 128])
            tp_free[b] = ("vv", INC(st, last, "vv"))

        if kt >= 16:
            st.tsp.step()
            st.tsp.step()


def _emit_ph1(st):
    d, sb, ps = st.d, st.sb, st.ps
    blk = st.block

    @blk.gpsimd
    def _(g):
        DMA(st, g, "gbq", sb["bq"][:], d["bq"])
    c_bq = ("gbq", st.cnt["gbq"])

    tp_free = [0, 0]
    xnf_free = [0, 0]
    cos_free = [0, 0]
    qst_free = [0, 0]
    pq_free = [0, 0]
    x_free = [0, 0]
    xntm_free = [0, 0]
    spill_free = [0, 0]
    qfst_free = [0, 0]
    wq_free = 0

    for pa in range(2):
        @blk.gpsimd
        def _(g, pa=pa):
            if pa == 1:
                W(g, st, "mm", wq_free)
            DMA(st, g, "gwq",
                sb["wq"][:, :].rearrange("p (dc n) -> p dc n", n=1024),
                d["wq"].rearrange("(dc p) n -> p dc n", p=128)[:, :, pa * 1024:(pa + 1) * 1024])
        c_wq = ("gwq", st.cnt["gwq"])

        ld_xo = [0] * NTL
        v_xn = [0] * NTL
        m_q = [0] * NTL
        v_rope = [0] * NTL
        m_qtp = [0] * NTL

        for tl in range(NTL):
            i = tl % 2

            @blk.sync
            def _(y, tl=tl, i=i, pa=pa):
                W(y, st, "vv", cos_free[i])
                if pa == 0:
                    W(y, st, "vv", x_free[i])
                    DMA(st, y, f"x{i}", sb[f"xo{i}"][:],
                        d["xo"][tl * 128:(tl + 1) * 128, :])
                else:
                    W(y, st, "mm", xnf_free[i])
                    W(y, st, spill_free[i])
                    DMA(st, y, f"x{i}",
                        sb[f"xnf{i}"][:, :].rearrange("p (dc t) -> p dc t", t=128),
                        d["xnh"][:, :, tl * 128:(tl + 1) * 128].rearrange("dc p t -> p dc t"))
                DMA(st, y, f"x{i}", sb[f"co{i}"][:],
                    d["cos4own"][tl * 128:(tl + 1) * 128, :], first=False)
                DMA(st, y, f"x{i}", sb[f"so{i}"][:],
                    d["sin4own"][tl * 128:(tl + 1) * 128, :], first=False)
                ld_xo[tl] = (f"x{i}", st.cnt[f"x{i}"])

            if pa == 0:
                @blk.vector
                def _(v, tl=tl, i=i):
                    W(v, st, ld_xo[tl])
                    _ln_part1_v(st, v, sb[f"xo{i}"][:])

                @blk.scalar
                def _(s, tl=tl):
                    _ln_part2_s(st, s)

                @blk.vector
                def _(v, tl=tl, i=i):
                    W(v, st, "mm", xntm_free[i])
                    v_xn[tl] = _ln_part3_v(st, v, sb[f"xo{i}"][:], sb[f"xntm{i}"][:])
                    x_free[i] = v_xn[tl]

                if not hasattr(st, "_tpg"):
                    st._tpg = {}
                for g4 in range(4):
                    @blk.tensor
                    def _(t, tl=tl, i=i, g4=g4):
                        if g4 == 0:
                            W(t, st, "vv", v_xn[tl])
                        b = g4 % 2
                        W(t, st, "mm", tp_free[b])
                        last = None
                        for q in range(4):
                            dcx = 4 * g4 + q
                            last = t.transpose(ps[f"tq{b}"][:, q * 128:(q + 1) * 128],
                                               sb[f"xntm{i}"][:, dcx * 128:(dcx + 1) * 128],
                                               sb["identb"][:])
                        st._tpg[(tl, g4)] = INC(st, last, "mm")
                        if g4 == 3:
                            xntm_free[i] = st._tpg[(tl, g4)]

                    @blk.vector
                    def _(v, tl=tl, i=i, g4=g4):
                        if g4 == 0:
                            W(v, st, "mm", xnf_free[i])
                            W(v, st, spill_free[i])
                        b = g4 % 2
                        W(v, st, "mm", st._tpg[(tl, g4)])
                        tp_free[b] = ("vv", INC(st, v.tensor_copy(
                            sb[f"xnf{i}"][:, g4 * 512:(g4 + 1) * 512], ps[f"tq{b}"][:]), "vv"))
                        if g4 == 3:
                            st._exnf = ("vv", st.cnt["vv"])

                @blk.gpsimd
                def _(g, tl=tl, i=i):
                    W(g, st, st._exnf)
                    DMA(st, g, f"gx{i}",
                        d["xnh"][:, :, tl * 128:(tl + 1) * 128].rearrange("dc p t -> p dc t"),
                        sb[f"xnf{i}"][:, :].rearrange("p (dc t) -> p dc t", t=128))
                    spill_free[i] = (f"gx{i}", st.cnt[f"gx{i}"])
                st._exnf_t = st._exnf
            @blk.tensor
            def _(t, tl=tl, i=i, pa=pa):
                if pa == 0:
                    W(t, st, st._exnf_t)
                else:
                    W(t, st, ld_xo[tl])
                if tl == 0:
                    W(t, st, c_wq)
                    W(t, st, c_bq)
                pb = tl % 2
                W(t, st, "vv", pq_free[pb])
                t.matmul(ps[f"pq{pb}"][:, 0:512], sb["onesrow"][:],
                         sb["bq"][:, pa * 1024:pa * 1024 + 512], start=True, stop=False)
                t.matmul(ps[f"pq{pb}"][:, 512:1024], sb["onesrow"][:],
                         sb["bq"][:, pa * 1024 + 512:(pa + 1) * 1024], start=True, stop=False)
                for dc in range(16):
                    lt = sb[f"xnf{i}"][:, dc * 128:(dc + 1) * 128]
                    t.matmul(ps[f"pq{pb}"][:, 0:512], lt,
                             sb["wq"][:, dc * 1024:dc * 1024 + 512],
                             start=False, stop=(dc == 15))
                    last = t.matmul(ps[f"pq{pb}"][:, 512:1024], lt,
                                    sb["wq"][:, dc * 1024 + 512:(dc + 1) * 1024],
                                    start=False, stop=(dc == 15))
                m_q[tl] = INC(st, last, "mm")
                xnf_free[i] = m_q[tl]

            @blk.vector
            def _(v, tl=tl, i=i):
                W(v, st, "mm", m_q[tl])
                W(v, st, "mm", qst_free[i])
                pb = tl % 2
                for hb in range(2):
                    ins = _rope6(st, v, ps[f"pq{pb}"][:, hb * 512:(hb + 1) * 512],
                                 sb[f"co{i}"][:], sb[f"so{i}"][:],
                                 sb[f"qst{i}"][:, hb * 512:(hb + 1) * 512])
                v_rope[tl] = INC(st, ins, "vv")
                pq_free[pb] = v_rope[tl]
                cos_free[i] = v_rope[tl]

            for hb in range(2):
                @blk.tensor
                def _(t, tl=tl, i=i, hb=hb):
                    if hb == 0:
                        W(t, st, "vv", v_rope[tl])
                    b = hb % 2
                    W(t, st, "mm", tp_free[b])
                    last = None
                    for q in range(4):
                        last = t.transpose(
                            ps[f"tq{b}"][:, q * 128:(q + 1) * 128],
                            sb[f"qst{i}"][:, (hb * 4 + q) * 128:(hb * 4 + q + 1) * 128],
                            sb["identb"][:])
                    st._tpg[(tl, "q", hb)] = INC(st, last, "mm")
                    if hb == 1:
                        m_qtp[tl] = st.cnt["mm"]
                        qst_free[i] = m_qtp[tl]

                @blk.vector
                def _(v, tl=tl, pa=pa, hb=hb, i=i):
                    b = hb % 2
                    W(v, st, "mm", st._tpg[(tl, "q", hb)])
                    if hb == 0:
                        W(v, st, qfst_free[i])
                    tp_free[b] = ("vv", INC(st, v.tensor_copy(
                        sb[f"qfst{i}"][:, hb * 512:(hb + 1) * 512], ps[f"tq{b}"][:]), "vv"))
                    if hb == 1:
                        st._eqf = ("vv", st.cnt["vv"])

            @blk.gpsimd
            def _(g, tl=tl, pa=pa, i=i):
                W(g, st, st._eqf)
                dst = bass.AP(st.dram["qh"], (pa * 8) * 128 * NOWN + tl * 128,
                              [[NOWN, 128], [128 * NOWN, 8], [1, 128]])
                DMA(st, g, f"gq{i}", dst,
                    sb[f"qfst{i}"][:, :].rearrange("p (h t) -> p h t", t=128))
                qfst_free[i] = (f"gq{i}", st.cnt[f"gq{i}"])

            st.tsp.step()
        wq_free = m_q[NTL - 1]
    st.tsp.finish()


def _emit_ph2(st):
    d, sb, ps = st.d, st.sb, st.ps
    blk = st.block

    s_free = [0, 0, 0]
    p_free = [0, 0, 0]
    pav_free = [0, 0]
    pl_free = [0, 0]
    linv_free = 0
    bc_free = 0
    stg_free = [0, 0]
    mask_free = [0, 0]
    qtl_free = [0, 0]
    ld_mask = [0] * NTL

    if st.preload_tables:
        @blk.scalar
        def _(s):
            s.preload_act_table(AF.Exp)

    gidx = 0
    for tl in range(NTL):
        mbuf = tl % 2

        @blk.sync
        def _(y, tl=tl, mbuf=mbuf):
            W(y, st, "mm", mask_free[mbuf])
            W(y, st, "mm", qtl_free[mbuf])
            if tl == 0:
                W(y, st, ("gq0", st.cnt["gq0"]))
                W(y, st, ("gq1", st.cnt["gq1"]))
            DMA(st, y, f"m{mbuf}",
                sb[f"mask{mbuf}"][:, :].rearrange("p (r n) -> p r n", n=512),
                d["maskt"][tl].rearrange("r p n -> p r n"))
            DMA(st, y, f"m{mbuf}",
                sb[f"qtl{mbuf}"][:, :].rearrange("p (h t) -> p h t", t=128),
                d["qh"][:, :, tl * 128:(tl + 1) * 128].rearrange("h p t -> p h t"),
                first=False)
            ld_mask[tl] = (f"m{mbuf}", st.cnt[f"m{mbuf}"])

        nch = 2 * tl + 2
        for j in range(4):
            ab = gidx % 2
            m_sc = [0] * nch
            e_exp = [0] * nch
            m_av = [0] * nch
            p_free0 = list(p_free)

            def emit_lav(c):
                pb = c % 3

                @blk.tensor
                def _(t, j=j, c=c, pb=pb, ab=ab, nch=nch):
                    W(t, st, "ex", e_exp[c])
                    t.matmul(ps["plp"][32 * ab:32 * ab + 1, :], sb["onescol"][:], sb[f"P{pb}"][:],
                             start=(c == 0), stop=(c == nch - 1))
                    last = t.matmul(ps[f"pav{ab}"][:],
                                    sb["v_tm"][:, c * 512 + j * 128: c * 512 + (j + 1) * 128],
                                    sb[f"P{pb}"][:],
                                    start=(c == 0), stop=(c == nch - 1))
                    m_av[c] = INC(st, last, "mm")
                    p_free[pb] = m_av[c]

            ex0 = st.cnt["ex"]
            for c in range(nch):
                e_exp[c] = ex0 + c + 1

            for c in range(nch):
                sbk = c % 3
                pb = c % 3

                @blk.tensor
                def _(t, tl=tl, j=j, c=c, sbk=sbk, pb=pb, ab=ab, mbuf=mbuf, nch=nch):
                    W(t, st, "ex", s_free[sbk])
                    if c >= 3:
                        W(t, st, "mm", m_av[c - 3])
                    else:
                        W(t, st, "mm", p_free[pb])
                        if c == 0:
                            W(t, st, "vv", pav_free[ab])
                            W(t, st, "vv", pl_free[ab])
                    if c >= 2 * tl:
                        W(t, st, ld_mask[tl])
                    if c == 0:
                        W(t, st, ld_mask[tl])
                    lt = sb["k_fm"][:, j * T + c * 128: j * T + (c + 1) * 128]
                    ride = (c >= 2 * tl)
                    for g in range(4):
                        h = g * 4 + j
                        last = t.matmul(ps[f"S{sbk}"][:, g * 128:(g + 1) * 128], lt,
                                        sb[f"qtl{mbuf}"][:, h * 128:(h + 1) * 128],
                                        start=(g == 0), stop=(g == 3 and not ride))
                    if ride:
                        r = c - 2 * tl
                        last = t.matmul(ps[f"S{sbk}"][:], sb["identb"][:],
                                        sb[f"mask{mbuf}"][:, r * 512:(r + 1) * 512],
                                        start=False, stop=True)
                    m_sc[c] = INC(st, last, "mm")
                    if ride and c == nch - 1:
                        mask_free[mbuf] = m_sc[c]
                        if j == 3:
                            qtl_free[mbuf] = m_sc[c]
                s_free[sbk] = e_exp[c]

                if c >= 2:
                    emit_lav(c - 2)
            if nch >= 2:
                emit_lav(nch - 2)
            emit_lav(nch - 1)

            @blk.scalar
            def _(s, nch=nch, ab=ab):
                for c in range(nch):
                    sbk = c % 3
                    pb = c % 3
                    s.wait_ge(st.sem["mm"], m_sc[c])
                    if c >= 3:
                        s.wait_ge(st.sem["mm"], m_av[c - 3])
                    else:
                        if p_free0[pb] > 0:
                            s.wait_ge(st.sem["mm"], p_free0[pb])
                    got = INC(st, s.activation(sb[f"P{pb}"][:], ps[f"S{sbk}"][:],
                                               AF.Exp, scale=SCALE), "ex")
                    assert got == e_exp[c], (got, e_exp[c])

            @blk.vector
            def _(v, ab=ab, nch=nch):
                W(v, st, "mm", m_av[nch - 1])
                W(v, st, "mm", linv_free)
                st.c_li = INC(st, v.reciprocal(sb["linv"][:], ps["plp"][32 * ab:32 * ab + 1, :]), "vv")
                pl_free[ab] = st.c_li

            @blk.tensor
            def _(t):
                W(t, st, "vv", st.c_li)
                W(t, st, "vv", bc_free)
                st.c_bc = INC(st, t.matmul(ps["pbc"][:], sb["onesrowf"][:], sb["linv"][:],
                                           start=True, stop=True), "mm")
            linv_free = st.c_bc

            stg = gidx % 2

            @blk.vector
            def _(v, ab=ab, stg=stg):
                W(v, st, "mm", st.c_bc)
                v.tensor_copy(sb["bc_sb"][:], ps["pbc"][:])
                v.drain()
                W(v, st, stg_free[stg])
                st.c_av = INC(st, v.tensor_mul(sb[f"avst{stg}"][:], ps[f"pav{ab}"][:],
                                               sb["bc_sb"][:]), "vv")
                pav_free[ab] = st.c_av
            bc_free = st.c_av

            @blk.gpsimd
            def _(g, tl=tl, j=j, stg=stg):
                W(g, st, "vv", st.c_av)
                dst = bass.AP(st.dram["avh"],
                              (j * 128 * NOWN) + tl * 128,
                              [[NOWN, 128], [4 * 128 * NOWN, 4], [1, 128]])
                DMA(st, g, f"gav{stg}", dst,
                    sb[f"avst{stg}"][:, :].rearrange("p (g4 t) -> p g4 t", t=128))
                stg_free[stg] = (f"gav{stg}", st.cnt[f"gav{stg}"])
            gidx += 1


class W2Prep:

    def __init__(self, st):
        self.st = st
        self.na = 0
        self.nb = 0
        self.v_bfly = [0] * 16
        self.g_out = [0] * 16
        self.tpw_free = [0, 0]
        self.ld = [0] * 16

    def part_a(self, dc):
        st = self.st
        d, sb, ps = st.d, st.sb, st.ps
        blk = st.block
        self.na = dc + 1
        m_tpw = [0] * 16

        @blk.sync
        def _(y, dc=dc):
            if dc >= 1:
                W(y, st, "vv", self.v_bfly[dc - 1])
            DMA(st, y, "wp", sb["wpB"][:], d["w2"][dc * 128:(dc + 1) * 128, :])
            self.ld[dc] = ("wp", st.cnt["wp"])

        for grp in range(16):
            @blk.tensor
            def _(t, dc=dc, grp=grp):
                if grp == 0:
                    W(t, st, self.ld[dc])
                    if dc >= 1:
                        W(t, st, "vv", self.v_bfly[dc - 1])
                b = grp % 2
                W(t, st, "mm", self.tpw_free[b])
                last = None
                for q in range(4):
                    hcx = 4 * grp + q
                    last = t.transpose(ps[f"tpw{b}"][:, q * 128:(q + 1) * 128],
                                       sb["wpB"][:, hcx * 128:(hcx + 1) * 128],
                                       sb["identf"][:])
                m_tpw[grp] = INC(st, last, "mm")

            @blk.vector
            def _(v, dc=dc, grp=grp):
                b = grp % 2
                W(v, st, "mm", m_tpw[grp])
                self.tpw_free[b] = ("vv", INC(st, v.tensor_copy(
                    sb["wpA"][:, grp * 512:(grp + 1) * 512], ps[f"tpw{b}"][:]), "vv"))

        cp_done = ("vv", st.cnt["vv"])

        @blk.gpsimd
        def _(g, dc=dc):
            W(g, st, cp_done)
            cur, oth = sb["wpA"], sb["wpB"]
            for stg in range(4):
                bit = 1 << stg
                hi = 64 // (2 * bit)
                lo = bit
                cv = cur[:, :].rearrange("p (h t l e) -> p h t l e", h=hi, t=2, l=lo)
                ov = oth[:, :].rearrange("p (h t l e) -> p h t l e", h=hi, t=2, l=lo)
                g.drain()
                g.tensor_add(ov[:, :, 0], cv[:, :, 0], cv[:, :, 1])
                last = g.tensor_sub(ov[:, :, 1], cv[:, :, 0], cv[:, :, 1])
                cur, oth = oth, cur
            self.g_bfly = INC(st, last, "gv")

        @blk.vector
        def _(v, dc=dc):
            W(v, st, "gv", self.g_bfly)
            cur, oth = sb["wpA"], sb["wpB"]
            for stg in range(4, 6):
                bit = 1 << stg
                hi = 64 // (2 * bit)
                lo = bit
                cv = cur[:, :].rearrange("p (h t l e) -> p h t l e", h=hi, t=2, l=lo)
                dstt = sb["w2xb"] if stg == 5 else oth
                ov = dstt[:, :].rearrange("p (h t l e) -> p h t l e", h=hi, t=2, l=lo)
                v.drain()
                v.tensor_add(ov[:, :, 0], cv[:, :, 0], cv[:, :, 1])
                last = v.tensor_sub(ov[:, :, 1], cv[:, :, 0], cv[:, :, 1])
                cur, oth = oth, cur
            self.v_bfly[dc] = INC(st, last, "vv")

    def part_b(self, dc):
        st = self.st
        d, sb, ps = st.d, st.sb, st.ps
        blk = st.block
        self.nb = dc + 1
        m_h = [0] * 16
        e_w = [0] * 16

        ex0 = st.cnt["ex"]
        for s512 in range(16):
            e_w[s512] = ex0 + s512 + 1

        for s512 in range(16):
            b = s512 % 2

            @blk.tensor
            def _(t, dc=dc, s512=s512, b=b):
                if s512 == 0:
                    W(t, st, "vv", self.v_bfly[dc])
                W(t, st, "mm", self.tpw_free[b])
                m_h[s512] = INC(st, t.matmul(ps[f"tpw{b}"][:], sb["h128b"][:],
                                             sb["w2xb"][:, s512 * 512:(s512 + 1) * 512],
                                             start=True, stop=True), "mm")
            self.tpw_free[b] = ("ex", e_w[s512])

        @blk.scalar
        def _(s, dc=dc):
            for s512 in range(16):
                b = s512 % 2
                if s512 == 0 and dc >= 1:
                    W(s, st, self.g_out[dc - 1])
                s.wait_ge(st.sem["mm"], m_h[s512])
                got = INC(st, s.activation(sb["w2xo"][:, s512 * 512:(s512 + 1) * 512],
                                           ps[f"tpw{b}"][:], AF.Copy, scale=RSQN), "ex")
                assert got == e_w[s512], (got, e_w[s512])

        @blk.gpsimd
        def _(g, dc=dc):
            W(g, st, "ex", e_w[15])
            DMA(st, g, "gw2", d["w2x"][dc], sb["w2xo"][:])
            self.g_out[dc] = ("gw2", st.cnt["gw2"])


def _emit_ph3a(st):
    d, sb, ps = st.d, st.sb, st.ps
    blk = st.block

    @blk.sync
    def _(y):
        DMA(st, y, "pre",
            sb["xnall"][:, :].rearrange("p (dc t) -> p dc t", t=NOWN),
            d["xnh"].rearrange("dc p t -> p dc t"))
    c_xnall = ("pre", st.cnt["pre"])

    if st.preload_tables:
        @blk.scalar
        def _(s):
            s.preload_act_table(AF.Silu)

    ld_w = [0] * 64
    m_g = [[0] * 4 for _ in range(64)]
    m_u = [[0] * 4 for _ in range(64)]
    e_sg = [[0] * 4 for _ in range(64)]
    v_h = [[0] * 4 for _ in range(64)]
    g_spill = [0] * 64
    w2 = st.w2prep

    for hc in range(64):
        i = hc % 2

        @blk.sync
        def _(y, hc=hc, i=i):
            if hc == 0:
                for cw in st.c_wt_done:
                    W(y, st, cw)
            if hc >= 2:
                W(y, st, "mm", m_u[hc - 2][3])
            DMA(st, y, f"wa{i}", sb[f"w1tb{i}"][:], d["w1t"][hc])
            DMA(st, y, f"wa{i}", sb[f"w3tb{i}"][:], d["w3t"][hc], first=False)
            ld_w[hc] = (f"wa{i}", st.cnt[f"wa{i}"])

        ex0 = st.cnt["ex"]
        vv0 = st.cnt["vv"]
        for tsub in range(4):
            e_sg[hc][tsub] = ex0 + tsub + 1
            v_h[hc][tsub] = vv0 + tsub + 1

        for tsub in range(4):
            pb = tsub % 2

            @blk.tensor
            def _(t, hc=hc, i=i, tsub=tsub, pb=pb):
                if hc == 0 and tsub == 0:
                    W(t, st, c_xnall)
                W(t, st, ld_w[hc])
                if tsub >= 2:
                    W(t, st, "ex", e_sg[hc][tsub - 2])
                    W(t, st, "vv", v_h[hc][tsub - 2])
                elif hc >= 1:
                    W(t, st, "ex", e_sg[hc - 1][tsub + 2])
                    W(t, st, "vv", v_h[hc - 1][tsub + 2])
                for dc in range(16):
                    last = t.matmul(ps[f"pg{pb}"][:], sb[f"w1tb{i}"][:, dc * 128:(dc + 1) * 128],
                                    sb["xnall"][:, dc * NOWN + tsub * 512: dc * NOWN + (tsub + 1) * 512],
                                    start=(dc == 0), stop=(dc == 15))
                m_g[hc][tsub] = INC(st, last, "mm")
                for dc in range(16):
                    last = t.matmul(ps[f"pu{pb}"][:], sb[f"w3tb{i}"][:, dc * 128:(dc + 1) * 128],
                                    sb["xnall"][:, dc * NOWN + tsub * 512: dc * NOWN + (tsub + 1) * 512],
                                    start=(dc == 0), stop=(dc == 15))
                m_u[hc][tsub] = INC(st, last, "mm")

        @blk.scalar
        def _(s, hc=hc):
            for tsub in range(4):
                pb = tsub % 2
                s.wait_ge(st.sem["mm"], m_g[hc][tsub])
                if tsub >= 2:
                    s.wait_ge(st.sem["vv"], v_h[hc][tsub - 2])
                elif hc >= 1:
                    s.wait_ge(st.sem["vv"], v_h[hc - 1][tsub + 2])
                got = INC(st, s.activation(sb[f"sg{pb}"][:], ps[f"pg{pb}"][:],
                                           AF.Silu), "ex")
                assert got == e_sg[hc][tsub], (got, e_sg[hc][tsub])

        @blk.vector
        def _(v, hc=hc, i=i):
            for tsub in range(4):
                pb = tsub % 2
                v.wait_ge(st.sem["ex"], e_sg[hc][tsub])
                v.wait_ge(st.sem["mm"], m_u[hc][tsub])
                if tsub == 0 and hc >= 2:
                    W(v, st, g_spill[hc - 2])
                got = INC(st, v.tensor_mul(
                    sb[f"hst{i}"][:, tsub * 512:(tsub + 1) * 512],
                    sb[f"sg{pb}"][:], ps[f"pu{pb}"][:]), "vv")
                assert got == v_h[hc][tsub], (got, v_h[hc][tsub])

        @blk.gpsimd
        def _(g, hc=hc, i=i):
            W(g, st, "vv", v_h[hc][3])
            DMA(st, g, f"gh{i}", d["hidh"][hc], sb[f"hst{i}"][:])
            g_spill[hc] = (f"gh{i}", st.cnt[f"gh{i}"])

        if hc % 4 == 1 and hc // 4 < 16:
            w2.part_a(hc // 4)
        if hc % 4 == 3:
            w2.part_b(hc // 4)
    while w2.na < 16:
        w2.part_a(w2.na)
        w2.part_b(w2.nb)
    while w2.nb < 16:
        w2.part_b(w2.nb)


def _emit_ph3b(st):
    d, sb, ps = st.d, st.sb, st.ps
    blk = st.block

    @blk.gpsimd
    def _(g):
        DMA(st, g, "gbo", sb["bo"][:], d["bo"])
    c_bo = ("gbo", st.cnt["gbo"])

    ld_tb = [0] * 4
    ld_wd = [0] * 64
    m_o = [0] * 64
    e_o = [0] * 64
    m_tr = [0] * 64
    v_add = [0] * 64
    g_out = [0] * 4

    for tb in range(4):
        @blk.sync
        def _(y, tb=tb):
            if tb > 0:
                W(y, st, "vv", v_add[tb * 16 - 1])
                W(y, st, g_out[tb - 1])
                W(y, st, "mm", m_o[tb * 16 - 1])
            DMA(st, y, "tbs",
                sb["hidtb"][:, :].rearrange("p (hc t) -> p hc t", t=512),
                d["hidh"][:, :, tb * 512:(tb + 1) * 512].rearrange("hc p t -> p hc t"))
            DMA(st, y, "tbs",
                sb["avtb"][:, :].rearrange("p (hc t) -> p hc t", t=512),
                d["avh"][:, :, tb * 512:(tb + 1) * 512].rearrange("hc p t -> p hc t"),
                first=False)
            for q in range(4):
                tl = 4 * tb + q
                DMA(st, y, "tbs", sb[f"xst{q}"][:],
                    d["xo"][tl * 128:(tl + 1) * 128, :], first=False)
            ld_tb[tb] = ("tbs", st.cnt["tbs"])

        for dc in range(16):
            k = tb * 16 + dc
            i = dc % 2

            @blk.sync
            def _(y, tb=tb, dc=dc, k=k, i=i):
                if k >= 2:
                    W(y, st, "mm", m_o[k - 2])
                if k == 0:
                    W(y, st, ("gw2", st.cnt["gw2"]))
                    W(y, st, st.c_wob)
                DMA(st, y, f"wd{i}", sb[f"w2xd{i}"][:], d["w2x"][dc])
                DMA(st, y, f"wd{i}", sb[f"wod{i}"][:], d["wob"][dc], first=False)
                ld_wd[k] = (f"wd{i}", st.cnt[f"wd{i}"])

            @blk.tensor
            def _(t, tb=tb, dc=dc, k=k, i=i):
                W(t, st, ld_tb[tb])
                W(t, st, ld_wd[k])
                if k == 0:
                    W(t, st, c_bo)
                ob = dc % 2
                if k >= 2:
                    W(t, st, e_o[k - 2])
                t.matmul(ps[f"po{ob}"][:], sb["bo"][:, dc * 128:(dc + 1) * 128],
                         sb["ones512"][:], start=True, stop=False)
                for hc in range(16):
                    t.matmul(ps[f"po{ob}"][:], sb[f"wod{i}"][:, hc * 128:(hc + 1) * 128],
                             sb["avtb"][:, hc * 512:(hc + 1) * 512],
                             start=False, stop=False)
                for hc in range(64):
                    last = t.matmul(ps[f"po{ob}"][:], sb[f"w2xd{i}"][:, hc * 128:(hc + 1) * 128],
                                    sb["hidtb"][:, hc * 512:(hc + 1) * 512],
                                    start=False, stop=(hc == 63))
                m_o[k] = INC(st, last, "mm")

            @blk.vector
            def _(v, k=k, dc=dc):
                ob = dc % 2
                W(v, st, "mm", m_o[k])
                if k >= 2:
                    W(v, st, "mm", m_tr[k - 2])
                e_o[k] = ("vv", INC(st, v.tensor_copy(sb[f"pev{ob}"][:], ps[f"po{ob}"][:]), "vv"))

            @blk.tensor
            def _(t, k=k, dc=dc):
                ob = dc % 2
                W(t, st, e_o[k])
                if k >= 2:
                    W(t, st, "vv", v_add[k - 2])
                last = None
                for q in range(4):
                    last = t.transpose(ps[f"ptr{ob}"][:, q * 128:(q + 1) * 128],
                                       sb[f"pev{ob}"][:, q * 128:(q + 1) * 128],
                                       sb["identf"][:])
                m_tr[k] = INC(st, last, "mm")

            @blk.vector
            def _(v, k=k, dc=dc, tb=tb):
                ob = dc % 2
                W(v, st, "mm", m_tr[k])
                if tb > 0 and dc == 0:
                    W(v, st, g_out[tb - 1])
                last = None
                for q in range(4):
                    last = v.tensor_add(sb[f"ost{q}"][:, dc * 128:(dc + 1) * 128],
                                        ps[f"ptr{ob}"][:, q * 128:(q + 1) * 128],
                                        sb[f"xst{q}"][:, dc * 128:(dc + 1) * 128])
                v_add[k] = INC(st, last, "vv")

        @blk.gpsimd
        def _(g, tb=tb):
            W(g, st, "vv", v_add[tb * 16 + 15])
            for q in range(4):
                tl = 4 * tb + q
                DMA(st, g, "gout", d["out"][tl * 128:(tl + 1) * 128, :],
                    sb[f"ost{q}"][:], first=(q == 0))
            g_out[tb] = ("gout", st.cnt["gout"])


def _emit_final(st):
    blk = st.block
    tot = dict(st.cnt)

    @blk.gpsimd
    def _(g):
        for s in DMA_SEMS:
            if s.startswith("g") and tot[s] > 0:
                g.wait_ge(st.sem[s], tot[s])

    @blk.sync
    def _(y):
        for s in DMA_SEMS:
            if tot[s] > 0:
                y.wait_ge(st.sem[s], tot[s])


def build(upto=4, dbg=False, preload_tables=False):
    nc = bass.Bass("TRN2", target_bir_lowering=False)
    st = St()
    st.preload_tables = preload_tables
    st.nc = nc

    dram = {}

    def din(name, shape, dt):
        dram[name] = nc.dram_tensor(name, list(shape), dt, kind="ExternalInput")

    def dint(name, shape, dt):
        dram[name] = nc.dram_tensor(name, list(shape), dt, kind="Internal")

    def do_(name, shape, dt):
        dram[name] = nc.dram_tensor(name, list(shape), dt, kind="ExternalOutput")

    din("x", (T, C), F32)
    din("xo", (NOWN, C), F32)
    din("wq", (C, C), F32)
    din("wk", (C, 512), F32)
    din("wv", (C, 512), F32)
    din("wo", (C, C), F32)
    din("w1", (HID, C), F32)
    din("w2", (C, HID), F32)
    din("w3", (HID, C), F32)
    din("bq", (1, C), F32)
    din("bk", (1, 512), F32)
    din("bv", (1, 512), F32)
    din("bo", (1, C), F32)
    din("identb", (128, 128), BF16)
    din("identf", (128, 128), F32)
    din("h128", (128, 128), BF16)
    din("onesrow", (1, 128), BF16)
    din("ones512", (1, 512), BF16)
    din("onescol", (128, 1), BF16)
    din("onesrowf", (1, 128), F32)
    din("cos4kv", (T, 256), F32)
    din("sin4kv", (T, 256), F32)
    din("cos4own", (NOWN, 256), F32)
    din("sin4own", (NOWN, 256), F32)
    din("maskt", (NTL, 2, 128, 512), BF16)

    dsc = do_ if dbg else dint
    dint("w1b", (HID, C), BF16)
    dint("w3b", (HID, C), BF16)
    dint("wob", (16, 128, 16 * 128), BF16)
    dsc("w1t", (64, 128, 16 * 128), BF16)
    dsc("w3t", (64, 128, 16 * 128), BF16)
    dsc("w2x", (16, 128, 64 * 128), BF16)
    dsc("xnh", (16, 128, NOWN), BF16)
    dsc("qh", (16, 128, NOWN), BF16)
    dsc("avh", (16, 128, NOWN), BF16)
    dsc("hidh", (64, 128, NOWN), BF16)

    do_("out", (NOWN, C), F32)
    if dbg:
        do_("kdump", (128, 4 * T), BF16)
        do_("vdump", (128, NKT * 512), BF16)

    st.dram = dram
    st.d = {k: v.ap() for k, v in dram.items()}

    with ExitStack() as ctx:
        st.ctx = ctx
        st.cnt = {}
        st.sb = {}
        st.ps = {}
        st.sem = {}
        for n in ("mm", "ex", "vv", "gv") + DMA_SEMS:
            st.sem[n] = ctx.enter_context(nc.semaphore(n))
            st.cnt[n] = 0
        st.block = ctx.enter_context(nc.Block())

        uid = [0]

        def SB(stack, name, shape, dt):
            uid[0] += 1
            st.sb[name] = stack.enter_context(
                nc.sbuf_tensor(f"s{uid[0]}_{name}", list(shape), dt))

        def PS(stack, name, shape, dt):
            uid[0] += 1
            st.ps[name] = stack.enter_context(
                nc.psum_tensor(f"p{uid[0]}_{name}", list(shape), dt))

        for nm, shp, dt in [("identb", (128, 128), BF16), ("identf", (128, 128), F32),
                            ("h128b", (128, 128), BF16), ("onesrow", (1, 128), BF16),
                            ("ones512", (1, 512), BF16), ("onescol", (128, 1), BF16),
                            ("onesrowf", (1, 128), F32)]:
            SB(ctx, nm, shp, dt)

        _emit_preload(st)
        st.tsp = TspPipe(st)

        with ExitStack() as ph012:
            for nm, shp, dt in [("k_fm", (128, 4 * T), BF16),
                                ("v_tm", (128, NKT * 512), BF16),
                                ("tsp0", (128, 4096), BF16),
                                ("tsp1", (128, 4096), BF16)]:
                SB(ph012, nm, shp, dt)
            with ExitStack() as p0:
                for nm, shp, dt in [("wkv", (128, 16 * 1024), BF16), ("bkv", (1, 1024), BF16),
                                    ("bn6", (128, 24), F32), ("mv", (128, 2), F32),
                                    ("vt", (128, 1), F32), ("rstd", (128, 1), F32),
                                    ("rt1", (128, 256), F32), ("rt2", (128, 256), F32)]:
                    SB(p0, nm, shp, dt)
                for i in range(2):
                    for nm, shp, dt in [(f"x{i}", (128, C), F32), (f"xntm{i}", (128, C), BF16),
                                        (f"xnf{i}", (128, C), BF16), (f"ktm{i}", (128, 512), BF16),
                                        (f"ckv{i}", (128, 256), F32), (f"skv{i}", (128, 256), F32)]:
                        SB(p0, nm, shp, dt)
                for nm, shp, dt in [("tp0", (128, 512), BF16), ("tp1", (128, 512), BF16),
                                    ("pk0", (128, 512), F32), ("pk1", (128, 512), F32),
                                    ("pv0", (128, 512), F32), ("pv1", (128, 512), F32)]:
                    PS(p0, nm, shp, dt)
                _emit_gp_weights(st)
                _emit_ph0(st)
            if upto >= 1:
                _barrier(st)
                with ExitStack() as p1:
                    for nm, shp, dt in [("wq", (128, 16 * 1024), BF16), ("bq", (1, C), BF16),
                                        ("bn6", (128, 24), F32), ("mv", (128, 2), F32),
                                        ("vt", (128, 1), F32), ("rstd", (128, 1), F32),
                                        ("rt1", (128, 256), F32), ("rt2", (128, 256), F32)]:
                        SB(p1, nm, shp, dt)
                    for i in range(2):
                        for nm, shp, dt in [(f"xo{i}", (128, C), F32), (f"xntm{i}", (128, C), BF16),
                                            (f"xnf{i}", (128, C), BF16), (f"qst{i}", (128, 1024), BF16),
                                            (f"qfst{i}", (128, 1024), BF16),
                                            (f"co{i}", (128, 256), F32), (f"so{i}", (128, 256), F32)]:
                            SB(p1, nm, shp, dt)
                    for nm, shp, dt in [("tq0", (128, 512), BF16), ("tq1", (128, 512), BF16),
                                        ("pq0", (128, 1024), F32), ("pq1", (128, 1024), F32)]:
                        PS(p1, nm, shp, dt)
                    _emit_ph1(st)
            if upto >= 2:
                _barrier(st)
                with ExitStack() as p2:
                    for nm, shp, dt in [("mask0", (128, 1024), BF16), ("mask1", (128, 1024), BF16),
                                        ("qtl0", (128, 2048), BF16), ("qtl1", (128, 2048), BF16),
                                        ("P0", (128, 512), BF16), ("P1", (128, 512), BF16),
                                        ("P2", (128, 512), BF16),
                                        ("linv", (1, 512), F32), ("bc_sb", (128, 512), F32),
                                        ("avst0", (128, 512), BF16), ("avst1", (128, 512), BF16)]:
                        SB(p2, nm, shp, dt)
                    for nm, shp, dt in [("S0", (128, 512), F32), ("S1", (128, 512), F32),
                                        ("S2", (128, 512), F32),
                                        ("pav0", (128, 512), F32), ("pav1", (128, 512), F32),
                                        ("plp", (64, 512), F32),
                                        ("pbc", (128, 512), F32)]:
                        PS(p2, nm, shp, dt)
                    _emit_ph2(st)
            if dbg:
                _barrier(st)

                @st.block.gpsimd
                def _(g):
                    DMA(st, g, "gout", st.d["kdump"], st.sb["k_fm"][:])
                    DMA(st, g, "gout", st.d["vdump"], st.sb["v_tm"][:], first=False)
        if upto >= 3:
            _barrier(st)
            with ExitStack() as p3:
                for nm, shp, dt in [("xnall", (128, 16 * NOWN), BF16),
                                    ("wpA", (128, HID), F32), ("wpB", (128, HID), F32),
                                    ("w2xb", (128, HID), BF16), ("w2xo", (128, HID), BF16),
                                    ("sg0", (128, 512), BF16), ("sg1", (128, 512), BF16),
                                    ("w1tb0", (128, 2048), BF16), ("w1tb1", (128, 2048), BF16),
                                    ("w3tb0", (128, 2048), BF16), ("w3tb1", (128, 2048), BF16),
                                    ("hst0", (128, 2048), BF16), ("hst1", (128, 2048), BF16)]:
                    SB(p3, nm, shp, dt)
                for nm, shp, dt in [("pg0", (128, 512), F32), ("pg1", (128, 512), F32),
                                    ("pu0", (128, 512), F32), ("pu1", (128, 512), F32),
                                    ("tpw0", (128, 512), F32), ("tpw1", (128, 512), F32)]:
                    PS(p3, nm, shp, dt)
                st.w2prep = W2Prep(st)
                _emit_ph3a(st)
        if upto >= 4:
            _barrier(st)
            with ExitStack() as p4:
                for nm, shp, dt in [("hidtb", (128, 64 * 512), BF16),
                                    ("avtb", (128, 16 * 512), BF16),
                                    ("w2xd0", (128, 64 * 128), BF16), ("w2xd1", (128, 64 * 128), BF16),
                                    ("wod0", (128, 16 * 128), BF16), ("wod1", (128, 16 * 128), BF16),
                                    ("pev0", (128, 512), F32), ("pev1", (128, 512), F32),
                                    ("bo", (1, C), BF16)]:
                    SB(p4, nm, shp, dt)
                for q in range(4):
                    SB(p4, f"ost{q}", (128, C), F32)
                    SB(p4, f"xst{q}", (128, C), F32)
                for nm, shp, dt in [("po0", (128, 512), F32), ("po1", (128, 512), F32),
                                    ("ptr0", (128, 512), F32), ("ptr1", (128, 512), F32)]:
                    PS(p4, nm, shp, dt)
                _emit_ph3b(st)
        _emit_final(st)

    return nc


def kernel(L0_attn_x, L0_attn_Wq, L0_attn_bq, L0_attn_Wk, L0_attn_bk,
           L0_attn_Wv, L0_attn_bv, L0_attn_Wo, L0_attn_bo,
           L0_ffn_W1, L0_ffn_W2, L0_ffn_W3):
    x = np.ascontiguousarray(np.asarray(L0_attn_x, np.float32))
    wq = np.ascontiguousarray(np.asarray(L0_attn_Wq, np.float32))
    wk = np.ascontiguousarray(np.asarray(L0_attn_Wk, np.float32))
    wv = np.ascontiguousarray(np.asarray(L0_attn_Wv, np.float32))
    wo = np.ascontiguousarray(np.asarray(L0_attn_Wo, np.float32))
    w1 = np.ascontiguousarray(np.asarray(L0_ffn_W1, np.float32))
    w2 = np.ascontiguousarray(np.asarray(L0_ffn_W2, np.float32))
    w3 = np.ascontiguousarray(np.asarray(L0_ffn_W3, np.float32))
    bq = np.ascontiguousarray(np.asarray(L0_attn_bq, np.float32).reshape(1, C))
    bk = np.ascontiguousarray(np.asarray(L0_attn_bk, np.float32).reshape(1, 512))
    bv = np.ascontiguousarray(np.asarray(L0_attn_bv, np.float32).reshape(1, 512))
    bo = np.ascontiguousarray(np.asarray(L0_attn_bo, np.float32).reshape(1, C))

    cos4, sin4 = _rope_tables()
    consts = dict(
        identb=np.eye(128, dtype=bf16),
        identf=np.eye(128, dtype=np.float32),
        h128=_h128(),
        onesrow=np.ones((1, 128), bf16),
        ones512=np.ones((1, 512), bf16),
        onescol=np.ones((128, 1), bf16),
        onesrowf=np.ones((1, 128), np.float32),
        cos4kv=cos4, sin4kv=sin4,
    )
    masks = [_masks(p) for p in range(2)]

    in_maps = []
    for core in range(8):
        s, p = core // 2, core % 2
        rows = _own_rows(p)
        im = dict(x=np.ascontiguousarray(x[s]),
                  xo=np.ascontiguousarray(x[s][rows]),
                  wq=wq, wk=wk, wv=wv, wo=wo, w1=w1, w2=w2, w3=w3,
                  bq=bq, bk=bk, bv=bv, bo=bo,
                  cos4own=np.ascontiguousarray(cos4[rows]),
                  sin4own=np.ascontiguousarray(sin4[rows]),
                  maskt=masks[p], **consts)
        in_maps.append(im)

    nc = build(upto=4)
    res = run_bass_kernel_spmd(nc, in_maps, core_ids=list(range(8)))

    out = np.empty((B, T, C), np.float32)
    for core in range(8):
        s, p = core // 2, core % 2
        o = np.asarray(res.results[core]["out"], np.float32)
        for i in range(NTL):
            tg = 2 * i + p
            out[s, tg * 128:(tg + 1) * 128, :] = o[i * 128:(i + 1) * 128, :]
    return out


# version 05 (final)
# speedup vs baseline: 1.0127x; 1.0127x over previous
import numpy as np
import ml_dtypes
from contextlib import ExitStack

import concourse.bass as bass
import concourse.mybir as mybir
from concourse.bass_utils import run_bass_kernel_spmd

F32 = mybir.dt.float32
BF16 = mybir.dt.bfloat16
AF = mybir.ActivationFunctionType
ALU = mybir.AluOpType
bf16 = ml_dtypes.bfloat16

B, T, C = 4, 4096, 2048
H, HKV, HD = 16, 4, 128
HID = 8192
NOWN = 2048
NTL = 16
NKT = 32
EPS = 1e-5
SCALE = 0.25
RSQN = float(1.0 / np.sqrt(HID))
NEG = -1e9


def _rope_tables():
    theta = 1.0 / 10000.0 ** (np.arange(0, HD, 2, dtype=np.float64) / HD)
    freqs = np.arange(T, dtype=np.float64)[:, None] * theta[None, :]
    cos = np.cos(freqs).astype(np.float32)
    sin = np.sin(freqs).astype(np.float32)
    return np.tile(cos, (1, 4)).copy(), np.tile(sin, (1, 4)).copy()


def _h128():
    i = np.arange(128)
    a = i[:, None] & i[None, :]
    pc = np.zeros((128, 128), np.int64)
    for b in range(7):
        pc += (a >> b) & 1
    return np.where(pc % 2 == 0, np.float32(1.0), np.float32(-1.0)).astype(bf16)


def _masks(p):
    m = np.zeros((NTL, 2, 128, 512), np.float32)
    tq = np.arange(128)
    tk = np.arange(128)
    for i in range(NTL):
        tg = 2 * i + p
        for r in range(2):
            c = 2 * i + r
            valid = (tg * 128 + tq[None, :]) >= (c * 128 + tk[:, None])
            blk = np.where(valid, np.float32(0.0), np.float32(NEG))
            m[i, r] = np.tile(blk, (1, 4))
    return m.astype(bf16)


def _own_rows(p):
    idx = []
    for i in range(NTL):
        tg = 2 * i + p
        idx.extend(range(tg * 128, (tg + 1) * 128))
    return np.array(idx)


class St:
    pass


def W(eng, st, sem, val=None):
    if val is None:
        if not isinstance(sem, tuple):
            return
        sem, val = sem
    if isinstance(val, tuple):
        sem, val = val
    if isinstance(val, int) and val > 0:
        eng.wait_ge(st.sem[sem], val)


def INC(st, instr, sem, k=1):
    instr.then_inc(st.sem[sem], k)
    st.cnt[sem] += k
    return st.cnt[sem]


DMA_SEMS = ("pre", "x0", "x1", "m0", "m1", "wa0", "wa1", "wp", "tbs",
            "wd0", "wd1", "t0", "t1", "tw0", "tw1",
            "gwk", "gw1", "gw3", "gwo", "gwq", "gbq", "gbo", "gw2", "gout",
            "gx0", "gx1", "gq0", "gq1", "gav0", "gav1", "gh0", "gh1")


def DMA(st, eng, key, out, in_, first=True, **kw):
    if first and st.cnt[key] > 0:
        eng.wait_ge(st.sem[key], st.cnt[key])
    return INC(st, eng.dma_start(out=out, in_=in_, **kw), key, 16)


def FR(st, sem):
    return (sem, st.cnt[sem])


def _barrier(st):
    blk = st.block
    tgt = dict(st.cnt)
    for en in ("sync", "tensor", "vector", "scalar", "gpsimd"):
        def mk(e, tgt=tgt):
            for s, v in tgt.items():
                if v > 0:
                    e.wait_ge(st.sem[s], v)
        getattr(blk, en)(mk)


def _ln_part1_v(st, v, xsrc):
    sb = st.sb
    v.drain()
    for b4 in range(4):
        v.bn_stats(sb["bn6"][:, b4 * 6:(b4 + 1) * 6], xsrc[:, b4 * 512:(b4 + 1) * 512])
    v.drain()
    v.bn_aggr(sb["mv"][:], sb["bn6"][:])
    v.drain()
    v.tensor_scalar_add(sb["vt"][:], sb["mv"][:, 1:2], EPS)
    v.drain()
    st.ln_vrec = INC(st, v.reciprocal(sb["vt"][:], sb["vt"][:]), "vv")


def _ln_part2_s(st, s):
    sb = st.sb
    W(s, st, "vv", st.ln_vrec)
    st.ln_esq = INC(st, s.activation(sb["rstd"][:], sb["vt"][:], AF.Sqrt), "ex")


def _ln_part3_v(st, v, xsrc, dst):
    sb = st.sb
    W(v, st, "ex", st.ln_esq)
    return INC(st, v.tensor_scalar(dst, xsrc, sb["mv"][:, 0:1], sb["rstd"][:],
                                   ALU.subtract, ALU.mult), "vv")


def _rope6(st, v, psrc, cos_t, sin_t, dst_bf):
    sb = st.sb
    pv = psrc.rearrange("p (h i two) -> p h i two", i=64, two=2)
    dv = dst_bf.rearrange("p (h i two) -> p h i two", i=64, two=2)
    a, b = pv[:, :, :, 0], pv[:, :, :, 1]
    ct = cos_t.rearrange("p (h i) -> p h i", i=64)
    sn = sin_t.rearrange("p (h i) -> p h i", i=64)
    r1 = sb["rt1"][:].rearrange("p (h i) -> p h i", i=64)
    r2 = sb["rt2"][:].rearrange("p (h i) -> p h i", i=64)
    v.drain()
    v.tensor_mul(r1, a, ct)
    v.tensor_mul(r2, b, sn)
    v.drain()
    v.tensor_sub(dv[:, :, :, 0], r1, r2)
    v.drain()
    v.tensor_mul(r1, a, sn)
    v.tensor_mul(r2, b, ct)
    v.drain()
    return v.tensor_add(dv[:, :, :, 1], r1, r2)


class TspPipe:

    NSTEP = 64

    def __init__(self, st):
        self.st = st
        self.i = 0
        self.c_tsp = [0] * self.NSTEP
        self.c_wr = [0] * self.NSTEP

    def _src_dst(self, i):
        d = self.st.d
        w = 0 if i < 32 else 1
        dc = (i % 32) // 2
        half = i % 2
        src = (d["w1b"] if w == 0 else d["w3b"])
        dst = (d["w1t"] if w == 0 else d["w3t"])
        return src, dst, dc, half

    def _emit_write(self, s, j):
        st = self.st
        src, dst, dc, half = self._src_dst(j)
        W(s, st, self.c_tsp[j])
        key = f"tw{j % 2}"
        out = dst[half * 32:(half + 1) * 32, :, dc * 128:(dc + 1) * 128]
        out = out.rearrange("hc p h -> p hc h")
        DMA(st, s, key, out,
            st.sb[f"tsp{j % 2}"][:, :].rearrange("p (hc h) -> p hc h", h=128))
        self.c_wr[j] = (key, st.cnt[key])

    def step(self):
        st = self.st
        if self.i >= self.NSTEP:
            return
        i = self.i
        self.i += 1
        blk = st.block

        @blk.scalar
        def _(s, i=i):
            src, dst, dc, half = self._src_dst(i)
            if i == 0:
                W(s, st, st.c_w1b)
            if i == 32:
                W(s, st, st.c_w3b)
            if i >= 2:
                W(s, st, self.c_wr[i - 2])
            key = f"t{i % 2}"
            DMA(st, s, key, st.sb[f"tsp{i % 2}"][:],
                src[half * 4096:(half + 1) * 4096, dc * 128:(dc + 1) * 128],
                transpose=True)
            self.c_tsp[i] = (key, st.cnt[key])
            if i >= 1:
                self._emit_write(s, i - 1)

    def finish(self):
        st = self.st
        blk = st.block
        while self.i < self.NSTEP:
            self.step()

        @blk.scalar
        def _(s):
            self._emit_write(s, self.NSTEP - 1)
        st.c_wt_done = [("tw0", st.cnt["tw0"]), ("tw1", st.cnt["tw1"])]


def _emit_preload(st):
    d, sb = st.d, st.sb
    blk = st.block

    @blk.sync
    def _(y):
        for k2, nm in enumerate(("identb", "identf", "h128b", "onesrow", "ones512",
                                 "onescol", "onesrowf")):
            src = "h128" if nm == "h128b" else nm
            DMA(st, y, "pre", sb[nm][:], d[src], first=(k2 == 0))
    st.c_preload = ("pre", st.cnt["pre"])


def _emit_gp_weights(st):
    d, sb = st.d, st.sb
    blk = st.block

    @blk.gpsimd
    def _(g):
        DMA(st, g, "gwk",
            sb["wkv"][:, :].rearrange("p (dc n) -> p dc n", n=1024)[:, :, 0:512],
            d["wk"].rearrange("(dc p) n -> p dc n", p=128))
        DMA(st, g, "gwk",
            sb["wkv"][:, :].rearrange("p (dc n) -> p dc n", n=1024)[:, :, 512:1024],
            d["wv"].rearrange("(dc p) n -> p dc n", p=128), first=False)
        DMA(st, g, "gwk", sb["bkv"][:, 0:512], d["bk"], first=False)
        DMA(st, g, "gwk", sb["bkv"][:, 512:1024], d["bv"], first=False)
        st.c_wkv = ("gwk", st.cnt["gwk"])
        st.c_w1b = ("gw1", DMA(st, g, "gw1", d["w1b"], d["w1"]))
        st.c_w3b = ("gw3", DMA(st, g, "gw3", d["w3b"], d["w3"]))
        for dc in range(16):
            DMA(st, g, "gwo", d["wob"][dc],
                d["wo"].rearrange("(hc p) (dc e) -> dc p hc e", p=128, e=128)[dc],
                first=(dc == 0))
        st.c_wob = ("gwo", st.cnt["gwo"])


def _emit_ph0(st):
    d, sb, ps = st.d, st.sb, st.ps
    blk = st.block

    ld_x = [0] * NKT
    v_xn = [0] * NKT
    m_tp4 = [[0] * 4 for _ in range(NKT)]
    e_xnf = [0] * NKT
    m_kv = [0] * NKT
    v_rope = [0] * NKT
    e_v = [0] * NKT
    m_ktp = [0] * NKT
    tp_free = [0, 0]

    for kt in range(NKT):
        i = kt % 2

        @blk.sync
        def _(y, kt=kt, i=i):
            if kt >= 2:
                W(y, st, "vv", v_rope[kt - 2])
            DMA(st, y, f"x{i}", sb[f"x{i}"][:], d["x"][kt * 128:(kt + 1) * 128, :])
            DMA(st, y, f"x{i}", sb[f"ckv{i}"][:],
                d["cos4kv"][kt * 128:(kt + 1) * 128, :], first=False)
            DMA(st, y, f"x{i}", sb[f"skv{i}"][:],
                d["sin4kv"][kt * 128:(kt + 1) * 128, :], first=False)
            ld_x[kt] = (f"x{i}", st.cnt[f"x{i}"])

        @blk.vector
        def _(v, kt=kt, i=i):
            W(v, st, ld_x[kt])
            _ln_part1_v(st, v, sb[f"x{i}"][:])

        @blk.scalar
        def _(s, kt=kt):
            _ln_part2_s(st, s)

        @blk.vector
        def _(v, kt=kt, i=i):
            if kt >= 2:
                W(v, st, "mm", m_tp4[kt - 2][3])
            v_xn[kt] = _ln_part3_v(st, v, sb[f"x{i}"][:], sb[f"xntm{i}"][:])

        for g4 in range(4):
            @blk.tensor
            def _(t, kt=kt, i=i, g4=g4):
                if kt == 0 and g4 == 0:
                    W(t, st, st.c_preload)
                if g4 == 0:
                    W(t, st, "vv", v_xn[kt])
                b = g4 % 2
                W(t, st, "mm", tp_free[b])
                last = None
                for q in range(4):
                    dcx = 4 * g4 + q
                    last = t.transpose(ps[f"tp{b}"][:, q * 128:(q + 1) * 128],
                                       sb[f"xntm{i}"][:, dcx * 128:(dcx + 1) * 128],
                                       sb["identb"][:])
                m_tp4[kt][g4] = INC(st, last, "mm")

            @blk.vector
            def _(v, kt=kt, i=i, g4=g4):
                if g4 == 0 and kt >= 2:
                    W(v, st, "mm", m_kv[kt - 2])
                b = g4 % 2
                W(v, st, "mm", m_tp4[kt][g4])
                tp_free[b] = ("vv", INC(st, v.tensor_copy(
                    sb[f"xnf{i}"][:, g4 * 512:(g4 + 1) * 512], ps[f"tp{b}"][:]), "vv"))
                if g4 == 3:
                    e_xnf[kt] = ("vv", st.cnt["vv"])

        @blk.tensor
        def _(t, kt=kt, i=i):
            W(t, st, e_xnf[kt])
            if kt == 0:
                W(t, st, st.c_wkv)
                W(t, st, st.c_preload)
            if kt >= 2:
                W(t, st, "vv", v_rope[kt - 2])
                W(t, st, e_v[kt - 2])
            t.matmul(ps[f"pk{i}"][:], sb["onesrow"][:], sb["bkv"][:, 0:512],
                     start=True, stop=False)
            t.matmul(ps[f"pv{i}"][:], sb["onesrow"][:], sb["bkv"][:, 512:1024],
                     start=True, stop=False)
            for dc in range(16):
                lt = sb[f"xnf{i}"][:, dc * 128:(dc + 1) * 128]
                t.matmul(ps[f"pk{i}"][:], lt, sb["wkv"][:, dc * 1024:dc * 1024 + 512],
                         start=False, stop=(dc == 15))
                last = t.matmul(ps[f"pv{i}"][:], lt,
                                sb["wkv"][:, dc * 1024 + 512:(dc + 1) * 1024],
                                start=False, stop=(dc == 15))
            m_kv[kt] = INC(st, last, "mm")

        @blk.vector
        def _(v, kt=kt, i=i):
            W(v, st, "mm", m_kv[kt])
            if kt >= 2:
                W(v, st, "mm", m_ktp[kt - 2])
            ins = _rope6(st, v, ps[f"pk{i}"][:], sb[f"ckv{i}"][:], sb[f"skv{i}"][:],
                         sb[f"ktm{i}"][:])
            v_rope[kt] = INC(st, ins, "vv")

        @blk.vector
        def _(v, kt=kt, i=i):
            W(v, st, "mm", m_kv[kt])
            e_v[kt] = ("vv", INC(st, v.tensor_copy(sb["v_tm"][:, kt * 512:(kt + 1) * 512],
                                                   ps[f"pv{i}"][:]), "vv"))

        @blk.tensor
        def _(t, kt=kt, i=i):
            W(t, st, "vv", v_rope[kt])
            b = kt % 2
            W(t, st, "mm", tp_free[b])
            last = None
            for j in range(4):
                last = t.transpose(ps[f"tp{b}"][:, j * 128:(j + 1) * 128],
                                   sb[f"ktm{i}"][:, j * 128:(j + 1) * 128],
                                   sb["identb"][:])
            m_ktp[kt] = INC(st, last, "mm")

        @blk.vector
        def _(v, kt=kt):
            b = kt % 2
            W(v, st, "mm", m_ktp[kt])
            last = None
            for j in range(4):
                last = v.tensor_copy(sb["k_fm"][:, j * T + kt * 128: j * T + (kt + 1) * 128],
                                     ps[f"tp{b}"][:, j * 128:(j + 1) * 128])
            tp_free[b] = ("vv", INC(st, last, "vv"))

        if kt >= 16:
            st.tsp.step()
            st.tsp.step()


def _emit_ph1(st):
    d, sb, ps = st.d, st.sb, st.ps
    blk = st.block

    @blk.gpsimd
    def _(g):
        DMA(st, g, "gbq", sb["bq"][:], d["bq"])
    c_bq = ("gbq", st.cnt["gbq"])

    tp_free = [0, 0]
    xnf_free = [0, 0]
    cos_free = [0, 0]
    qst_free = [0, 0]
    pq_free = [0, 0]
    x_free = [0, 0]
    xntm_free = [0, 0]
    spill_free = [0, 0]
    qfst_free = [0, 0]
    wq_free = 0

    for pa in range(2):
        @blk.gpsimd
        def _(g, pa=pa):
            if pa == 1:
                W(g, st, "mm", wq_free)
            DMA(st, g, "gwq",
                sb["wq"][:, :].rearrange("p (dc n) -> p dc n", n=1024),
                d["wq"].rearrange("(dc p) n -> p dc n", p=128)[:, :, pa * 1024:(pa + 1) * 1024])
        c_wq = ("gwq", st.cnt["gwq"])

        ld_xo = [0] * NTL
        v_xn = [0] * NTL
        m_q = [0] * NTL
        v_rope = [0] * NTL
        m_qtp = [0] * NTL

        for tl in range(NTL):
            i = tl % 2

            @blk.sync
            def _(y, tl=tl, i=i, pa=pa):
                W(y, st, "vv", cos_free[i])
                if pa == 0:
                    W(y, st, "vv", x_free[i])
                    DMA(st, y, f"x{i}", sb[f"xo{i}"][:],
                        d["xo"][tl * 128:(tl + 1) * 128, :])
                else:
                    W(y, st, "mm", xnf_free[i])
                    W(y, st, spill_free[i])
                    DMA(st, y, f"x{i}",
                        sb[f"xnf{i}"][:, :].rearrange("p (dc t) -> p dc t", t=128),
                        d["xnh"][:, :, tl * 128:(tl + 1) * 128].rearrange("dc p t -> p dc t"))
                DMA(st, y, f"x{i}", sb[f"co{i}"][:],
                    d["cos4own"][tl * 128:(tl + 1) * 128, :], first=False)
                DMA(st, y, f"x{i}", sb[f"so{i}"][:],
                    d["sin4own"][tl * 128:(tl + 1) * 128, :], first=False)
                ld_xo[tl] = (f"x{i}", st.cnt[f"x{i}"])

            if pa == 0:
                @blk.vector
                def _(v, tl=tl, i=i):
                    W(v, st, ld_xo[tl])
                    _ln_part1_v(st, v, sb[f"xo{i}"][:])

                @blk.scalar
                def _(s, tl=tl):
                    _ln_part2_s(st, s)

                @blk.vector
                def _(v, tl=tl, i=i):
                    W(v, st, "mm", xntm_free[i])
                    v_xn[tl] = _ln_part3_v(st, v, sb[f"xo{i}"][:], sb[f"xntm{i}"][:])
                    x_free[i] = v_xn[tl]

                if not hasattr(st, "_tpg"):
                    st._tpg = {}
                for g4 in range(4):
                    @blk.tensor
                    def _(t, tl=tl, i=i, g4=g4):
                        if g4 == 0:
                            W(t, st, "vv", v_xn[tl])
                        b = g4 % 2
                        W(t, st, "mm", tp_free[b])
                        last = None
                        for q in range(4):
                            dcx = 4 * g4 + q
                            last = t.transpose(ps[f"tq{b}"][:, q * 128:(q + 1) * 128],
                                               sb[f"xntm{i}"][:, dcx * 128:(dcx + 1) * 128],
                                               sb["identb"][:])
                        st._tpg[(tl, g4)] = INC(st, last, "mm")
                        if g4 == 3:
                            xntm_free[i] = st._tpg[(tl, g4)]

                    @blk.vector
                    def _(v, tl=tl, i=i, g4=g4):
                        if g4 == 0:
                            W(v, st, "mm", xnf_free[i])
                            W(v, st, spill_free[i])
                        b = g4 % 2
                        W(v, st, "mm", st._tpg[(tl, g4)])
                        tp_free[b] = ("vv", INC(st, v.tensor_copy(
                            sb[f"xnf{i}"][:, g4 * 512:(g4 + 1) * 512], ps[f"tq{b}"][:]), "vv"))
                        if g4 == 3:
                            st._exnf = ("vv", st.cnt["vv"])

                @blk.gpsimd
                def _(g, tl=tl, i=i):
                    W(g, st, st._exnf)
                    DMA(st, g, f"gx{i}",
                        d["xnh"][:, :, tl * 128:(tl + 1) * 128].rearrange("dc p t -> p dc t"),
                        sb[f"xnf{i}"][:, :].rearrange("p (dc t) -> p dc t", t=128))
                    spill_free[i] = (f"gx{i}", st.cnt[f"gx{i}"])
                st._exnf_t = st._exnf
            @blk.tensor
            def _(t, tl=tl, i=i, pa=pa):
                if pa == 0:
                    W(t, st, st._exnf_t)
                else:
                    W(t, st, ld_xo[tl])
                if tl == 0:
                    W(t, st, c_wq)
                    W(t, st, c_bq)
                pb = tl % 2
                W(t, st, "vv", pq_free[pb])
                t.matmul(ps[f"pq{pb}"][:, 0:512], sb["onesrow"][:],
                         sb["bq"][:, pa * 1024:pa * 1024 + 512], start=True, stop=False)
                t.matmul(ps[f"pq{pb}"][:, 512:1024], sb["onesrow"][:],
                         sb["bq"][:, pa * 1024 + 512:(pa + 1) * 1024], start=True, stop=False)
                for dc in range(16):
                    lt = sb[f"xnf{i}"][:, dc * 128:(dc + 1) * 128]
                    t.matmul(ps[f"pq{pb}"][:, 0:512], lt,
                             sb["wq"][:, dc * 1024:dc * 1024 + 512],
                             start=False, stop=(dc == 15))
                    last = t.matmul(ps[f"pq{pb}"][:, 512:1024], lt,
                                    sb["wq"][:, dc * 1024 + 512:(dc + 1) * 1024],
                                    start=False, stop=(dc == 15))
                m_q[tl] = INC(st, last, "mm")
                xnf_free[i] = m_q[tl]

            @blk.vector
            def _(v, tl=tl, i=i):
                W(v, st, "mm", m_q[tl])
                W(v, st, "mm", qst_free[i])
                pb = tl % 2
                for hb in range(2):
                    ins = _rope6(st, v, ps[f"pq{pb}"][:, hb * 512:(hb + 1) * 512],
                                 sb[f"co{i}"][:], sb[f"so{i}"][:],
                                 sb[f"qst{i}"][:, hb * 512:(hb + 1) * 512])
                v_rope[tl] = INC(st, ins, "vv")
                pq_free[pb] = v_rope[tl]
                cos_free[i] = v_rope[tl]

            for hb in range(2):
                @blk.tensor
                def _(t, tl=tl, i=i, hb=hb):
                    if hb == 0:
                        W(t, st, "vv", v_rope[tl])
                    b = hb % 2
                    W(t, st, "mm", tp_free[b])
                    last = None
                    for q in range(4):
                        last = t.transpose(
                            ps[f"tq{b}"][:, q * 128:(q + 1) * 128],
                            sb[f"qst{i}"][:, (hb * 4 + q) * 128:(hb * 4 + q + 1) * 128],
                            sb["identb"][:])
                    st._tpg[(tl, "q", hb)] = INC(st, last, "mm")
                    if hb == 1:
                        m_qtp[tl] = st.cnt["mm"]
                        qst_free[i] = m_qtp[tl]

                @blk.vector
                def _(v, tl=tl, pa=pa, hb=hb, i=i):
                    b = hb % 2
                    W(v, st, "mm", st._tpg[(tl, "q", hb)])
                    if hb == 0:
                        W(v, st, qfst_free[i])
                    tp_free[b] = ("vv", INC(st, v.tensor_copy(
                        sb[f"qfst{i}"][:, hb * 512:(hb + 1) * 512], ps[f"tq{b}"][:]), "vv"))
                    if hb == 1:
                        st._eqf = ("vv", st.cnt["vv"])

            @blk.gpsimd
            def _(g, tl=tl, pa=pa, i=i):
                W(g, st, st._eqf)
                dst = bass.AP(st.dram["qh"], (pa * 8) * 128 * NOWN + tl * 128,
                              [[NOWN, 128], [128 * NOWN, 8], [1, 128]])
                DMA(st, g, f"gq{i}", dst,
                    sb[f"qfst{i}"][:, :].rearrange("p (h t) -> p h t", t=128))
                qfst_free[i] = (f"gq{i}", st.cnt[f"gq{i}"])

            st.tsp.step()
        wq_free = m_q[NTL - 1]
    st.tsp.finish()


def _emit_ph2(st):
    d, sb, ps = st.d, st.sb, st.ps
    blk = st.block

    s_free = [0, 0, 0]
    p_free = [0, 0, 0]
    pav_free = [0, 0]
    pl_free = [0, 0]
    linv_free = 0
    bc_free = 0
    stg_free = [0, 0]
    mask_free = [0, 0]
    qtl_free = [0, 0]
    ld_mask = [0] * NTL

    if st.preload_tables:
        @blk.scalar
        def _(s):
            s.preload_act_table(AF.Exp)

    gidx = 0
    for tl in range(NTL):
        mbuf = tl % 2

        @blk.sync
        def _(y, tl=tl, mbuf=mbuf):
            W(y, st, "mm", mask_free[mbuf])
            W(y, st, "mm", qtl_free[mbuf])
            if tl == 0:
                W(y, st, ("gq0", st.cnt["gq0"]))
                W(y, st, ("gq1", st.cnt["gq1"]))
            DMA(st, y, f"m{mbuf}",
                sb[f"mask{mbuf}"][:, :].rearrange("p (r n) -> p r n", n=512),
                d["maskt"][tl].rearrange("r p n -> p r n"))
            DMA(st, y, f"m{mbuf}",
                sb[f"qtl{mbuf}"][:, :].rearrange("p (h t) -> p h t", t=128),
                d["qh"][:, :, tl * 128:(tl + 1) * 128].rearrange("h p t -> p h t"),
                first=False)
            ld_mask[tl] = (f"m{mbuf}", st.cnt[f"m{mbuf}"])

        nch = 2 * tl + 2
        for j in range(4):
            ab = gidx % 2
            m_sc = [0] * nch
            e_exp = [0] * nch
            m_av = [0] * nch
            p_free0 = list(p_free)

            def emit_lav(c):
                pb = c % 3

                @blk.tensor
                def _(t, j=j, c=c, pb=pb, ab=ab, nch=nch):
                    W(t, st, "ex", e_exp[c])
                    t.matmul(ps[f"pl{ab}"][:], sb["onescol"][:], sb[f"P{pb}"][:],
                             start=(c == 0), stop=(c == nch - 1))
                    last = t.matmul(ps[f"pav{ab}"][:],
                                    sb["v_tm"][:, c * 512 + j * 128: c * 512 + (j + 1) * 128],
                                    sb[f"P{pb}"][:],
                                    start=(c == 0), stop=(c == nch - 1))
                    m_av[c] = INC(st, last, "mm")
                    p_free[pb] = m_av[c]

            ex0 = st.cnt["ex"]
            for c in range(nch):
                e_exp[c] = ex0 + c + 1

            for c in range(nch):
                sbk = c % 3
                pb = c % 3

                @blk.tensor
                def _(t, tl=tl, j=j, c=c, sbk=sbk, pb=pb, ab=ab, mbuf=mbuf, nch=nch):
                    W(t, st, "ex", s_free[sbk])
                    if c >= 3:
                        W(t, st, "mm", m_av[c - 3])
                    else:
                        W(t, st, "mm", p_free[pb])
                        if c == 0:
                            W(t, st, "vv", pav_free[ab])
                            W(t, st, "vv", pl_free[ab])
                    if c >= 2 * tl:
                        W(t, st, ld_mask[tl])
                    if c == 0:
                        W(t, st, ld_mask[tl])
                    lt = sb["k_fm"][:, j * T + c * 128: j * T + (c + 1) * 128]
                    ride = (c >= 2 * tl)
                    for g in range(4):
                        h = g * 4 + j
                        last = t.matmul(ps[f"S{sbk}"][:, g * 128:(g + 1) * 128], lt,
                                        sb[f"qtl{mbuf}"][:, h * 128:(h + 1) * 128],
                                        start=(g == 0), stop=(g == 3 and not ride))
                    if ride:
                        r = c - 2 * tl
                        last = t.matmul(ps[f"S{sbk}"][:], sb["identb"][:],
                                        sb[f"mask{mbuf}"][:, r * 512:(r + 1) * 512],
                                        start=False, stop=True)
                    m_sc[c] = INC(st, last, "mm")
                    if ride and c == nch - 1:
                        mask_free[mbuf] = m_sc[c]
                        if j == 3:
                            qtl_free[mbuf] = m_sc[c]
                s_free[sbk] = e_exp[c]

                if c >= 2:
                    emit_lav(c - 2)
            if nch >= 2:
                emit_lav(nch - 2)
            emit_lav(nch - 1)

            @blk.scalar
            def _(s, nch=nch, ab=ab):
                for c in range(nch):
                    sbk = c % 3
                    pb = c % 3
                    s.wait_ge(st.sem["mm"], m_sc[c])
                    if c >= 3:
                        s.wait_ge(st.sem["mm"], m_av[c - 3])
                    else:
                        if p_free0[pb] > 0:
                            s.wait_ge(st.sem["mm"], p_free0[pb])
                    got = INC(st, s.activation(sb[f"P{pb}"][:], ps[f"S{sbk}"][:],
                                               AF.Exp, scale=SCALE), "ex")
                    assert got == e_exp[c], (got, e_exp[c])

            @blk.vector
            def _(v, ab=ab, nch=nch):
                W(v, st, "mm", m_av[nch - 1])
                W(v, st, "mm", linv_free)
                st.c_li = INC(st, v.reciprocal(sb["linv"][:], ps[f"pl{ab}"][:]), "vv")
                pl_free[ab] = st.c_li

            @blk.tensor
            def _(t):
                W(t, st, "vv", st.c_li)
                W(t, st, "vv", bc_free)
                st.c_bc = INC(st, t.matmul(ps["pbc"][:], sb["onesrowf"][:], sb["linv"][:],
                                           start=True, stop=True), "mm")
            linv_free = st.c_bc

            stg = gidx % 2

            @blk.vector
            def _(v, ab=ab, stg=stg):
                W(v, st, "mm", st.c_bc)
                v.tensor_copy(sb["bc_sb"][:], ps["pbc"][:])
                v.drain()
                W(v, st, stg_free[stg])
                st.c_av = INC(st, v.tensor_mul(sb[f"avst{stg}"][:], ps[f"pav{ab}"][:],
                                               sb["bc_sb"][:]), "vv")
                pav_free[ab] = st.c_av
            bc_free = st.c_av

            @blk.gpsimd
            def _(g, tl=tl, j=j, stg=stg):
                W(g, st, "vv", st.c_av)
                dst = bass.AP(st.dram["avh"],
                              (j * 128 * NOWN) + tl * 128,
                              [[NOWN, 128], [4 * 128 * NOWN, 4], [1, 128]])
                DMA(st, g, f"gav{stg}", dst,
                    sb[f"avst{stg}"][:, :].rearrange("p (g4 t) -> p g4 t", t=128))
                stg_free[stg] = (f"gav{stg}", st.cnt[f"gav{stg}"])
            gidx += 1


class W2Prep:

    def __init__(self, st):
        self.st = st
        self.na = 0
        self.nb = 0
        self.v_bfly = [0] * 16
        self.g_out = [0] * 16
        self.tpw_free = [0, 0]
        self.ld = [0] * 16

    def part_a(self, dc):
        st = self.st
        d, sb, ps = st.d, st.sb, st.ps
        blk = st.block
        self.na = dc + 1
        m_tpw = [0] * 16

        @blk.sync
        def _(y, dc=dc):
            if dc >= 1:
                W(y, st, "vv", self.v_bfly[dc - 1])
            DMA(st, y, "wp", sb["wpB"][:], d["w2"][dc * 128:(dc + 1) * 128, :])
            self.ld[dc] = ("wp", st.cnt["wp"])

        for grp in range(16):
            @blk.tensor
            def _(t, dc=dc, grp=grp):
                if grp == 0:
                    W(t, st, self.ld[dc])
                    if dc >= 1:
                        W(t, st, "vv", self.v_bfly[dc - 1])
                b = grp % 2
                W(t, st, "mm", self.tpw_free[b])
                last = None
                for q in range(4):
                    hcx = 4 * grp + q
                    last = t.transpose(ps[f"tpw{b}"][:, q * 128:(q + 1) * 128],
                                       sb["wpB"][:, hcx * 128:(hcx + 1) * 128],
                                       sb["identf"][:])
                m_tpw[grp] = INC(st, last, "mm")

            @blk.vector
            def _(v, dc=dc, grp=grp):
                b = grp % 2
                W(v, st, "mm", m_tpw[grp])
                self.tpw_free[b] = ("vv", INC(st, v.tensor_copy(
                    sb["wpA"][:, grp * 512:(grp + 1) * 512], ps[f"tpw{b}"][:]), "vv"))

        cp_done = ("vv", st.cnt["vv"])

        @blk.gpsimd
        def _(g, dc=dc):
            W(g, st, cp_done)
            cur, oth = sb["wpA"], sb["wpB"]
            for stg in range(4):
                bit = 1 << stg
                hi = 64 // (2 * bit)
                lo = bit
                cv = cur[:, :].rearrange("p (h t l e) -> p h t l e", h=hi, t=2, l=lo)
                ov = oth[:, :].rearrange("p (h t l e) -> p h t l e", h=hi, t=2, l=lo)
                g.drain()
                g.tensor_add(ov[:, :, 0], cv[:, :, 0], cv[:, :, 1])
                last = g.tensor_sub(ov[:, :, 1], cv[:, :, 0], cv[:, :, 1])
                cur, oth = oth, cur
            self.g_bfly = INC(st, last, "gv")

        @blk.vector
        def _(v, dc=dc):
            W(v, st, "gv", self.g_bfly)
            cur, oth = sb["wpA"], sb["wpB"]
            for stg in range(4, 6):
                bit = 1 << stg
                hi = 64 // (2 * bit)
                lo = bit
                cv = cur[:, :].rearrange("p (h t l e) -> p h t l e", h=hi, t=2, l=lo)
                dstt = sb["w2xb"] if stg == 5 else oth
                ov = dstt[:, :].rearrange("p (h t l e) -> p h t l e", h=hi, t=2, l=lo)
                v.drain()
                v.tensor_add(ov[:, :, 0], cv[:, :, 0], cv[:, :, 1])
                last = v.tensor_sub(ov[:, :, 1], cv[:, :, 0], cv[:, :, 1])
                cur, oth = oth, cur
            self.v_bfly[dc] = INC(st, last, "vv")

    def part_b(self, dc):
        st = self.st
        d, sb, ps = st.d, st.sb, st.ps
        blk = st.block
        self.nb = dc + 1
        m_h = [0] * 16
        e_w = [0] * 16

        ex0 = st.cnt["ex"]
        for s512 in range(16):
            e_w[s512] = ex0 + s512 + 1

        for s512 in range(16):
            b = s512 % 2

            @blk.tensor
            def _(t, dc=dc, s512=s512, b=b):
                if s512 == 0:
                    W(t, st, "vv", self.v_bfly[dc])
                W(t, st, "mm", self.tpw_free[b])
                m_h[s512] = INC(st, t.matmul(ps[f"tpw{b}"][:], sb["h128b"][:],
                                             sb["w2xb"][:, s512 * 512:(s512 + 1) * 512],
                                             start=True, stop=True), "mm")
            self.tpw_free[b] = ("ex", e_w[s512])

        @blk.scalar
        def _(s, dc=dc):
            for s512 in range(16):
                b = s512 % 2
                if s512 == 0 and dc >= 1:
                    W(s, st, self.g_out[dc - 1])
                s.wait_ge(st.sem["mm"], m_h[s512])
                got = INC(st, s.activation(sb["w2xo"][:, s512 * 512:(s512 + 1) * 512],
                                           ps[f"tpw{b}"][:], AF.Copy, scale=RSQN), "ex")
                assert got == e_w[s512], (got, e_w[s512])

        @blk.gpsimd
        def _(g, dc=dc):
            W(g, st, "ex", e_w[15])
            DMA(st, g, "gw2", d["w2x"][dc], sb["w2xo"][:])
            self.g_out[dc] = ("gw2", st.cnt["gw2"])


def _emit_ph3a(st):
    d, sb, ps = st.d, st.sb, st.ps
    blk = st.block

    @blk.sync
    def _(y):
        DMA(st, y, "pre",
            sb["xnall"][:, :].rearrange("p (dc t) -> p dc t", t=NOWN),
            d["xnh"].rearrange("dc p t -> p dc t"))
    c_xnall = ("pre", st.cnt["pre"])

    if st.preload_tables:
        @blk.scalar
        def _(s):
            s.preload_act_table(AF.Silu)

    ld_w = [0] * 64
    m_g = [[0] * 4 for _ in range(64)]
    m_u = [[0] * 4 for _ in range(64)]
    e_sg = [[0] * 4 for _ in range(64)]
    v_h = [[0] * 4 for _ in range(64)]
    g_spill = [0] * 64
    w2 = st.w2prep

    for hc in range(64):
        i = hc % 2

        @blk.sync
        def _(y, hc=hc, i=i):
            if hc == 0:
                for cw in st.c_wt_done:
                    W(y, st, cw)
            if hc >= 2:
                W(y, st, "mm", m_u[hc - 2][3])
            DMA(st, y, f"wa{i}", sb[f"w1tb{i}"][:], d["w1t"][hc])
            DMA(st, y, f"wa{i}", sb[f"w3tb{i}"][:], d["w3t"][hc], first=False)
            ld_w[hc] = (f"wa{i}", st.cnt[f"wa{i}"])

        ex0 = st.cnt["ex"]
        vv0 = st.cnt["vv"]
        for tsub in range(4):
            e_sg[hc][tsub] = ex0 + tsub + 1
            v_h[hc][tsub] = vv0 + tsub + 1

        for tsub in range(4):
            pb = tsub % 2

            @blk.tensor
            def _(t, hc=hc, i=i, tsub=tsub, pb=pb):
                if hc == 0 and tsub == 0:
                    W(t, st, c_xnall)
                W(t, st, ld_w[hc])
                if tsub >= 2:
                    W(t, st, "ex", e_sg[hc][tsub - 2])
                    W(t, st, "vv", v_h[hc][tsub - 2])
                elif hc >= 1:
                    W(t, st, "ex", e_sg[hc - 1][tsub + 2])
                    W(t, st, "vv", v_h[hc - 1][tsub + 2])
                for dc in range(16):
                    last = t.matmul(ps[f"pg{pb}"][:], sb[f"w1tb{i}"][:, dc * 128:(dc + 1) * 128],
                                    sb["xnall"][:, dc * NOWN + tsub * 512: dc * NOWN + (tsub + 1) * 512],
                                    start=(dc == 0), stop=(dc == 15))
                m_g[hc][tsub] = INC(st, last, "mm")
                for dc in range(16):
                    last = t.matmul(ps[f"pu{pb}"][:], sb[f"w3tb{i}"][:, dc * 128:(dc + 1) * 128],
                                    sb["xnall"][:, dc * NOWN + tsub * 512: dc * NOWN + (tsub + 1) * 512],
                                    start=(dc == 0), stop=(dc == 15))
                m_u[hc][tsub] = INC(st, last, "mm")

        @blk.scalar
        def _(s, hc=hc):
            for tsub in range(4):
                pb = tsub % 2
                s.wait_ge(st.sem["mm"], m_g[hc][tsub])
                if tsub >= 2:
                    s.wait_ge(st.sem["vv"], v_h[hc][tsub - 2])
                elif hc >= 1:
                    s.wait_ge(st.sem["vv"], v_h[hc - 1][tsub + 2])
                got = INC(st, s.activation(sb[f"sg{pb}"][:], ps[f"pg{pb}"][:],
                                           AF.Silu), "ex")
                assert got == e_sg[hc][tsub], (got, e_sg[hc][tsub])

        @blk.vector
        def _(v, hc=hc, i=i):
            for tsub in range(4):
                pb = tsub % 2
                v.wait_ge(st.sem["ex"], e_sg[hc][tsub])
                v.wait_ge(st.sem["mm"], m_u[hc][tsub])
                if tsub == 0 and hc >= 2:
                    W(v, st, g_spill[hc - 2])
                got = INC(st, v.tensor_mul(
                    sb[f"hst{i}"][:, tsub * 512:(tsub + 1) * 512],
                    sb[f"sg{pb}"][:], ps[f"pu{pb}"][:]), "vv")
                assert got == v_h[hc][tsub], (got, v_h[hc][tsub])

        @blk.gpsimd
        def _(g, hc=hc, i=i):
            W(g, st, "vv", v_h[hc][3])
            DMA(st, g, f"gh{i}", d["hidh"][hc], sb[f"hst{i}"][:])
            g_spill[hc] = (f"gh{i}", st.cnt[f"gh{i}"])

        if hc % 4 == 1 and hc // 4 < 16:
            w2.part_a(hc // 4)
        if hc % 4 == 3:
            w2.part_b(hc // 4)
    while w2.na < 16:
        w2.part_a(w2.na)
        w2.part_b(w2.nb)
    while w2.nb < 16:
        w2.part_b(w2.nb)


def _emit_ph3b(st):
    d, sb, ps = st.d, st.sb, st.ps
    blk = st.block

    @blk.gpsimd
    def _(g):
        DMA(st, g, "gbo", sb["bo"][:], d["bo"])
    c_bo = ("gbo", st.cnt["gbo"])

    ld_tb = [0] * 4
    ld_wd = [0] * 64
    m_o = [0] * 64
    e_o = [0] * 64
    m_tr = [0] * 64
    v_add = [0] * 64
    g_out = [0] * 4

    for tb in range(4):
        @blk.sync
        def _(y, tb=tb):
            if tb > 0:
                W(y, st, "vv", v_add[tb * 16 - 1])
                W(y, st, g_out[tb - 1])
                W(y, st, "mm", m_o[tb * 16 - 1])
            DMA(st, y, "tbs",
                sb["hidtb"][:, :].rearrange("p (hc t) -> p hc t", t=512),
                d["hidh"][:, :, tb * 512:(tb + 1) * 512].rearrange("hc p t -> p hc t"))
            DMA(st, y, "tbs",
                sb["avtb"][:, :].rearrange("p (hc t) -> p hc t", t=512),
                d["avh"][:, :, tb * 512:(tb + 1) * 512].rearrange("hc p t -> p hc t"),
                first=False)
            for q in range(4):
                tl = 4 * tb + q
                DMA(st, y, "tbs", sb[f"xst{q}"][:],
                    d["xo"][tl * 128:(tl + 1) * 128, :], first=False)
            ld_tb[tb] = ("tbs", st.cnt["tbs"])

        for dc in range(16):
            k = tb * 16 + dc
            i = dc % 2

            @blk.sync
            def _(y, tb=tb, dc=dc, k=k, i=i):
                if k >= 2:
                    W(y, st, "mm", m_o[k - 2])
                if k == 0:
                    W(y, st, ("gw2", st.cnt["gw2"]))
                    W(y, st, st.c_wob)
                DMA(st, y, f"wd{i}", sb[f"w2xd{i}"][:], d["w2x"][dc])
                DMA(st, y, f"wd{i}", sb[f"wod{i}"][:], d["wob"][dc], first=False)
                ld_wd[k] = (f"wd{i}", st.cnt[f"wd{i}"])

            @blk.tensor
            def _(t, tb=tb, dc=dc, k=k, i=i):
                W(t, st, ld_tb[tb])
                W(t, st, ld_wd[k])
                if k == 0:
                    W(t, st, c_bo)
                ob = dc % 2
                if k >= 2:
                    W(t, st, e_o[k - 2])
                t.matmul(ps[f"po{ob}"][:], sb["bo"][:, dc * 128:(dc + 1) * 128],
                         sb["ones512"][:], start=True, stop=False)
                for hc in range(16):
                    t.matmul(ps[f"po{ob}"][:], sb[f"wod{i}"][:, hc * 128:(hc + 1) * 128],
                             sb["avtb"][:, hc * 512:(hc + 1) * 512],
                             start=False, stop=False)
                for hc in range(64):
                    last = t.matmul(ps[f"po{ob}"][:], sb[f"w2xd{i}"][:, hc * 128:(hc + 1) * 128],
                                    sb["hidtb"][:, hc * 512:(hc + 1) * 512],
                                    start=False, stop=(hc == 63))
                m_o[k] = INC(st, last, "mm")

            @blk.vector
            def _(v, k=k, dc=dc):
                ob = dc % 2
                W(v, st, "mm", m_o[k])
                if k >= 2:
                    W(v, st, "mm", m_tr[k - 2])
                e_o[k] = ("vv", INC(st, v.tensor_copy(sb[f"pev{ob}"][:], ps[f"po{ob}"][:]), "vv"))

            @blk.tensor
            def _(t, k=k, dc=dc):
                ob = dc % 2
                W(t, st, e_o[k])
                if k >= 2:
                    W(t, st, "vv", v_add[k - 2])
                last = None
                for q in range(4):
                    last = t.transpose(ps[f"ptr{ob}"][:, q * 128:(q + 1) * 128],
                                       sb[f"pev{ob}"][:, q * 128:(q + 1) * 128],
                                       sb["identf"][:])
                m_tr[k] = INC(st, last, "mm")

            @blk.vector
            def _(v, k=k, dc=dc, tb=tb):
                ob = dc % 2
                W(v, st, "mm", m_tr[k])
                if tb > 0 and dc == 0:
                    W(v, st, g_out[tb - 1])
                last = None
                for q in range(4):
                    last = v.tensor_add(sb[f"ost{q}"][:, dc * 128:(dc + 1) * 128],
                                        ps[f"ptr{ob}"][:, q * 128:(q + 1) * 128],
                                        sb[f"xst{q}"][:, dc * 128:(dc + 1) * 128])
                v_add[k] = INC(st, last, "vv")

        @blk.gpsimd
        def _(g, tb=tb):
            W(g, st, "vv", v_add[tb * 16 + 15])
            for q in range(4):
                tl = 4 * tb + q
                DMA(st, g, "gout", d["out"][tl * 128:(tl + 1) * 128, :],
                    sb[f"ost{q}"][:], first=(q == 0))
            g_out[tb] = ("gout", st.cnt["gout"])


def _emit_final(st):
    blk = st.block
    tot = dict(st.cnt)

    @blk.gpsimd
    def _(g):
        for s in DMA_SEMS:
            if s.startswith("g") and tot[s] > 0:
                g.wait_ge(st.sem[s], tot[s])

    @blk.sync
    def _(y):
        for s in DMA_SEMS:
            if tot[s] > 0:
                y.wait_ge(st.sem[s], tot[s])


def build(upto=4, dbg=False, preload_tables=False):
    nc = bass.Bass("TRN2", target_bir_lowering=False)
    st = St()
    st.preload_tables = preload_tables
    st.nc = nc

    dram = {}

    def din(name, shape, dt):
        dram[name] = nc.dram_tensor(name, list(shape), dt, kind="ExternalInput")

    def dint(name, shape, dt):
        dram[name] = nc.dram_tensor(name, list(shape), dt, kind="Internal")

    def do_(name, shape, dt):
        dram[name] = nc.dram_tensor(name, list(shape), dt, kind="ExternalOutput")

    din("x", (T, C), F32)
    din("xo", (NOWN, C), F32)
    din("wq", (C, C), F32)
    din("wk", (C, 512), F32)
    din("wv", (C, 512), F32)
    din("wo", (C, C), F32)
    din("w1", (HID, C), F32)
    din("w2", (C, HID), F32)
    din("w3", (HID, C), F32)
    din("bq", (1, C), F32)
    din("bk", (1, 512), F32)
    din("bv", (1, 512), F32)
    din("bo", (1, C), F32)
    din("identb", (128, 128), BF16)
    din("identf", (128, 128), F32)
    din("h128", (128, 128), BF16)
    din("onesrow", (1, 128), BF16)
    din("ones512", (1, 512), BF16)
    din("onescol", (128, 1), BF16)
    din("onesrowf", (1, 128), F32)
    din("cos4kv", (T, 256), F32)
    din("sin4kv", (T, 256), F32)
    din("cos4own", (NOWN, 256), F32)
    din("sin4own", (NOWN, 256), F32)
    din("maskt", (NTL, 2, 128, 512), BF16)

    dsc = do_ if dbg else dint
    dint("w1b", (HID, C), BF16)
    dint("w3b", (HID, C), BF16)
    dint("wob", (16, 128, 16 * 128), BF16)
    dsc("w1t", (64, 128, 16 * 128), BF16)
    dsc("w3t", (64, 128, 16 * 128), BF16)
    dsc("w2x", (16, 128, 64 * 128), BF16)
    dsc("xnh", (16, 128, NOWN), BF16)
    dsc("qh", (16, 128, NOWN), BF16)
    dsc("avh", (16, 128, NOWN), BF16)
    dsc("hidh", (64, 128, NOWN), BF16)

    do_("out", (NOWN, C), F32)
    if dbg:
        do_("kdump", (128, 4 * T), BF16)
        do_("vdump", (128, NKT * 512), BF16)

    st.dram = dram
    st.d = {k: v.ap() for k, v in dram.items()}

    with ExitStack() as ctx:
        st.ctx = ctx
        st.cnt = {}
        st.sb = {}
        st.ps = {}
        st.sem = {}
        for n in ("mm", "ex", "vv", "gv") + DMA_SEMS:
            st.sem[n] = ctx.enter_context(nc.semaphore(n))
            st.cnt[n] = 0
        st.block = ctx.enter_context(nc.Block())

        uid = [0]

        def SB(stack, name, shape, dt):
            uid[0] += 1
            st.sb[name] = stack.enter_context(
                nc.sbuf_tensor(f"s{uid[0]}_{name}", list(shape), dt))

        def PS(stack, name, shape, dt):
            uid[0] += 1
            st.ps[name] = stack.enter_context(
                nc.psum_tensor(f"p{uid[0]}_{name}", list(shape), dt))

        for nm, shp, dt in [("identb", (128, 128), BF16), ("identf", (128, 128), F32),
                            ("h128b", (128, 128), BF16), ("onesrow", (1, 128), BF16),
                            ("ones512", (1, 512), BF16), ("onescol", (128, 1), BF16),
                            ("onesrowf", (1, 128), F32)]:
            SB(ctx, nm, shp, dt)

        _emit_preload(st)
        st.tsp = TspPipe(st)

        with ExitStack() as ph012:
            for nm, shp, dt in [("k_fm", (128, 4 * T), BF16),
                                ("v_tm", (128, NKT * 512), BF16),
                                ("tsp0", (128, 4096), BF16),
                                ("tsp1", (128, 4096), BF16)]:
                SB(ph012, nm, shp, dt)
            with ExitStack() as p0:
                for nm, shp, dt in [("wkv", (128, 16 * 1024), BF16), ("bkv", (1, 1024), BF16),
                                    ("bn6", (128, 24), F32), ("mv", (128, 2), F32),
                                    ("vt", (128, 1), F32), ("rstd", (128, 1), F32),
                                    ("rt1", (128, 256), F32), ("rt2", (128, 256), F32)]:
                    SB(p0, nm, shp, dt)
                for i in range(2):
                    for nm, shp, dt in [(f"x{i}", (128, C), F32), (f"xntm{i}", (128, C), BF16),
                                        (f"xnf{i}", (128, C), BF16), (f"ktm{i}", (128, 512), BF16),
                                        (f"ckv{i}", (128, 256), F32), (f"skv{i}", (128, 256), F32)]:
                        SB(p0, nm, shp, dt)
                for nm, shp, dt in [("tp0", (128, 512), BF16), ("tp1", (128, 512), BF16),
                                    ("pk0", (128, 512), F32), ("pk1", (128, 512), F32),
                                    ("pv0", (128, 512), F32), ("pv1", (128, 512), F32)]:
                    PS(p0, nm, shp, dt)
                _emit_gp_weights(st)
                _emit_ph0(st)
            if upto >= 1:
                _barrier(st)
                with ExitStack() as p1:
                    for nm, shp, dt in [("wq", (128, 16 * 1024), BF16), ("bq", (1, C), BF16),
                                        ("bn6", (128, 24), F32), ("mv", (128, 2), F32),
                                        ("vt", (128, 1), F32), ("rstd", (128, 1), F32),
                                        ("rt1", (128, 256), F32), ("rt2", (128, 256), F32)]:
                        SB(p1, nm, shp, dt)
                    for i in range(2):
                        for nm, shp, dt in [(f"xo{i}", (128, C), F32), (f"xntm{i}", (128, C), BF16),
                                            (f"xnf{i}", (128, C), BF16), (f"qst{i}", (128, 1024), BF16),
                                            (f"qfst{i}", (128, 1024), BF16),
                                            (f"co{i}", (128, 256), F32), (f"so{i}", (128, 256), F32)]:
                            SB(p1, nm, shp, dt)
                    for nm, shp, dt in [("tq0", (128, 512), BF16), ("tq1", (128, 512), BF16),
                                        ("pq0", (128, 1024), F32), ("pq1", (128, 1024), F32)]:
                        PS(p1, nm, shp, dt)
                    _emit_ph1(st)
            if upto >= 2:
                _barrier(st)
                with ExitStack() as p2:
                    for nm, shp, dt in [("mask0", (128, 1024), BF16), ("mask1", (128, 1024), BF16),
                                        ("qtl0", (128, 2048), BF16), ("qtl1", (128, 2048), BF16),
                                        ("P0", (128, 512), BF16), ("P1", (128, 512), BF16),
                                        ("P2", (128, 512), BF16),
                                        ("linv", (1, 512), F32), ("bc_sb", (128, 512), F32),
                                        ("avst0", (128, 512), BF16), ("avst1", (128, 512), BF16)]:
                        SB(p2, nm, shp, dt)
                    for nm, shp, dt in [("S0", (128, 512), F32), ("S1", (128, 512), F32),
                                        ("S2", (128, 512), F32),
                                        ("pav0", (128, 512), F32), ("pav1", (128, 512), F32),
                                        ("pl0", (1, 512), F32), ("pl1", (1, 512), F32),
                                        ("pbc", (128, 512), F32)]:
                        PS(p2, nm, shp, dt)
                    _emit_ph2(st)
            if dbg:
                _barrier(st)

                @st.block.gpsimd
                def _(g):
                    DMA(st, g, "gout", st.d["kdump"], st.sb["k_fm"][:])
                    DMA(st, g, "gout", st.d["vdump"], st.sb["v_tm"][:], first=False)
        if upto >= 3:
            _barrier(st)
            with ExitStack() as p3:
                for nm, shp, dt in [("xnall", (128, 16 * NOWN), BF16),
                                    ("wpA", (128, HID), F32), ("wpB", (128, HID), F32),
                                    ("w2xb", (128, HID), BF16), ("w2xo", (128, HID), BF16),
                                    ("sg0", (128, 512), BF16), ("sg1", (128, 512), BF16),
                                    ("w1tb0", (128, 2048), BF16), ("w1tb1", (128, 2048), BF16),
                                    ("w3tb0", (128, 2048), BF16), ("w3tb1", (128, 2048), BF16),
                                    ("hst0", (128, 2048), BF16), ("hst1", (128, 2048), BF16)]:
                    SB(p3, nm, shp, dt)
                for nm, shp, dt in [("pg0", (128, 512), F32), ("pg1", (128, 512), F32),
                                    ("pu0", (128, 512), F32), ("pu1", (128, 512), F32),
                                    ("tpw0", (128, 512), F32), ("tpw1", (128, 512), F32)]:
                    PS(p3, nm, shp, dt)
                st.w2prep = W2Prep(st)
                _emit_ph3a(st)
        if upto >= 4:
            _barrier(st)
            with ExitStack() as p4:
                for nm, shp, dt in [("hidtb", (128, 64 * 512), BF16),
                                    ("avtb", (128, 16 * 512), BF16),
                                    ("w2xd0", (128, 64 * 128), BF16), ("w2xd1", (128, 64 * 128), BF16),
                                    ("wod0", (128, 16 * 128), BF16), ("wod1", (128, 16 * 128), BF16),
                                    ("pev0", (128, 512), F32), ("pev1", (128, 512), F32),
                                    ("bo", (1, C), BF16)]:
                    SB(p4, nm, shp, dt)
                for q in range(4):
                    SB(p4, f"ost{q}", (128, C), F32)
                    SB(p4, f"xst{q}", (128, C), F32)
                for nm, shp, dt in [("po0", (128, 512), F32), ("po1", (128, 512), F32),
                                    ("ptr0", (128, 512), F32), ("ptr1", (128, 512), F32)]:
                    PS(p4, nm, shp, dt)
                _emit_ph3b(st)
        _emit_final(st)

    return nc


def kernel(L0_attn_x, L0_attn_Wq, L0_attn_bq, L0_attn_Wk, L0_attn_bk,
           L0_attn_Wv, L0_attn_bv, L0_attn_Wo, L0_attn_bo,
           L0_ffn_W1, L0_ffn_W2, L0_ffn_W3):
    x = np.ascontiguousarray(np.asarray(L0_attn_x, np.float32))
    wq = np.ascontiguousarray(np.asarray(L0_attn_Wq, np.float32))
    wk = np.ascontiguousarray(np.asarray(L0_attn_Wk, np.float32))
    wv = np.ascontiguousarray(np.asarray(L0_attn_Wv, np.float32))
    wo = np.ascontiguousarray(np.asarray(L0_attn_Wo, np.float32))
    w1 = np.ascontiguousarray(np.asarray(L0_ffn_W1, np.float32))
    w2 = np.ascontiguousarray(np.asarray(L0_ffn_W2, np.float32))
    w3 = np.ascontiguousarray(np.asarray(L0_ffn_W3, np.float32))
    bq = np.ascontiguousarray(np.asarray(L0_attn_bq, np.float32).reshape(1, C))
    bk = np.ascontiguousarray(np.asarray(L0_attn_bk, np.float32).reshape(1, 512))
    bv = np.ascontiguousarray(np.asarray(L0_attn_bv, np.float32).reshape(1, 512))
    bo = np.ascontiguousarray(np.asarray(L0_attn_bo, np.float32).reshape(1, C))

    cos4, sin4 = _rope_tables()
    consts = dict(
        identb=np.eye(128, dtype=bf16),
        identf=np.eye(128, dtype=np.float32),
        h128=_h128(),
        onesrow=np.ones((1, 128), bf16),
        ones512=np.ones((1, 512), bf16),
        onescol=np.ones((128, 1), bf16),
        onesrowf=np.ones((1, 128), np.float32),
        cos4kv=cos4, sin4kv=sin4,
    )
    masks = [_masks(p) for p in range(2)]

    in_maps = []
    for core in range(8):
        s, p = core // 2, core % 2
        rows = _own_rows(p)
        im = dict(x=np.ascontiguousarray(x[s]),
                  xo=np.ascontiguousarray(x[s][rows]),
                  wq=wq, wk=wk, wv=wv, wo=wo, w1=w1, w2=w2, w3=w3,
                  bq=bq, bk=bk, bv=bv, bo=bo,
                  cos4own=np.ascontiguousarray(cos4[rows]),
                  sin4own=np.ascontiguousarray(sin4[rows]),
                  maskt=masks[p], **consts)
        in_maps.append(im)

    nc = build(upto=4)
    res = run_bass_kernel_spmd(nc, in_maps, core_ids=list(range(8)))

    out = np.empty((B, T, C), np.float32)
    for core in range(8):
        s, p = core // 2, core % 2
        o = np.asarray(res.results[core]["out"], np.float32)
        for i in range(NTL):
            tg = 2 * i + p
            out[s, tg * 128:(tg + 1) * 128, :] = o[i * 128:(i + 1) * 128, :]
    return out


# version 06
# speedup vs baseline: 1.0711x; 1.0576x over previous
import numpy as np
import ml_dtypes
from contextlib import ExitStack

import concourse.bass as bass
import concourse.mybir as mybir
from concourse.bass_utils import run_bass_kernel_spmd

F32 = mybir.dt.float32
BF16 = mybir.dt.bfloat16
AF = mybir.ActivationFunctionType
ALU = mybir.AluOpType
bf16 = ml_dtypes.bfloat16

B, T, C = 4, 4096, 2048
H, HKV, HD = 16, 4, 128
HID = 8192
NOWN = 2048
NTL = 16
NKT = 32
EPS = 1e-5
SCALE = 0.25
RSQN = float(1.0 / np.sqrt(HID))
NEG = -1e9


def _rope_tables():
    theta = 1.0 / 10000.0 ** (np.arange(0, HD, 2, dtype=np.float64) / HD)
    freqs = np.arange(T, dtype=np.float64)[:, None] * theta[None, :]
    cos = np.cos(freqs).astype(np.float32)
    sin = np.sin(freqs).astype(np.float32)
    return np.tile(cos, (1, 4)).copy(), np.tile(sin, (1, 4)).copy()


def _h128():
    i = np.arange(128)
    a = i[:, None] & i[None, :]
    pc = np.zeros((128, 128), np.int64)
    for b in range(7):
        pc += (a >> b) & 1
    return np.where(pc % 2 == 0, np.float32(1.0), np.float32(-1.0)).astype(bf16)


def _masks(p):
    m = np.zeros((NTL, 2, 128, 512), np.float32)
    tq = np.arange(128)
    tk = np.arange(128)
    for i in range(NTL):
        tg = 2 * i + p
        for r in range(2):
            c = 2 * i + r
            valid = (tg * 128 + tq[None, :]) >= (c * 128 + tk[:, None])
            blk = np.where(valid, np.float32(0.0), np.float32(NEG))
            m[i, r] = np.tile(blk, (1, 4))
    return m.astype(bf16)


def _own_rows(p):
    idx = []
    for i in range(NTL):
        tg = 2 * i + p
        idx.extend(range(tg * 128, (tg + 1) * 128))
    return np.array(idx)


class St:
    pass


def W(eng, st, sem, val=None):
    if val is None:
        if not isinstance(sem, tuple):
            return
        sem, val = sem
    if isinstance(val, tuple):
        sem, val = val
    if isinstance(val, int) and val > 0:
        eng.wait_ge(st.sem[sem], val)


def INC(st, instr, sem, k=1):
    instr.then_inc(st.sem[sem], k)
    st.cnt[sem] += k
    return st.cnt[sem]


DMA_SEMS = ("pre", "x0", "x1", "m0", "m1", "wa0", "wa1", "wp", "tbs",
            "wd0", "wd1", "t0", "t1", "tw0", "tw1",
            "gwk", "gw1", "gw3", "gwo", "gwq", "gbq", "gbo", "gw2", "gout",
            "gx0", "gx1", "gq0", "gq1", "gav0", "gav1", "gh0", "gh1")


def DMA(st, eng, key, out, in_, first=True, **kw):
    if first and st.cnt[key] > 0:
        eng.wait_ge(st.sem[key], st.cnt[key])
    return INC(st, eng.dma_start(out=out, in_=in_, **kw), key, 16)


def FR(st, sem):
    return (sem, st.cnt[sem])


def _barrier(st):
    blk = st.block
    tgt = dict(st.cnt)
    for en in ("sync", "tensor", "vector", "scalar", "gpsimd"):
        def mk(e, tgt=tgt):
            for s, v in tgt.items():
                if v > 0:
                    e.wait_ge(st.sem[s], v)
        getattr(blk, en)(mk)


def _ln_part1_v(st, v, xsrc):
    sb = st.sb
    v.drain()
    for b4 in range(4):
        v.bn_stats(sb["bn6"][:, b4 * 6:(b4 + 1) * 6], xsrc[:, b4 * 512:(b4 + 1) * 512])
    v.drain()
    v.bn_aggr(sb["mv"][:], sb["bn6"][:])
    v.drain()
    v.tensor_scalar_add(sb["vt"][:], sb["mv"][:, 1:2], EPS)
    v.drain()
    st.ln_vrec = INC(st, v.reciprocal(sb["vt"][:], sb["vt"][:]), "vv")


def _ln_part2_s(st, s):
    sb = st.sb
    W(s, st, "vv", st.ln_vrec)
    st.ln_esq = INC(st, s.activation(sb["rstd"][:], sb["vt"][:], AF.Sqrt), "ex")


def _ln_part3_v(st, v, xsrc, dst):
    sb = st.sb
    W(v, st, "ex", st.ln_esq)
    return INC(st, v.tensor_scalar(dst, xsrc, sb["mv"][:, 0:1], sb["rstd"][:],
                                   ALU.subtract, ALU.mult), "vv")


def _rope6(st, v, psrc, cos_t, sin_t, dst_bf):
    sb = st.sb
    pv = psrc.rearrange("p (h i two) -> p h i two", i=64, two=2)
    dv = dst_bf.rearrange("p (h i two) -> p h i two", i=64, two=2)
    a, b = pv[:, :, :, 0], pv[:, :, :, 1]
    ct = cos_t.rearrange("p (h i) -> p h i", i=64)
    sn = sin_t.rearrange("p (h i) -> p h i", i=64)
    r1 = sb["rt1"][:].rearrange("p (h i) -> p h i", i=64)
    r2 = sb["rt2"][:].rearrange("p (h i) -> p h i", i=64)
    v.drain()
    v.tensor_mul(r1, a, ct)
    v.tensor_mul(r2, b, sn)
    v.drain()
    v.tensor_sub(dv[:, :, :, 0], r1, r2)
    v.drain()
    v.tensor_mul(r1, a, sn)
    v.tensor_mul(r2, b, ct)
    v.drain()
    return v.tensor_add(dv[:, :, :, 1], r1, r2)


class TspPipe:

    NSTEP = 64

    def __init__(self, st):
        self.st = st
        self.i = 0
        self.c_tsp = [0] * self.NSTEP
        self.c_wr = [0] * self.NSTEP

    def _src_dst(self, i):
        d = self.st.d
        w = 0 if i < 32 else 1
        dc = (i % 32) // 2
        half = i % 2
        src = (d["w1b"] if w == 0 else d["w3b"])
        dst = (d["w1t"] if w == 0 else d["w3t"])
        return src, dst, dc, half

    def _emit_write(self, s, j):
        st = self.st
        src, dst, dc, half = self._src_dst(j)
        W(s, st, self.c_tsp[j])
        key = f"tw{j % 2}"
        out = dst[half * 32:(half + 1) * 32, :, dc * 128:(dc + 1) * 128]
        out = out.rearrange("hc p h -> p hc h")
        DMA(st, s, key, out,
            st.sb[f"tsp{j % 2}"][:, :].rearrange("p (hc h) -> p hc h", h=128))
        self.c_wr[j] = (key, st.cnt[key])

    def step(self):
        st = self.st
        if self.i >= self.NSTEP:
            return
        i = self.i
        self.i += 1
        blk = st.block

        @blk.scalar
        def _(s, i=i):
            src, dst, dc, half = self._src_dst(i)
            if i == 0:
                W(s, st, st.c_w1b)
            if i == 32:
                W(s, st, st.c_w3b)
            if i >= 2:
                W(s, st, self.c_wr[i - 2])
            key = f"t{i % 2}"
            DMA(st, s, key, st.sb[f"tsp{i % 2}"][:],
                src[half * 4096:(half + 1) * 4096, dc * 128:(dc + 1) * 128],
                transpose=True)
            self.c_tsp[i] = (key, st.cnt[key])
            if i >= 1:
                self._emit_write(s, i - 1)

    def finish(self):
        st = self.st
        blk = st.block
        while self.i < self.NSTEP:
            self.step()

        @blk.scalar
        def _(s):
            self._emit_write(s, self.NSTEP - 1)
        st.c_wt_done = [("tw0", st.cnt["tw0"]), ("tw1", st.cnt["tw1"])]


def _emit_preload(st):
    d, sb = st.d, st.sb
    blk = st.block

    @blk.sync
    def _(y):
        for k2, nm in enumerate(("identb", "identf", "h128b", "onesrow", "ones512",
                                 "onescol", "onesrowf")):
            src = "h128" if nm == "h128b" else nm
            DMA(st, y, "pre", sb[nm][:], d[src], first=(k2 == 0))
    st.c_preload = ("pre", st.cnt["pre"])


def _emit_gp_weights(st):
    d, sb = st.d, st.sb
    blk = st.block

    @blk.gpsimd
    def _(g):
        DMA(st, g, "gwk",
            sb["wkv"][:, :].rearrange("p (dc n) -> p dc n", n=1024)[:, :, 0:512],
            d["wk"].rearrange("(dc p) n -> p dc n", p=128))
        DMA(st, g, "gwk",
            sb["wkv"][:, :].rearrange("p (dc n) -> p dc n", n=1024)[:, :, 512:1024],
            d["wv"].rearrange("(dc p) n -> p dc n", p=128), first=False)
        DMA(st, g, "gwk", sb["bkv"][:, 0:512], d["bk"], first=False)
        DMA(st, g, "gwk", sb["bkv"][:, 512:1024], d["bv"], first=False)
        st.c_wkv = ("gwk", st.cnt["gwk"])
        st.c_w1b = ("gw1", DMA(st, g, "gw1", d["w1b"], d["w1"]))
        st.c_w3b = ("gw3", DMA(st, g, "gw3", d["w3b"], d["w3"]))
        for dc in range(16):
            DMA(st, g, "gwo", d["wob"][dc],
                d["wo"].rearrange("(hc p) (dc e) -> dc p hc e", p=128, e=128)[dc],
                first=(dc == 0))
        st.c_wob = ("gwo", st.cnt["gwo"])


def _emit_ph0(st):
    d, sb, ps = st.d, st.sb, st.ps
    blk = st.block

    ld_x = [0] * NKT
    v_xn = [0] * NKT
    m_tp4 = [[0] * 4 for _ in range(NKT)]
    e_xnf = [0] * NKT
    m_kv = [0] * NKT
    v_rope = [0] * NKT
    e_v = [0] * NKT
    m_ktp = [0] * NKT
    tp_free = [0, 0]

    for kt in range(NKT):
        i = kt % 2

        @blk.sync
        def _(y, kt=kt, i=i):
            if kt >= 2:
                W(y, st, "vv", v_rope[kt - 2])
            DMA(st, y, f"x{i}", sb[f"x{i}"][:], d["x"][kt * 128:(kt + 1) * 128, :])
            DMA(st, y, f"x{i}", sb[f"ckv{i}"][:],
                d["cos4kv"][kt * 128:(kt + 1) * 128, :], first=False)
            DMA(st, y, f"x{i}", sb[f"skv{i}"][:],
                d["sin4kv"][kt * 128:(kt + 1) * 128, :], first=False)
            ld_x[kt] = (f"x{i}", st.cnt[f"x{i}"])

        @blk.vector
        def _(v, kt=kt, i=i):
            W(v, st, ld_x[kt])
            _ln_part1_v(st, v, sb[f"x{i}"][:])

        @blk.scalar
        def _(s, kt=kt):
            _ln_part2_s(st, s)

        @blk.vector
        def _(v, kt=kt, i=i):
            if kt >= 2:
                W(v, st, "mm", m_tp4[kt - 2][3])
            v_xn[kt] = _ln_part3_v(st, v, sb[f"x{i}"][:], sb[f"xntm{i}"][:])

        for g4 in range(4):
            @blk.tensor
            def _(t, kt=kt, i=i, g4=g4):
                if kt == 0 and g4 == 0:
                    W(t, st, st.c_preload)
                if g4 == 0:
                    W(t, st, "vv", v_xn[kt])
                b = g4 % 2
                W(t, st, "mm", tp_free[b])
                last = None
                for q in range(4):
                    dcx = 4 * g4 + q
                    last = t.transpose(ps[f"tp{b}"][:, q * 128:(q + 1) * 128],
                                       sb[f"xntm{i}"][:, dcx * 128:(dcx + 1) * 128],
                                       sb["identb"][:])
                m_tp4[kt][g4] = INC(st, last, "mm")

            @blk.vector
            def _(v, kt=kt, i=i, g4=g4):
                if g4 == 0 and kt >= 2:
                    W(v, st, "mm", m_kv[kt - 2])
                b = g4 % 2
                W(v, st, "mm", m_tp4[kt][g4])
                tp_free[b] = ("vv", INC(st, v.tensor_copy(
                    sb[f"xnf{i}"][:, g4 * 512:(g4 + 1) * 512], ps[f"tp{b}"][:]), "vv"))
                if g4 == 3:
                    e_xnf[kt] = ("vv", st.cnt["vv"])

        @blk.tensor
        def _(t, kt=kt, i=i):
            W(t, st, e_xnf[kt])
            if kt == 0:
                W(t, st, st.c_wkv)
                W(t, st, st.c_preload)
            if kt >= 2:
                W(t, st, "vv", v_rope[kt - 2])
                W(t, st, e_v[kt - 2])
            t.matmul(ps[f"pk{i}"][:], sb["onesrow"][:], sb["bkv"][:, 0:512],
                     start=True, stop=False)
            t.matmul(ps[f"pv{i}"][:], sb["onesrow"][:], sb["bkv"][:, 512:1024],
                     start=True, stop=False)
            for dc in range(16):
                lt = sb[f"xnf{i}"][:, dc * 128:(dc + 1) * 128]
                t.matmul(ps[f"pk{i}"][:], lt, sb["wkv"][:, dc * 1024:dc * 1024 + 512],
                         start=False, stop=(dc == 15))
                last = t.matmul(ps[f"pv{i}"][:], lt,
                                sb["wkv"][:, dc * 1024 + 512:(dc + 1) * 1024],
                                start=False, stop=(dc == 15))
            m_kv[kt] = INC(st, last, "mm")

        @blk.vector
        def _(v, kt=kt, i=i):
            W(v, st, "mm", m_kv[kt])
            if kt >= 2:
                W(v, st, "mm", m_ktp[kt - 2])
            ins = _rope6(st, v, ps[f"pk{i}"][:], sb[f"ckv{i}"][:], sb[f"skv{i}"][:],
                         sb[f"ktm{i}"][:])
            v_rope[kt] = INC(st, ins, "vv")

        @blk.vector
        def _(v, kt=kt, i=i):
            W(v, st, "mm", m_kv[kt])
            e_v[kt] = ("vv", INC(st, v.tensor_copy(sb["v_tm"][:, kt * 512:(kt + 1) * 512],
                                                   ps[f"pv{i}"][:]), "vv"))

        @blk.tensor
        def _(t, kt=kt, i=i):
            W(t, st, "vv", v_rope[kt])
            b = kt % 2
            W(t, st, "mm", tp_free[b])
            last = None
            for j in range(4):
                last = t.transpose(ps[f"tp{b}"][:, j * 128:(j + 1) * 128],
                                   sb[f"ktm{i}"][:, j * 128:(j + 1) * 128],
                                   sb["identb"][:])
            m_ktp[kt] = INC(st, last, "mm")

        @blk.vector
        def _(v, kt=kt):
            b = kt % 2
            W(v, st, "mm", m_ktp[kt])
            last = None
            for j in range(4):
                last = v.tensor_copy(sb["k_fm"][:, j * T + kt * 128: j * T + (kt + 1) * 128],
                                     ps[f"tp{b}"][:, j * 128:(j + 1) * 128])
            tp_free[b] = ("vv", INC(st, last, "vv"))

        if kt >= 16:
            st.tsp.step()
            st.tsp.step()


def _emit_ph1(st):
    d, sb, ps = st.d, st.sb, st.ps
    blk = st.block

    @blk.gpsimd
    def _(g):
        DMA(st, g, "gbq", sb["bq"][:], d["bq"])
    c_bq = ("gbq", st.cnt["gbq"])

    tp_free = [0, 0]
    xnf_free = [0, 0]
    cos_free = [0, 0]
    qst_free = [0, 0]
    pq_free = [0, 0]
    x_free = [0, 0]
    xntm_free = [0, 0]
    spill_free = [0, 0]
    qfst_free = [0, 0]
    wq_free = 0

    for pa in range(2):
        @blk.gpsimd
        def _(g, pa=pa):
            if pa == 1:
                W(g, st, "mm", wq_free)
            DMA(st, g, "gwq",
                sb["wq"][:, :].rearrange("p (dc n) -> p dc n", n=1024),
                d["wq"].rearrange("(dc p) n -> p dc n", p=128)[:, :, pa * 1024:(pa + 1) * 1024])
        c_wq = ("gwq", st.cnt["gwq"])

        ld_xo = [0] * NTL
        v_xn = [0] * NTL
        m_q = [0] * NTL
        v_rope = [0] * NTL
        m_qtp = [0] * NTL

        for tl in range(NTL):
            i = tl % 2

            @blk.sync
            def _(y, tl=tl, i=i, pa=pa):
                W(y, st, "vv", cos_free[i])
                if pa == 0:
                    W(y, st, "vv", x_free[i])
                    DMA(st, y, f"x{i}", sb[f"xo{i}"][:],
                        d["xo"][tl * 128:(tl + 1) * 128, :])
                else:
                    W(y, st, "mm", xnf_free[i])
                    W(y, st, spill_free[i])
                    DMA(st, y, f"x{i}",
                        sb[f"xnf{i}"][:, :].rearrange("p (dc t) -> p dc t", t=128),
                        d["xnh"][:, :, tl * 128:(tl + 1) * 128].rearrange("dc p t -> p dc t"))
                DMA(st, y, f"x{i}", sb[f"co{i}"][:],
                    d["cos4own"][tl * 128:(tl + 1) * 128, :], first=False)
                DMA(st, y, f"x{i}", sb[f"so{i}"][:],
                    d["sin4own"][tl * 128:(tl + 1) * 128, :], first=False)
                ld_xo[tl] = (f"x{i}", st.cnt[f"x{i}"])

            if pa == 0:
                @blk.vector
                def _(v, tl=tl, i=i):
                    W(v, st, ld_xo[tl])
                    _ln_part1_v(st, v, sb[f"xo{i}"][:])

                @blk.scalar
                def _(s, tl=tl):
                    _ln_part2_s(st, s)

                @blk.vector
                def _(v, tl=tl, i=i):
                    W(v, st, "mm", xntm_free[i])
                    v_xn[tl] = _ln_part3_v(st, v, sb[f"xo{i}"][:], sb[f"xntm{i}"][:])
                    x_free[i] = v_xn[tl]

                if not hasattr(st, "_tpg"):
                    st._tpg = {}
                for g4 in range(4):
                    @blk.tensor
                    def _(t, tl=tl, i=i, g4=g4):
                        if g4 == 0:
                            W(t, st, "vv", v_xn[tl])
                        b = g4 % 2
                        W(t, st, "mm", tp_free[b])
                        last = None
                        for q in range(4):
                            dcx = 4 * g4 + q
                            last = t.transpose(ps[f"tq{b}"][:, q * 128:(q + 1) * 128],
                                               sb[f"xntm{i}"][:, dcx * 128:(dcx + 1) * 128],
                                               sb["identb"][:])
                        st._tpg[(tl, g4)] = INC(st, last, "mm")
                        if g4 == 3:
                            xntm_free[i] = st._tpg[(tl, g4)]

                    @blk.vector
                    def _(v, tl=tl, i=i, g4=g4):
                        if g4 == 0:
                            W(v, st, "mm", xnf_free[i])
                            W(v, st, spill_free[i])
                        b = g4 % 2
                        W(v, st, "mm", st._tpg[(tl, g4)])
                        tp_free[b] = ("vv", INC(st, v.tensor_copy(
                            sb[f"xnf{i}"][:, g4 * 512:(g4 + 1) * 512], ps[f"tq{b}"][:]), "vv"))
                        if g4 == 3:
                            st._exnf = ("vv", st.cnt["vv"])

                @blk.gpsimd
                def _(g, tl=tl, i=i):
                    W(g, st, st._exnf)
                    DMA(st, g, f"gx{i}",
                        d["xnh"][:, :, tl * 128:(tl + 1) * 128].rearrange("dc p t -> p dc t"),
                        sb[f"xnf{i}"][:, :].rearrange("p (dc t) -> p dc t", t=128))
                    spill_free[i] = (f"gx{i}", st.cnt[f"gx{i}"])
                st._exnf_t = st._exnf
            @blk.tensor
            def _(t, tl=tl, i=i, pa=pa):
                if pa == 0:
                    W(t, st, st._exnf_t)
                else:
                    W(t, st, ld_xo[tl])
                if tl == 0:
                    W(t, st, c_wq)
                    W(t, st, c_bq)
                pb = tl % 2
                W(t, st, "vv", pq_free[pb])
                t.matmul(ps[f"pq{pb}"][:, 0:512], sb["onesrow"][:],
                         sb["bq"][:, pa * 1024:pa * 1024 + 512], start=True, stop=False)
                t.matmul(ps[f"pq{pb}"][:, 512:1024], sb["onesrow"][:],
                         sb["bq"][:, pa * 1024 + 512:(pa + 1) * 1024], start=True, stop=False)
                for dc in range(16):
                    lt = sb[f"xnf{i}"][:, dc * 128:(dc + 1) * 128]
                    t.matmul(ps[f"pq{pb}"][:, 0:512], lt,
                             sb["wq"][:, dc * 1024:dc * 1024 + 512],
                             start=False, stop=(dc == 15))
                    last = t.matmul(ps[f"pq{pb}"][:, 512:1024], lt,
                                    sb["wq"][:, dc * 1024 + 512:(dc + 1) * 1024],
                                    start=False, stop=(dc == 15))
                m_q[tl] = INC(st, last, "mm")
                xnf_free[i] = m_q[tl]

            @blk.vector
            def _(v, tl=tl, i=i):
                W(v, st, "mm", m_q[tl])
                W(v, st, "mm", qst_free[i])
                pb = tl % 2
                for hb in range(2):
                    ins = _rope6(st, v, ps[f"pq{pb}"][:, hb * 512:(hb + 1) * 512],
                                 sb[f"co{i}"][:], sb[f"so{i}"][:],
                                 sb[f"qst{i}"][:, hb * 512:(hb + 1) * 512])
                v_rope[tl] = INC(st, ins, "vv")
                pq_free[pb] = v_rope[tl]
                cos_free[i] = v_rope[tl]

            for hb in range(2):
                @blk.tensor
                def _(t, tl=tl, i=i, hb=hb):
                    if hb == 0:
                        W(t, st, "vv", v_rope[tl])
                    b = hb % 2
                    W(t, st, "mm", tp_free[b])
                    last = None
                    for q in range(4):
                        last = t.transpose(
                            ps[f"tq{b}"][:, q * 128:(q + 1) * 128],
                            sb[f"qst{i}"][:, (hb * 4 + q) * 128:(hb * 4 + q + 1) * 128],
                            sb["identb"][:])
                    st._tpg[(tl, "q", hb)] = INC(st, last, "mm")
                    if hb == 1:
                        m_qtp[tl] = st.cnt["mm"]
                        qst_free[i] = m_qtp[tl]

                @blk.vector
                def _(v, tl=tl, pa=pa, hb=hb, i=i):
                    b = hb % 2
                    W(v, st, "mm", st._tpg[(tl, "q", hb)])
                    if hb == 0:
                        W(v, st, qfst_free[i])
                    tp_free[b] = ("vv", INC(st, v.tensor_copy(
                        sb[f"qfst{i}"][:, hb * 512:(hb + 1) * 512], ps[f"tq{b}"][:]), "vv"))
                    if hb == 1:
                        st._eqf = ("vv", st.cnt["vv"])

            @blk.gpsimd
            def _(g, tl=tl, pa=pa, i=i):
                W(g, st, st._eqf)
                for hb in range(2):
                    dst = bass.AP(st.dram["qh"],
                                  (pa * 2 + hb) * 128 * NOWN + tl * 128,
                                  [[NOWN, 128], [4 * 128 * NOWN, 4], [1, 128]])
                    DMA(st, g, f"gq{i}", dst,
                        sb[f"qfst{i}"][:, hb * 512:(hb + 1) * 512]
                        .rearrange("p (q t) -> p q t", t=128), first=(hb == 0))
                qfst_free[i] = (f"gq{i}", st.cnt[f"gq{i}"])

            st.tsp.step()
        wq_free = m_q[NTL - 1]
    st.tsp.finish()


def _emit_ph2(st):
    d, sb, ps = st.d, st.sb, st.ps
    blk = st.block

    s_free = [0, 0, 0]
    p_free = [0, 0, 0]
    pav_free = [0, 0]
    pl_free = [0, 0]
    linv_free = 0
    bc_free = 0
    stg_free = [0, 0]
    mask_free = [0, 0]
    qtl_free = [0, 0]
    ld_mask = [0] * NTL

    if st.preload_tables:
        @blk.scalar
        def _(s):
            s.preload_act_table(AF.Exp)

    gidx = 0
    for tl in range(NTL):
        mbuf = tl % 2

        @blk.sync
        def _(y, tl=tl, mbuf=mbuf):
            W(y, st, "mm", mask_free[mbuf])
            W(y, st, "mm", qtl_free[mbuf])
            if tl == 0:
                W(y, st, ("gq0", st.cnt["gq0"]))
                W(y, st, ("gq1", st.cnt["gq1"]))
            DMA(st, y, f"m{mbuf}",
                sb[f"mask{mbuf}"][:, :].rearrange("p (r n) -> p r n", n=512),
                d["maskt"][tl].rearrange("r p n -> p r n"))
            DMA(st, y, f"m{mbuf}",
                sb[f"qtl{mbuf}"][:, :].rearrange("p (h t) -> p h t", t=128),
                d["qh"][:, :, tl * 128:(tl + 1) * 128].rearrange("h p t -> p h t"),
                first=False)
            ld_mask[tl] = (f"m{mbuf}", st.cnt[f"m{mbuf}"])

        nch = 2 * tl + 2
        for j in range(4):
            ab = gidx % 2
            m_sc = [0] * nch
            e_exp = [0] * nch
            m_av = [0] * nch
            p_free0 = list(p_free)

            def emit_lav(c):
                pb = c % 3

                @blk.tensor
                def _(t, j=j, c=c, pb=pb, ab=ab, nch=nch):
                    W(t, st, "ex", e_exp[c])
                    t.matmul(ps[f"pl{ab}"][:], sb["onescol"][:], sb[f"P{pb}"][:],
                             start=(c == 0), stop=(c == nch - 1))
                    last = t.matmul(ps[f"pav{ab}"][:],
                                    sb["v_tm"][:, c * 512 + j * 128: c * 512 + (j + 1) * 128],
                                    sb[f"P{pb}"][:],
                                    start=(c == 0), stop=(c == nch - 1))
                    m_av[c] = INC(st, last, "mm")
                    p_free[pb] = m_av[c]

            ex0 = st.cnt["ex"]
            for c in range(nch):
                e_exp[c] = ex0 + c + 1

            for c in range(nch):
                sbk = c % 3
                pb = c % 3

                @blk.tensor
                def _(t, tl=tl, j=j, c=c, sbk=sbk, pb=pb, ab=ab, mbuf=mbuf, nch=nch):
                    W(t, st, "ex", s_free[sbk])
                    if c >= 3:
                        W(t, st, "mm", m_av[c - 3])
                    else:
                        W(t, st, "mm", p_free[pb])
                        if c == 0:
                            W(t, st, "vv", pav_free[ab])
                            W(t, st, "vv", pl_free[ab])
                    if c >= 2 * tl:
                        W(t, st, ld_mask[tl])
                    if c == 0:
                        W(t, st, ld_mask[tl])
                    lt = sb["k_fm"][:, j * T + c * 128: j * T + (c + 1) * 128]
                    ride = (c >= 2 * tl)
                    last = t.matmul(ps[f"S{sbk}"][:], lt,
                                    sb[f"qtl{mbuf}"][:, j * 512:(j + 1) * 512],
                                    start=True, stop=not ride)
                    if ride:
                        r = c - 2 * tl
                        last = t.matmul(ps[f"S{sbk}"][:], sb["identb"][:],
                                        sb[f"mask{mbuf}"][:, r * 512:(r + 1) * 512],
                                        start=False, stop=True)
                    m_sc[c] = INC(st, last, "mm")
                    if ride and c == nch - 1:
                        mask_free[mbuf] = m_sc[c]
                        if j == 3:
                            qtl_free[mbuf] = m_sc[c]
                s_free[sbk] = e_exp[c]

                if c >= 2:
                    emit_lav(c - 2)
            if nch >= 2:
                emit_lav(nch - 2)
            emit_lav(nch - 1)

            @blk.scalar
            def _(s, nch=nch, ab=ab):
                for c in range(nch):
                    sbk = c % 3
                    pb = c % 3
                    s.wait_ge(st.sem["mm"], m_sc[c])
                    if c >= 3:
                        s.wait_ge(st.sem["mm"], m_av[c - 3])
                    else:
                        if p_free0[pb] > 0:
                            s.wait_ge(st.sem["mm"], p_free0[pb])
                    got = INC(st, s.activation(sb[f"P{pb}"][:], ps[f"S{sbk}"][:],
                                               AF.Exp, scale=SCALE), "ex")
                    assert got == e_exp[c], (got, e_exp[c])

            @blk.vector
            def _(v, ab=ab, nch=nch):
                W(v, st, "mm", m_av[nch - 1])
                W(v, st, "mm", linv_free)
                st.c_li = INC(st, v.reciprocal(sb["linv"][:], ps[f"pl{ab}"][:]), "vv")
                pl_free[ab] = st.c_li

            @blk.tensor
            def _(t):
                W(t, st, "vv", st.c_li)
                W(t, st, "vv", bc_free)
                st.c_bc = INC(st, t.matmul(ps["pbc"][:], sb["onesrowf"][:], sb["linv"][:],
                                           start=True, stop=True), "mm")
            linv_free = st.c_bc

            stg = gidx % 2

            @blk.vector
            def _(v, ab=ab, stg=stg):
                W(v, st, "mm", st.c_bc)
                v.tensor_copy(sb["bc_sb"][:], ps["pbc"][:])
                v.drain()
                W(v, st, stg_free[stg])
                st.c_av = INC(st, v.tensor_mul(sb[f"avst{stg}"][:], ps[f"pav{ab}"][:],
                                               sb["bc_sb"][:]), "vv")
                pav_free[ab] = st.c_av
            bc_free = st.c_av

            @blk.gpsimd
            def _(g, tl=tl, j=j, stg=stg):
                W(g, st, "vv", st.c_av)
                dst = bass.AP(st.dram["avh"],
                              (j * 128 * NOWN) + tl * 128,
                              [[NOWN, 128], [4 * 128 * NOWN, 4], [1, 128]])
                DMA(st, g, f"gav{stg}", dst,
                    sb[f"avst{stg}"][:, :].rearrange("p (g4 t) -> p g4 t", t=128))
                stg_free[stg] = (f"gav{stg}", st.cnt[f"gav{stg}"])
            gidx += 1


class W2Prep:

    def __init__(self, st):
        self.st = st
        self.v_bfly = [0] * 16
        self.g_out = [0] * 16
        self.tpw_free = [0, 0]
        self.ld = [0] * 16
        self.cp_done = [0] * 16
        self.e_w_last = [0] * 16
        self.m_tpw = [[0] * 16 for _ in range(16)]

    def part_load(self, dc):
        st = self.st
        d, sb = st.d, st.sb
        blk = st.block

        @blk.sync
        def _(y, dc=dc):
            if dc >= 1:
                W(y, st, "vv", self.v_bfly[dc - 1])
            DMA(st, y, "wp", sb["wpB"][:], d["w2"][dc * 128:(dc + 1) * 128, :])
            self.ld[dc] = ("wp", st.cnt["wp"])

    def part_tr(self, dc, half):
        st = self.st
        sb, ps = st.sb, st.ps
        blk = st.block
        for grp in range(8 * half, 8 * half + 8):
            b = grp % 2

            @blk.tensor
            def _(t, dc=dc, grp=grp, b=b):
                if grp == 0:
                    W(t, st, self.ld[dc])
                    if dc >= 1:
                        W(t, st, "vv", self.v_bfly[dc - 1])
                W(t, st, "mm", self.tpw_free[b])
                last = None
                for q in range(4):
                    hcx = 4 * grp + q
                    last = t.transpose(ps[f"tpw{b}"][:, q * 128:(q + 1) * 128],
                                       sb["wpB"][:, hcx * 128:(hcx + 1) * 128],
                                       sb["identf"][:])
                self.m_tpw[dc][grp] = INC(st, last, "mm")

            @blk.vector
            def _(v, dc=dc, grp=grp, b=b):
                W(v, st, "mm", self.m_tpw[dc][grp])
                self.tpw_free[b] = ("vv", INC(st, v.tensor_copy(
                    sb["wpA"][:, grp * 512:(grp + 1) * 512], ps[f"tpw{b}"][:]), "vv"))
                if grp == 15:
                    self.cp_done[dc] = ("vv", st.cnt["vv"])

    def part_bf(self, dc, piece):
        st = self.st
        sb = st.sb
        blk = st.block

        @blk.vector
        def _(v, dc=dc, piece=piece):
            if piece == 0:
                W(v, st, self.cp_done[dc])
            bufs = [sb["wpA"], sb["wpB"]]
            for stg in (2 * piece, 2 * piece + 1):
                cur = bufs[stg % 2]
                dstt = sb["w2xb"] if stg == 5 else bufs[(stg + 1) % 2]
                if stg == 5 and dc >= 1:
                    W(v, st, ("ex", self.e_w_last[dc - 1]))
                bit = 1 << stg
                hi = 64 // (2 * bit)
                lo = bit
                cv = cur[:, :].rearrange("p (h t l e) -> p h t l e", h=hi, t=2, l=lo)
                ov = dstt[:, :].rearrange("p (h t l e) -> p h t l e", h=hi, t=2, l=lo)
                v.drain()
                for eh in range(2):
                    e128 = cv.shape[4]
                    eh_sl = slice(eh * e128 // 2, (eh + 1) * e128 // 2)
                    v.tensor_add(ov[:, :, 0, :, eh_sl], cv[:, :, 0, :, eh_sl],
                                 cv[:, :, 1, :, eh_sl])
                    last = v.tensor_sub(ov[:, :, 1, :, eh_sl], cv[:, :, 0, :, eh_sl],
                                        cv[:, :, 1, :, eh_sl])
            if piece == 2:
                self.v_bfly[dc] = INC(st, last, "vv")

    def part_h(self, dc, half):
        st = self.st
        sb, ps = st.sb, st.ps
        blk = st.block
        m_h = [0] * 8
        ex0 = st.cnt["ex"]

        for k in range(8):
            s512 = 8 * half + k
            b = s512 % 2

            @blk.tensor
            def _(t, dc=dc, s512=s512, b=b, k=k):
                if k == 0 and half == 0:
                    W(t, st, "vv", self.v_bfly[dc])
                W(t, st, "mm", self.tpw_free[b])
                m_h[k] = INC(st, t.matmul(ps[f"tpw{b}"][:], sb["h128b"][:],
                                          sb["w2xb"][:, s512 * 512:(s512 + 1) * 512],
                                          start=True, stop=True), "mm")
            self.tpw_free[b] = ("ex", ex0 + k + 1)

        @blk.scalar
        def _(s, dc=dc, half=half):
            for k in range(8):
                s512 = 8 * half + k
                b = s512 % 2
                if k == 0 and half == 0 and dc >= 1:
                    W(s, st, self.g_out[dc - 1])
                s.wait_ge(st.sem["mm"], m_h[k])
                got = INC(st, s.activation(sb["w2xo"][:, s512 * 512:(s512 + 1) * 512],
                                           ps[f"tpw{b}"][:], AF.Copy, scale=RSQN), "ex")
                assert got == ex0 + k + 1
            if half == 1:
                self.e_w_last[dc] = st.cnt["ex"]

    def part_spill(self, dc):
        st = self.st
        blk = st.block

        @blk.gpsimd
        def _(g, dc=dc):
            W(g, st, ("ex", self.e_w_last[dc]))
            DMA(st, g, "gw2", st.d["w2x"][dc], st.sb["w2xo"][:])
            self.g_out[dc] = ("gw2", st.cnt["gw2"])

    def steps_for(self, hc):
        dc = hc // 4
        ph = hc % 4
        if dc >= 16:
            return
        if ph == 0:
            self.part_load(dc)
            self.part_tr(dc, 0)
        elif ph == 1:
            self.part_tr(dc, 1)
            self.part_bf(dc, 0)
        elif ph == 2:
            self.part_bf(dc, 1)
            self.part_bf(dc, 2)
        else:
            self.part_h(dc, 0)
            self.part_h(dc, 1)
            self.part_spill(dc)


def _emit_ph3a(st):
    d, sb, ps = st.d, st.sb, st.ps
    blk = st.block

    @blk.sync
    def _(y):
        DMA(st, y, "pre",
            sb["xnall"][:, :].rearrange("p (dc t) -> p dc t", t=NOWN),
            d["xnh"].rearrange("dc p t -> p dc t"))
    c_xnall = ("pre", st.cnt["pre"])

    if st.preload_tables:
        @blk.scalar
        def _(s):
            s.preload_act_table(AF.Silu)

    ld_w = [0] * 64
    m_g = [[0] * 4 for _ in range(64)]
    m_u = [[0] * 4 for _ in range(64)]
    e_sg = [[0] * 4 for _ in range(64)]
    v_h = [[0] * 4 for _ in range(64)]
    g_spill = [0] * 64
    w2 = st.w2prep

    for hc in range(64):
        i = hc % 2

        @blk.sync
        def _(y, hc=hc, i=i):
            if hc == 0:
                for cw in st.c_wt_done:
                    W(y, st, cw)
            if hc >= 2:
                W(y, st, "mm", m_g[hc - 2][0])
                W(y, st, "mm", m_u[hc - 2][1])
            DMA(st, y, f"wa{i}", sb[f"w1tb{i}"][:], d["w1t"][hc])
            DMA(st, y, f"wa{i}", sb[f"w3tb{i}"][:], d["w3t"][hc], first=False)
            ld_w[hc] = (f"wa{i}", st.cnt[f"wa{i}"])

        ex0 = st.cnt["ex"]
        vv0 = st.cnt["vv"]
        for tsub in range(4):
            e_sg[hc][tsub] = ex0 + tsub + 1
            v_h[hc][tsub] = vv0 + tsub + 1

        @blk.tensor
        def _(t, hc=hc, i=i):
            if hc == 0:
                W(t, st, c_xnall)
            W(t, st, ld_w[hc])
            if hc >= 1:
                W(t, st, "ex", e_sg[hc - 1][3])
            for dc in range(16):
                for tsub in range(4):
                    last = t.matmul(
                        ps[f"pg{tsub}"][:], sb[f"w1tb{i}"][:, dc * 128:(dc + 1) * 128],
                        sb["xnall"][:, dc * NOWN + tsub * 512: dc * NOWN + (tsub + 1) * 512],
                        start=(dc == 0), stop=(dc == 15))
            m_g[hc][0] = INC(st, last, "mm")

        @blk.scalar
        def _(s, hc=hc):
            for tsub in range(4):
                s.wait_ge(st.sem["mm"], m_g[hc][0])
                if hc >= 1:
                    s.wait_ge(st.sem["vv"], v_h[hc - 1][3])
                got = INC(st, s.activation(sb[f"sg{tsub}"][:], ps[f"pg{tsub}"][:],
                                           AF.Silu), "ex")
                assert got == e_sg[hc][tsub], (got, e_sg[hc][tsub])

        for pair in range(2):
            @blk.tensor
            def _(t, hc=hc, i=i, pair=pair):
                if pair == 0:
                    if hc >= 1:
                        W(t, st, "vv", v_h[hc - 1][3])
                else:
                    W(t, st, "vv", v_h[hc][1])
                for dc in range(16):
                    for tsub in (2 * pair, 2 * pair + 1):
                        last = t.matmul(
                            ps[f"pu{tsub % 2}"][:], sb[f"w3tb{i}"][:, dc * 128:(dc + 1) * 128],
                            sb["xnall"][:, dc * NOWN + tsub * 512: dc * NOWN + (tsub + 1) * 512],
                            start=(dc == 0), stop=(dc == 15))
                m_u[hc][pair] = INC(st, last, "mm")

            @blk.vector
            def _(v, hc=hc, i=i, pair=pair):
                for tsub in (2 * pair, 2 * pair + 1):
                    v.wait_ge(st.sem["ex"], e_sg[hc][tsub])
                    v.wait_ge(st.sem["mm"], m_u[hc][pair])
                    if pair == 0 and tsub == 0 and hc >= 2:
                        W(v, st, g_spill[hc - 2])
                    got = INC(st, v.tensor_mul(
                        sb[f"hst{i}"][:, tsub * 512:(tsub + 1) * 512],
                        sb[f"sg{tsub}"][:], ps[f"pu{tsub % 2}"][:]), "vv")
                    assert got == v_h[hc][tsub], (got, v_h[hc][tsub])

        @blk.gpsimd
        def _(g, hc=hc, i=i):
            W(g, st, "vv", v_h[hc][3])
            DMA(st, g, f"gh{i}", d["hidh"][hc], sb[f"hst{i}"][:])
            g_spill[hc] = (f"gh{i}", st.cnt[f"gh{i}"])

        w2.steps_for(hc)


def _emit_ph3b(st):
    d, sb, ps = st.d, st.sb, st.ps
    blk = st.block

    @blk.gpsimd
    def _(g):
        DMA(st, g, "gbo", sb["bo"][:], d["bo"])
    c_bo = ("gbo", st.cnt["gbo"])

    ld_tb = [0] * 4
    ld_wd = [0] * 64
    m_o = [0] * 64
    e_o = [0] * 64
    m_tr = [0] * 64
    v_add = [0] * 64
    g_out = [0] * 4

    for tb in range(4):
        @blk.sync
        def _(y, tb=tb):
            if tb > 0:
                W(y, st, "vv", v_add[tb * 16 - 1])
                W(y, st, g_out[tb - 1])
                W(y, st, "mm", m_o[tb * 16 - 1])
            DMA(st, y, "tbs",
                sb["hidtb"][:, :].rearrange("p (hc t) -> p hc t", t=512),
                d["hidh"][:, :, tb * 512:(tb + 1) * 512].rearrange("hc p t -> p hc t"))
            DMA(st, y, "tbs",
                sb["avtb"][:, :].rearrange("p (hc t) -> p hc t", t=512),
                d["avh"][:, :, tb * 512:(tb + 1) * 512].rearrange("hc p t -> p hc t"),
                first=False)
            for q in range(4):
                tl = 4 * tb + q
                DMA(st, y, "tbs", sb[f"xst{q}"][:],
                    d["xo"][tl * 128:(tl + 1) * 128, :], first=False)
            ld_tb[tb] = ("tbs", st.cnt["tbs"])

        for dc in range(16):
            k = tb * 16 + dc
            i = dc % 2

            @blk.sync
            def _(y, tb=tb, dc=dc, k=k, i=i):
                if k >= 2:
                    W(y, st, "mm", m_o[k - 2])
                if k == 0:
                    W(y, st, ("gw2", st.cnt["gw2"]))
                    W(y, st, st.c_wob)
                DMA(st, y, f"wd{i}", sb[f"w2xd{i}"][:], d["w2x"][dc])
                DMA(st, y, f"wd{i}", sb[f"wod{i}"][:], d["wob"][dc], first=False)
                ld_wd[k] = (f"wd{i}", st.cnt[f"wd{i}"])

            @blk.tensor
            def _(t, tb=tb, dc=dc, k=k, i=i):
                W(t, st, ld_tb[tb])
                W(t, st, ld_wd[k])
                if k == 0:
                    W(t, st, c_bo)
                ob = dc % 2
                if k >= 2:
                    W(t, st, e_o[k - 2])
                t.matmul(ps[f"po{ob}"][:], sb["bo"][:, dc * 128:(dc + 1) * 128],
                         sb["ones512"][:], start=True, stop=False)
                for hc in range(16):
                    t.matmul(ps[f"po{ob}"][:], sb[f"wod{i}"][:, hc * 128:(hc + 1) * 128],
                             sb["avtb"][:, hc * 512:(hc + 1) * 512],
                             start=False, stop=False)
                for hc in range(64):
                    last = t.matmul(ps[f"po{ob}"][:], sb[f"w2xd{i}"][:, hc * 128:(hc + 1) * 128],
                                    sb["hidtb"][:, hc * 512:(hc + 1) * 512],
                                    start=False, stop=(hc == 63))
                m_o[k] = INC(st, last, "mm")

            @blk.vector
            def _(v, k=k, dc=dc):
                ob = dc % 2
                W(v, st, "mm", m_o[k])
                if k >= 2:
                    W(v, st, "mm", m_tr[k - 2])
                e_o[k] = ("vv", INC(st, v.tensor_copy(sb[f"pev{ob}"][:], ps[f"po{ob}"][:]), "vv"))

            @blk.tensor
            def _(t, k=k, dc=dc):
                ob = dc % 2
                W(t, st, e_o[k])
                if k >= 2:
                    W(t, st, "vv", v_add[k - 2])
                last = None
                for q in range(4):
                    last = t.transpose(ps[f"ptr{ob}"][:, q * 128:(q + 1) * 128],
                                       sb[f"pev{ob}"][:, q * 128:(q + 1) * 128],
                                       sb["identf"][:])
                m_tr[k] = INC(st, last, "mm")

            @blk.vector
            def _(v, k=k, dc=dc, tb=tb):
                ob = dc % 2
                W(v, st, "mm", m_tr[k])
                if tb > 0 and dc == 0:
                    W(v, st, g_out[tb - 1])
                last = None
                for q in range(4):
                    last = v.tensor_add(sb[f"ost{q}"][:, dc * 128:(dc + 1) * 128],
                                        ps[f"ptr{ob}"][:, q * 128:(q + 1) * 128],
                                        sb[f"xst{q}"][:, dc * 128:(dc + 1) * 128])
                v_add[k] = INC(st, last, "vv")

        @blk.gpsimd
        def _(g, tb=tb):
            W(g, st, "vv", v_add[tb * 16 + 15])
            for q in range(4):
                tl = 4 * tb + q
                DMA(st, g, "gout", d["out"][tl * 128:(tl + 1) * 128, :],
                    sb[f"ost{q}"][:], first=(q == 0))
            g_out[tb] = ("gout", st.cnt["gout"])


def _emit_final(st):
    blk = st.block
    tot = dict(st.cnt)

    @blk.gpsimd
    def _(g):
        for s in DMA_SEMS:
            if s.startswith("g") and tot[s] > 0:
                g.wait_ge(st.sem[s], tot[s])

    @blk.sync
    def _(y):
        for s in DMA_SEMS:
            if tot[s] > 0:
                y.wait_ge(st.sem[s], tot[s])


def build(upto=4, dbg=False, preload_tables=False):
    nc = bass.Bass("TRN2", target_bir_lowering=False)
    st = St()
    st.preload_tables = preload_tables
    st.nc = nc

    dram = {}

    def din(name, shape, dt):
        dram[name] = nc.dram_tensor(name, list(shape), dt, kind="ExternalInput")

    def dint(name, shape, dt):
        dram[name] = nc.dram_tensor(name, list(shape), dt, kind="Internal")

    def do_(name, shape, dt):
        dram[name] = nc.dram_tensor(name, list(shape), dt, kind="ExternalOutput")

    din("x", (T, C), F32)
    din("xo", (NOWN, C), F32)
    din("wq", (C, C), F32)
    din("wk", (C, 512), F32)
    din("wv", (C, 512), F32)
    din("wo", (C, C), F32)
    din("w1", (HID, C), F32)
    din("w2", (C, HID), F32)
    din("w3", (HID, C), F32)
    din("bq", (1, C), F32)
    din("bk", (1, 512), F32)
    din("bv", (1, 512), F32)
    din("bo", (1, C), F32)
    din("identb", (128, 128), BF16)
    din("identf", (128, 128), F32)
    din("h128", (128, 128), BF16)
    din("onesrow", (1, 128), BF16)
    din("ones512", (1, 512), BF16)
    din("onescol", (128, 1), BF16)
    din("onesrowf", (1, 128), F32)
    din("cos4kv", (T, 256), F32)
    din("sin4kv", (T, 256), F32)
    din("cos4own", (NOWN, 256), F32)
    din("sin4own", (NOWN, 256), F32)
    din("maskt", (NTL, 2, 128, 512), BF16)

    dsc = do_ if dbg else dint
    dint("w1b", (HID, C), BF16)
    dint("w3b", (HID, C), BF16)
    dint("wob", (16, 128, 16 * 128), BF16)
    dsc("w1t", (64, 128, 16 * 128), BF16)
    dsc("w3t", (64, 128, 16 * 128), BF16)
    dsc("w2x", (16, 128, 64 * 128), BF16)
    dsc("xnh", (16, 128, NOWN), BF16)
    dsc("qh", (16, 128, NOWN), BF16)
    dsc("avh", (16, 128, NOWN), BF16)
    dsc("hidh", (64, 128, NOWN), BF16)

    do_("out", (NOWN, C), F32)
    if dbg:
        do_("kdump", (128, 4 * T), BF16)
        do_("vdump", (128, NKT * 512), BF16)

    st.dram = dram
    st.d = {k: v.ap() for k, v in dram.items()}

    with ExitStack() as ctx:
        st.ctx = ctx
        st.cnt = {}
        st.sb = {}
        st.ps = {}
        st.sem = {}
        for n in ("mm", "ex", "vv", "gv") + DMA_SEMS:
            st.sem[n] = ctx.enter_context(nc.semaphore(n))
            st.cnt[n] = 0
        st.block = ctx.enter_context(nc.Block())

        uid = [0]

        def SB(stack, name, shape, dt):
            uid[0] += 1
            st.sb[name] = stack.enter_context(
                nc.sbuf_tensor(f"s{uid[0]}_{name}", list(shape), dt))

        def PS(stack, name, shape, dt):
            uid[0] += 1
            st.ps[name] = stack.enter_context(
                nc.psum_tensor(f"p{uid[0]}_{name}", list(shape), dt))

        for nm, shp, dt in [("identb", (128, 128), BF16), ("identf", (128, 128), F32),
                            ("h128b", (128, 128), BF16), ("onesrow", (1, 128), BF16),
                            ("ones512", (1, 512), BF16), ("onescol", (128, 1), BF16),
                            ("onesrowf", (1, 128), F32)]:
            SB(ctx, nm, shp, dt)

        _emit_preload(st)
        st.tsp = TspPipe(st)

        with ExitStack() as ph012:
            for nm, shp, dt in [("k_fm", (128, 4 * T), BF16),
                                ("v_tm", (128, NKT * 512), BF16),
                                ("tsp0", (128, 4096), BF16),
                                ("tsp1", (128, 4096), BF16)]:
                SB(ph012, nm, shp, dt)
            with ExitStack() as p0:
                for nm, shp, dt in [("wkv", (128, 16 * 1024), BF16), ("bkv", (1, 1024), BF16),
                                    ("bn6", (128, 24), F32), ("mv", (128, 2), F32),
                                    ("vt", (128, 1), F32), ("rstd", (128, 1), F32),
                                    ("rt1", (128, 256), F32), ("rt2", (128, 256), F32)]:
                    SB(p0, nm, shp, dt)
                for i in range(2):
                    for nm, shp, dt in [(f"x{i}", (128, C), F32), (f"xntm{i}", (128, C), BF16),
                                        (f"xnf{i}", (128, C), BF16), (f"ktm{i}", (128, 512), BF16),
                                        (f"ckv{i}", (128, 256), F32), (f"skv{i}", (128, 256), F32)]:
                        SB(p0, nm, shp, dt)
                for nm, shp, dt in [("tp0", (128, 512), BF16), ("tp1", (128, 512), BF16),
                                    ("pk0", (128, 512), F32), ("pk1", (128, 512), F32),
                                    ("pv0", (128, 512), F32), ("pv1", (128, 512), F32)]:
                    PS(p0, nm, shp, dt)
                _emit_gp_weights(st)
                _emit_ph0(st)
            if upto >= 1:
                _barrier(st)
                with ExitStack() as p1:
                    for nm, shp, dt in [("wq", (128, 16 * 1024), BF16), ("bq", (1, C), BF16),
                                        ("bn6", (128, 24), F32), ("mv", (128, 2), F32),
                                        ("vt", (128, 1), F32), ("rstd", (128, 1), F32),
                                        ("rt1", (128, 256), F32), ("rt2", (128, 256), F32)]:
                        SB(p1, nm, shp, dt)
                    for i in range(2):
                        for nm, shp, dt in [(f"xo{i}", (128, C), F32), (f"xntm{i}", (128, C), BF16),
                                            (f"xnf{i}", (128, C), BF16), (f"qst{i}", (128, 1024), BF16),
                                            (f"qfst{i}", (128, 1024), BF16),
                                            (f"co{i}", (128, 256), F32), (f"so{i}", (128, 256), F32)]:
                            SB(p1, nm, shp, dt)
                    for nm, shp, dt in [("tq0", (128, 512), BF16), ("tq1", (128, 512), BF16),
                                        ("pq0", (128, 1024), F32), ("pq1", (128, 1024), F32)]:
                        PS(p1, nm, shp, dt)
                    _emit_ph1(st)
            if upto >= 2:
                _barrier(st)
                with ExitStack() as p2:
                    for nm, shp, dt in [("mask0", (128, 1024), BF16), ("mask1", (128, 1024), BF16),
                                        ("qtl0", (128, 2048), BF16), ("qtl1", (128, 2048), BF16),
                                        ("P0", (128, 512), BF16), ("P1", (128, 512), BF16),
                                        ("P2", (128, 512), BF16),
                                        ("linv", (1, 512), F32), ("bc_sb", (128, 512), F32),
                                        ("avst0", (128, 512), BF16), ("avst1", (128, 512), BF16)]:
                        SB(p2, nm, shp, dt)
                    for nm, shp, dt in [("S0", (128, 512), F32), ("S1", (128, 512), F32),
                                        ("S2", (128, 512), F32),
                                        ("pav0", (128, 512), F32), ("pav1", (128, 512), F32),
                                        ("pl0", (1, 512), F32), ("pl1", (1, 512), F32),
                                        ("pbc", (128, 512), F32)]:
                        PS(p2, nm, shp, dt)
                    _emit_ph2(st)
            if dbg:
                _barrier(st)

                @st.block.gpsimd
                def _(g):
                    DMA(st, g, "gout", st.d["kdump"], st.sb["k_fm"][:])
                    DMA(st, g, "gout", st.d["vdump"], st.sb["v_tm"][:], first=False)
        if upto >= 3:
            _barrier(st)
            with ExitStack() as p3:
                for nm, shp, dt in [("xnall", (128, 16 * NOWN), BF16),
                                    ("wpA", (128, HID), F32), ("wpB", (128, HID), F32),
                                    ("w2xb", (128, HID), BF16), ("w2xo", (128, HID), BF16),
                                    ("sg0", (128, 512), BF16), ("sg1", (128, 512), BF16),
                                    ("sg2", (128, 512), BF16), ("sg3", (128, 512), BF16),
                                    ("w1tb0", (128, 2048), BF16), ("w1tb1", (128, 2048), BF16),
                                    ("w3tb0", (128, 2048), BF16), ("w3tb1", (128, 2048), BF16),
                                    ("hst0", (128, 2048), BF16), ("hst1", (128, 2048), BF16)]:
                    SB(p3, nm, shp, dt)
                for nm, shp, dt in [("pg0", (128, 512), F32), ("pg1", (128, 512), F32),
                                    ("pg2", (128, 512), F32), ("pg3", (128, 512), F32),
                                    ("pu0", (128, 512), F32), ("pu1", (128, 512), F32),
                                    ("tpw0", (128, 512), F32), ("tpw1", (128, 512), F32)]:
                    PS(p3, nm, shp, dt)
                st.w2prep = W2Prep(st)
                _emit_ph3a(st)
        if upto >= 4:
            _barrier(st)
            with ExitStack() as p4:
                for nm, shp, dt in [("hidtb", (128, 64 * 512), BF16),
                                    ("avtb", (128, 16 * 512), BF16),
                                    ("w2xd0", (128, 64 * 128), BF16), ("w2xd1", (128, 64 * 128), BF16),
                                    ("wod0", (128, 16 * 128), BF16), ("wod1", (128, 16 * 128), BF16),
                                    ("pev0", (128, 512), F32), ("pev1", (128, 512), F32),
                                    ("bo", (1, C), BF16)]:
                    SB(p4, nm, shp, dt)
                for q in range(4):
                    SB(p4, f"ost{q}", (128, C), F32)
                    SB(p4, f"xst{q}", (128, C), F32)
                for nm, shp, dt in [("po0", (128, 512), F32), ("po1", (128, 512), F32),
                                    ("ptr0", (128, 512), F32), ("ptr1", (128, 512), F32)]:
                    PS(p4, nm, shp, dt)
                _emit_ph3b(st)
        _emit_final(st)

    return nc


def kernel(L0_attn_x, L0_attn_Wq, L0_attn_bq, L0_attn_Wk, L0_attn_bk,
           L0_attn_Wv, L0_attn_bv, L0_attn_Wo, L0_attn_bo,
           L0_ffn_W1, L0_ffn_W2, L0_ffn_W3):
    x = np.ascontiguousarray(np.asarray(L0_attn_x, np.float32))
    wq = np.ascontiguousarray(np.asarray(L0_attn_Wq, np.float32))
    wk = np.ascontiguousarray(np.asarray(L0_attn_Wk, np.float32))
    wv = np.ascontiguousarray(np.asarray(L0_attn_Wv, np.float32))
    wo = np.ascontiguousarray(np.asarray(L0_attn_Wo, np.float32))
    w1 = np.ascontiguousarray(np.asarray(L0_ffn_W1, np.float32))
    w2 = np.ascontiguousarray(np.asarray(L0_ffn_W2, np.float32))
    w3 = np.ascontiguousarray(np.asarray(L0_ffn_W3, np.float32))
    bq = np.ascontiguousarray(np.asarray(L0_attn_bq, np.float32).reshape(1, C))
    bk = np.ascontiguousarray(np.asarray(L0_attn_bk, np.float32).reshape(1, 512))
    bv = np.ascontiguousarray(np.asarray(L0_attn_bv, np.float32).reshape(1, 512))
    bo = np.ascontiguousarray(np.asarray(L0_attn_bo, np.float32).reshape(1, C))

    cos4, sin4 = _rope_tables()
    consts = dict(
        identb=np.eye(128, dtype=bf16),
        identf=np.eye(128, dtype=np.float32),
        h128=_h128(),
        onesrow=np.ones((1, 128), bf16),
        ones512=np.ones((1, 512), bf16),
        onescol=np.ones((128, 1), bf16),
        onesrowf=np.ones((1, 128), np.float32),
        cos4kv=cos4, sin4kv=sin4,
    )
    masks = [_masks(p) for p in range(2)]

    in_maps = []
    for core in range(8):
        s, p = core // 2, core % 2
        rows = _own_rows(p)
        im = dict(x=np.ascontiguousarray(x[s]),
                  xo=np.ascontiguousarray(x[s][rows]),
                  wq=wq, wk=wk, wv=wv, wo=wo, w1=w1, w2=w2, w3=w3,
                  bq=bq, bk=bk, bv=bv, bo=bo,
                  cos4own=np.ascontiguousarray(cos4[rows]),
                  sin4own=np.ascontiguousarray(sin4[rows]),
                  maskt=masks[p], **consts)
        in_maps.append(im)

    nc = build(upto=4)
    res = run_bass_kernel_spmd(nc, in_maps, core_ids=list(range(8)))

    out = np.empty((B, T, C), np.float32)
    for core in range(8):
        s, p = core // 2, core % 2
        o = np.asarray(res.results[core]["out"], np.float32)
        for i in range(NTL):
            tg = 2 * i + p
            out[s, tg * 128:(tg + 1) * 128, :] = o[i * 128:(i + 1) * 128, :]
    return out


# version 07
# speedup vs baseline: 1.1190x; 1.0447x over previous
import numpy as np
import ml_dtypes
from contextlib import ExitStack

import concourse.bass as bass
import concourse.mybir as mybir
from concourse.bass_utils import run_bass_kernel_spmd

F32 = mybir.dt.float32
BF16 = mybir.dt.bfloat16
AF = mybir.ActivationFunctionType
ALU = mybir.AluOpType
bf16 = ml_dtypes.bfloat16

B, T, C = 4, 4096, 2048
H, HKV, HD = 16, 4, 128
HID = 8192
NOWN = 2048
NTL = 16
NKT = 32
EPS = 1e-5
SCALE = 0.25
RSQN = float(1.0 / np.sqrt(HID))
NEG = -1e9


def _rope_tables():
    theta = 1.0 / 10000.0 ** (np.arange(0, HD, 2, dtype=np.float64) / HD)
    freqs = np.arange(T, dtype=np.float64)[:, None] * theta[None, :]
    cos = np.cos(freqs).astype(np.float32)
    sin = np.sin(freqs).astype(np.float32)
    return np.tile(cos, (1, 4)).copy(), np.tile(sin, (1, 4)).copy()


def _h128():
    i = np.arange(128)
    a = i[:, None] & i[None, :]
    pc = np.zeros((128, 128), np.int64)
    for b in range(7):
        pc += (a >> b) & 1
    return np.where(pc % 2 == 0, np.float32(1.0), np.float32(-1.0)).astype(bf16)


def _masks(p):
    m = np.zeros((NTL, 2, 128, 512), np.float32)
    tq = np.arange(128)
    tk = np.arange(128)
    for i in range(NTL):
        tg = 2 * i + p
        for r in range(2):
            c = 2 * i + r
            valid = (tg * 128 + tq[None, :]) >= (c * 128 + tk[:, None])
            blk = np.where(valid, np.float32(0.0), np.float32(NEG))
            m[i, r] = np.tile(blk, (1, 4))
    return m.astype(bf16)


def _own_rows(p):
    idx = []
    for i in range(NTL):
        tg = 2 * i + p
        idx.extend(range(tg * 128, (tg + 1) * 128))
    return np.array(idx)


class St:
    pass


def W(eng, st, sem, val=None):
    if val is None:
        if not isinstance(sem, tuple):
            return
        sem, val = sem
    if isinstance(val, tuple):
        sem, val = val
    if isinstance(val, int) and val > 0:
        eng.wait_ge(st.sem[sem], val)


def INC(st, instr, sem, k=1):
    instr.then_inc(st.sem[sem], k)
    st.cnt[sem] += k
    return st.cnt[sem]


DMA_SEMS = ("pre", "x0", "x1", "m0", "m1", "wa0", "wa1", "wp", "tbs",
            "wd0", "wd1", "t0", "t1", "tw0", "tw1",
            "gwk", "gw1", "gw3", "gwo", "gwq", "gbq", "gbo", "gw2", "gout",
            "gx0", "gx1", "gq0", "gq1", "gav0", "gav1", "gh0", "gh1")


def DMA(st, eng, key, out, in_, first=True, **kw):
    if first and st.cnt[key] > 0:
        eng.wait_ge(st.sem[key], st.cnt[key])
    return INC(st, eng.dma_start(out=out, in_=in_, **kw), key, 16)


def FR(st, sem):
    return (sem, st.cnt[sem])


def HEAT(st, t, n=1):
    for _ in range(n):
        t.matmul(st.ps["heat"][:], st.sb["onesrow"][:], st.sb["ones512"][:, 0:64],
                 start=True, stop=True)


def _barrier(st):
    blk = st.block
    tgt = dict(st.cnt)
    for en in ("sync", "tensor", "vector", "scalar", "gpsimd"):
        def mk(e, tgt=tgt):
            for s, v in tgt.items():
                if v > 0:
                    e.wait_ge(st.sem[s], v)
        getattr(blk, en)(mk)


def _ln_part1_v(st, v, xsrc):
    sb = st.sb
    v.drain()
    for b4 in range(4):
        v.bn_stats(sb["bn6"][:, b4 * 6:(b4 + 1) * 6], xsrc[:, b4 * 512:(b4 + 1) * 512])
    v.drain()
    v.bn_aggr(sb["mv"][:], sb["bn6"][:])
    v.drain()
    v.tensor_scalar_add(sb["vt"][:], sb["mv"][:, 1:2], EPS)
    v.drain()
    st.ln_vrec = INC(st, v.reciprocal(sb["vt"][:], sb["vt"][:]), "vv")


def _ln_part2_s(st, s):
    sb = st.sb
    W(s, st, "vv", st.ln_vrec)
    st.ln_esq = INC(st, s.activation(sb["rstd"][:], sb["vt"][:], AF.Sqrt), "ex")


def _ln_part3_v(st, v, xsrc, dst):
    sb = st.sb
    W(v, st, "ex", st.ln_esq)
    return INC(st, v.tensor_scalar(dst, xsrc, sb["mv"][:, 0:1], sb["rstd"][:],
                                   ALU.subtract, ALU.mult), "vv")


def _rope6(st, v, psrc, cos_t, sin_t, dst_bf):
    sb = st.sb
    pv = psrc.rearrange("p (h i two) -> p h i two", i=64, two=2)
    dv = dst_bf.rearrange("p (h i two) -> p h i two", i=64, two=2)
    a, b = pv[:, :, :, 0], pv[:, :, :, 1]
    ct = cos_t.rearrange("p (h i) -> p h i", i=64)
    sn = sin_t.rearrange("p (h i) -> p h i", i=64)
    r1 = sb["rt1"][:].rearrange("p (h i) -> p h i", i=64)
    r2 = sb["rt2"][:].rearrange("p (h i) -> p h i", i=64)
    v.drain()
    v.tensor_mul(r1, a, ct)
    v.tensor_mul(r2, b, sn)
    v.drain()
    v.tensor_sub(dv[:, :, :, 0], r1, r2)
    v.drain()
    v.tensor_mul(r1, a, sn)
    v.tensor_mul(r2, b, ct)
    v.drain()
    return v.tensor_add(dv[:, :, :, 1], r1, r2)


class TspPipe:

    NSTEP = 64

    def __init__(self, st):
        self.st = st
        self.i = 0
        self.c_tsp = [0] * self.NSTEP
        self.c_wr = [0] * self.NSTEP

    def _src_dst(self, i):
        d = self.st.d
        w = 0 if i < 32 else 1
        dc = (i % 32) // 2
        half = i % 2
        src = (d["w1b"] if w == 0 else d["w3b"])
        dst = (d["w1t"] if w == 0 else d["w3t"])
        return src, dst, dc, half

    def _emit_write(self, s, j):
        st = self.st
        src, dst, dc, half = self._src_dst(j)
        W(s, st, self.c_tsp[j])
        key = f"tw{j % 2}"
        out = dst[half * 32:(half + 1) * 32, :, dc * 128:(dc + 1) * 128]
        out = out.rearrange("hc p h -> p hc h")
        DMA(st, s, key, out,
            st.sb[f"tsp{j % 2}"][:, :].rearrange("p (hc h) -> p hc h", h=128))
        self.c_wr[j] = (key, st.cnt[key])

    def step(self):
        st = self.st
        if self.i >= self.NSTEP:
            return
        i = self.i
        self.i += 1
        blk = st.block

        @blk.scalar
        def _(s, i=i):
            src, dst, dc, half = self._src_dst(i)
            if i == 0:
                W(s, st, st.c_w1b)
            if i == 32:
                W(s, st, st.c_w3b)
            if i >= 2:
                W(s, st, self.c_wr[i - 2])
            key = f"t{i % 2}"
            DMA(st, s, key, st.sb[f"tsp{i % 2}"][:],
                src[half * 4096:(half + 1) * 4096, dc * 128:(dc + 1) * 128],
                transpose=True)
            self.c_tsp[i] = (key, st.cnt[key])
            if i >= 1:
                self._emit_write(s, i - 1)

    def finish(self):
        st = self.st
        blk = st.block
        while self.i < self.NSTEP:
            self.step()

        @blk.scalar
        def _(s):
            self._emit_write(s, self.NSTEP - 1)
        st.c_wt_done = [("tw0", st.cnt["tw0"]), ("tw1", st.cnt["tw1"])]


def _emit_preload(st):
    d, sb = st.d, st.sb
    blk = st.block

    @blk.sync
    def _(y):
        for k2, nm in enumerate(("identb", "identf", "h128b", "onesrow", "ones512",
                                 "onescol", "onesrowf")):
            src = "h128" if nm == "h128b" else nm
            DMA(st, y, "pre", sb[nm][:], d[src], first=(k2 == 0))
    st.c_preload = ("pre", st.cnt["pre"])


def _emit_gp_weights(st):
    d, sb = st.d, st.sb
    blk = st.block

    @blk.gpsimd
    def _(g):
        DMA(st, g, "gwk",
            sb["wkv"][:, :].rearrange("p (dc n) -> p dc n", n=1024)[:, :, 0:512],
            d["wk"].rearrange("(dc p) n -> p dc n", p=128))
        DMA(st, g, "gwk",
            sb["wkv"][:, :].rearrange("p (dc n) -> p dc n", n=1024)[:, :, 512:1024],
            d["wv"].rearrange("(dc p) n -> p dc n", p=128), first=False)
        DMA(st, g, "gwk", sb["bkv"][:, 0:512], d["bk"], first=False)
        DMA(st, g, "gwk", sb["bkv"][:, 512:1024], d["bv"], first=False)
        st.c_wkv = ("gwk", st.cnt["gwk"])
        st.c_w1b = ("gw1", DMA(st, g, "gw1", d["w1b"], d["w1"]))
        st.c_w3b = ("gw3", DMA(st, g, "gw3", d["w3b"], d["w3"]))
        for dc in range(16):
            DMA(st, g, "gwo", d["wob"][dc],
                d["wo"].rearrange("(hc p) (dc e) -> dc p hc e", p=128, e=128)[dc],
                first=(dc == 0))
        st.c_wob = ("gwo", st.cnt["gwo"])


def _emit_ph0(st):
    d, sb, ps = st.d, st.sb, st.ps
    blk = st.block

    ld_x = [0] * NKT
    v_xn = [0] * NKT
    m_tp4 = [[0] * 4 for _ in range(NKT)]
    e_xnf = [0] * NKT
    m_kv = [0] * NKT
    v_rope = [0] * NKT
    e_v = [0] * NKT
    m_ktp = [0] * NKT
    tp_free = [0, 0]

    for kt in range(NKT):
        i = kt % 2

        @blk.sync
        def _(y, kt=kt, i=i):
            if kt >= 2:
                W(y, st, "vv", v_rope[kt - 2])
            DMA(st, y, f"x{i}", sb[f"x{i}"][:], d["x"][kt * 128:(kt + 1) * 128, :])
            DMA(st, y, f"x{i}", sb[f"ckv{i}"][:],
                d["cos4kv"][kt * 128:(kt + 1) * 128, :], first=False)
            DMA(st, y, f"x{i}", sb[f"skv{i}"][:],
                d["sin4kv"][kt * 128:(kt + 1) * 128, :], first=False)
            ld_x[kt] = (f"x{i}", st.cnt[f"x{i}"])

        @blk.vector
        def _(v, kt=kt, i=i):
            W(v, st, ld_x[kt])
            _ln_part1_v(st, v, sb[f"x{i}"][:])

        @blk.scalar
        def _(s, kt=kt):
            _ln_part2_s(st, s)

        @blk.vector
        def _(v, kt=kt, i=i):
            if kt >= 2:
                W(v, st, "mm", m_tp4[kt - 2][3])
            v_xn[kt] = _ln_part3_v(st, v, sb[f"x{i}"][:], sb[f"xntm{i}"][:])

        for g4 in range(4):
            @blk.tensor
            def _(t, kt=kt, i=i, g4=g4):
                if kt == 0 and g4 == 0:
                    W(t, st, st.c_preload)
                HEAT(st, t)
                if g4 == 0:
                    W(t, st, "vv", v_xn[kt])
                b = g4 % 2
                W(t, st, "mm", tp_free[b])
                last = None
                for q in range(4):
                    dcx = 4 * g4 + q
                    last = t.transpose(ps[f"tp{b}"][:, q * 128:(q + 1) * 128],
                                       sb[f"xntm{i}"][:, dcx * 128:(dcx + 1) * 128],
                                       sb["identb"][:])
                m_tp4[kt][g4] = INC(st, last, "mm")

            @blk.vector
            def _(v, kt=kt, i=i, g4=g4):
                if g4 == 0 and kt >= 2:
                    W(v, st, "mm", m_kv[kt - 2])
                b = g4 % 2
                W(v, st, "mm", m_tp4[kt][g4])
                tp_free[b] = ("vv", INC(st, v.tensor_copy(
                    sb[f"xnf{i}"][:, g4 * 512:(g4 + 1) * 512], ps[f"tp{b}"][:]), "vv"))
                if g4 == 3:
                    e_xnf[kt] = ("vv", st.cnt["vv"])

        @blk.tensor
        def _(t, kt=kt, i=i):
            HEAT(st, t, 2)
            W(t, st, e_xnf[kt])
            if kt == 0:
                W(t, st, st.c_wkv)
                W(t, st, st.c_preload)
            if kt >= 2:
                W(t, st, "vv", v_rope[kt - 2])
                W(t, st, e_v[kt - 2])
            t.matmul(ps[f"pk{i}"][:], sb["onesrow"][:], sb["bkv"][:, 0:512],
                     start=True, stop=False)
            t.matmul(ps[f"pv{i}"][:], sb["onesrow"][:], sb["bkv"][:, 512:1024],
                     start=True, stop=False)
            for dc in range(16):
                lt = sb[f"xnf{i}"][:, dc * 128:(dc + 1) * 128]
                t.matmul(ps[f"pk{i}"][:], lt, sb["wkv"][:, dc * 1024:dc * 1024 + 512],
                         start=False, stop=(dc == 15))
                last = t.matmul(ps[f"pv{i}"][:], lt,
                                sb["wkv"][:, dc * 1024 + 512:(dc + 1) * 1024],
                                start=False, stop=(dc == 15))
            m_kv[kt] = INC(st, last, "mm")

        @blk.vector
        def _(v, kt=kt, i=i):
            W(v, st, "mm", m_kv[kt])
            if kt >= 2:
                W(v, st, "mm", m_ktp[kt - 2])
            ins = _rope6(st, v, ps[f"pk{i}"][:], sb[f"ckv{i}"][:], sb[f"skv{i}"][:],
                         sb[f"ktm{i}"][:])
            v_rope[kt] = INC(st, ins, "vv")

        @blk.vector
        def _(v, kt=kt, i=i):
            W(v, st, "mm", m_kv[kt])
            e_v[kt] = ("vv", INC(st, v.tensor_copy(sb["v_tm"][:, kt * 512:(kt + 1) * 512],
                                                   ps[f"pv{i}"][:]), "vv"))

        @blk.tensor
        def _(t, kt=kt, i=i):
            W(t, st, "vv", v_rope[kt])
            b = kt % 2
            W(t, st, "mm", tp_free[b])
            last = None
            for j in range(4):
                last = t.transpose(ps[f"tp{b}"][:, j * 128:(j + 1) * 128],
                                   sb[f"ktm{i}"][:, j * 128:(j + 1) * 128],
                                   sb["identb"][:])
            m_ktp[kt] = INC(st, last, "mm")

        @blk.vector
        def _(v, kt=kt):
            b = kt % 2
            W(v, st, "mm", m_ktp[kt])
            last = None
            for j in range(4):
                last = v.tensor_copy(sb["k_fm"][:, j * T + kt * 128: j * T + (kt + 1) * 128],
                                     ps[f"tp{b}"][:, j * 128:(j + 1) * 128])
            tp_free[b] = ("vv", INC(st, last, "vv"))

        if kt >= 16:
            st.tsp.step()
            st.tsp.step()


def _emit_ph1(st):
    d, sb, ps = st.d, st.sb, st.ps
    blk = st.block

    @blk.gpsimd
    def _(g):
        DMA(st, g, "gbq", sb["bq"][:], d["bq"])
    c_bq = ("gbq", st.cnt["gbq"])

    tp_free = [0, 0]
    xnf_free = [0, 0]
    cos_free = [0, 0]
    qst_free = [0, 0]
    pq_free = [0, 0]
    x_free = [0, 0]
    xntm_free = [0, 0]
    spill_free = [0, 0]
    qfst_free = [0, 0]
    wq_free = 0

    for pa in range(2):
        @blk.gpsimd
        def _(g, pa=pa):
            if pa == 1:
                W(g, st, "mm", wq_free)
            DMA(st, g, "gwq",
                sb["wq"][:, :].rearrange("p (dc n) -> p dc n", n=1024),
                d["wq"].rearrange("(dc p) n -> p dc n", p=128)[:, :, pa * 1024:(pa + 1) * 1024])
        c_wq = ("gwq", st.cnt["gwq"])

        ld_xo = [0] * NTL
        v_xn = [0] * NTL
        m_q = [0] * NTL
        v_rope = [0] * NTL
        m_qtp = [0] * NTL

        for tl in range(NTL):
            i = tl % 2

            @blk.sync
            def _(y, tl=tl, i=i, pa=pa):
                W(y, st, "vv", cos_free[i])
                if pa == 0:
                    W(y, st, "vv", x_free[i])
                    DMA(st, y, f"x{i}", sb[f"xo{i}"][:],
                        d["xo"][tl * 128:(tl + 1) * 128, :])
                else:
                    W(y, st, "mm", xnf_free[i])
                    W(y, st, spill_free[i])
                    DMA(st, y, f"x{i}",
                        sb[f"xnf{i}"][:, :].rearrange("p (dc t) -> p dc t", t=128),
                        d["xnh"][:, :, tl * 128:(tl + 1) * 128].rearrange("dc p t -> p dc t"))
                DMA(st, y, f"x{i}", sb[f"co{i}"][:],
                    d["cos4own"][tl * 128:(tl + 1) * 128, :], first=False)
                DMA(st, y, f"x{i}", sb[f"so{i}"][:],
                    d["sin4own"][tl * 128:(tl + 1) * 128, :], first=False)
                ld_xo[tl] = (f"x{i}", st.cnt[f"x{i}"])

            if pa == 0:
                @blk.vector
                def _(v, tl=tl, i=i):
                    W(v, st, ld_xo[tl])
                    _ln_part1_v(st, v, sb[f"xo{i}"][:])

                @blk.scalar
                def _(s, tl=tl):
                    _ln_part2_s(st, s)

                @blk.vector
                def _(v, tl=tl, i=i):
                    W(v, st, "mm", xntm_free[i])
                    v_xn[tl] = _ln_part3_v(st, v, sb[f"xo{i}"][:], sb[f"xntm{i}"][:])
                    x_free[i] = v_xn[tl]

                if not hasattr(st, "_tpg"):
                    st._tpg = {}
                for g4 in range(4):
                    @blk.tensor
                    def _(t, tl=tl, i=i, g4=g4):
                        if g4 == 0:
                            W(t, st, "vv", v_xn[tl])
                        b = g4 % 2
                        W(t, st, "mm", tp_free[b])
                        last = None
                        for q in range(4):
                            dcx = 4 * g4 + q
                            last = t.transpose(ps[f"tq{b}"][:, q * 128:(q + 1) * 128],
                                               sb[f"xntm{i}"][:, dcx * 128:(dcx + 1) * 128],
                                               sb["identb"][:])
                        st._tpg[(tl, g4)] = INC(st, last, "mm")
                        if g4 == 3:
                            xntm_free[i] = st._tpg[(tl, g4)]

                    @blk.vector
                    def _(v, tl=tl, i=i, g4=g4):
                        if g4 == 0:
                            W(v, st, "mm", xnf_free[i])
                            W(v, st, spill_free[i])
                        b = g4 % 2
                        W(v, st, "mm", st._tpg[(tl, g4)])
                        tp_free[b] = ("vv", INC(st, v.tensor_copy(
                            sb[f"xnf{i}"][:, g4 * 512:(g4 + 1) * 512], ps[f"tq{b}"][:]), "vv"))
                        if g4 == 3:
                            st._exnf = ("vv", st.cnt["vv"])

                @blk.gpsimd
                def _(g, tl=tl, i=i):
                    W(g, st, st._exnf)
                    DMA(st, g, f"gx{i}",
                        d["xnh"][:, :, tl * 128:(tl + 1) * 128].rearrange("dc p t -> p dc t"),
                        sb[f"xnf{i}"][:, :].rearrange("p (dc t) -> p dc t", t=128))
                    spill_free[i] = (f"gx{i}", st.cnt[f"gx{i}"])
                st._exnf_t = st._exnf
            @blk.tensor
            def _(t, tl=tl, i=i, pa=pa):
                HEAT(st, t, 2)
                if pa == 0:
                    W(t, st, st._exnf_t)
                else:
                    W(t, st, ld_xo[tl])
                if tl == 0:
                    W(t, st, c_wq)
                    W(t, st, c_bq)
                pb = tl % 2
                W(t, st, "vv", pq_free[pb])
                t.matmul(ps[f"pq{pb}"][:, 0:512], sb["onesrow"][:],
                         sb["bq"][:, pa * 1024:pa * 1024 + 512], start=True, stop=False)
                t.matmul(ps[f"pq{pb}"][:, 512:1024], sb["onesrow"][:],
                         sb["bq"][:, pa * 1024 + 512:(pa + 1) * 1024], start=True, stop=False)
                for dc in range(16):
                    lt = sb[f"xnf{i}"][:, dc * 128:(dc + 1) * 128]
                    t.matmul(ps[f"pq{pb}"][:, 0:512], lt,
                             sb["wq"][:, dc * 1024:dc * 1024 + 512],
                             start=False, stop=(dc == 15))
                    last = t.matmul(ps[f"pq{pb}"][:, 512:1024], lt,
                                    sb["wq"][:, dc * 1024 + 512:(dc + 1) * 1024],
                                    start=False, stop=(dc == 15))
                m_q[tl] = INC(st, last, "mm")
                xnf_free[i] = m_q[tl]

            @blk.vector
            def _(v, tl=tl, i=i):
                W(v, st, "mm", m_q[tl])
                W(v, st, "mm", qst_free[i])
                pb = tl % 2
                for hb in range(2):
                    ins = _rope6(st, v, ps[f"pq{pb}"][:, hb * 512:(hb + 1) * 512],
                                 sb[f"co{i}"][:], sb[f"so{i}"][:],
                                 sb[f"qst{i}"][:, hb * 512:(hb + 1) * 512])
                v_rope[tl] = INC(st, ins, "vv")
                pq_free[pb] = v_rope[tl]
                cos_free[i] = v_rope[tl]

            for hb in range(2):
                @blk.tensor
                def _(t, tl=tl, i=i, hb=hb):
                    if hb == 0:
                        W(t, st, "vv", v_rope[tl])
                    b = hb % 2
                    W(t, st, "mm", tp_free[b])
                    last = None
                    for q in range(4):
                        last = t.transpose(
                            ps[f"tq{b}"][:, q * 128:(q + 1) * 128],
                            sb[f"qst{i}"][:, (hb * 4 + q) * 128:(hb * 4 + q + 1) * 128],
                            sb["identb"][:])
                    st._tpg[(tl, "q", hb)] = INC(st, last, "mm")
                    if hb == 1:
                        m_qtp[tl] = st.cnt["mm"]
                        qst_free[i] = m_qtp[tl]

                @blk.vector
                def _(v, tl=tl, pa=pa, hb=hb, i=i):
                    b = hb % 2
                    W(v, st, "mm", st._tpg[(tl, "q", hb)])
                    if hb == 0:
                        W(v, st, qfst_free[i])
                    tp_free[b] = ("vv", INC(st, v.tensor_copy(
                        sb[f"qfst{i}"][:, hb * 512:(hb + 1) * 512], ps[f"tq{b}"][:]), "vv"))
                    if hb == 1:
                        st._eqf = ("vv", st.cnt["vv"])

            @blk.gpsimd
            def _(g, tl=tl, pa=pa, i=i):
                W(g, st, st._eqf)
                for hb in range(2):
                    dst = bass.AP(st.dram["qh"],
                                  (pa * 2 + hb) * 128 * NOWN + tl * 128,
                                  [[NOWN, 128], [4 * 128 * NOWN, 4], [1, 128]])
                    DMA(st, g, f"gq{i}", dst,
                        sb[f"qfst{i}"][:, hb * 512:(hb + 1) * 512]
                        .rearrange("p (q t) -> p q t", t=128), first=(hb == 0))
                qfst_free[i] = (f"gq{i}", st.cnt[f"gq{i}"])

            st.tsp.step()
        wq_free = m_q[NTL - 1]
    st.tsp.finish()


def _emit_ph2(st):
    d, sb, ps = st.d, st.sb, st.ps
    blk = st.block

    s_free = [0, 0]
    p_free = [0, 0, 0]
    pav_free = [0, 0]
    pl_free = [0, 0]
    linv_free = 0
    bc_free = 0
    stg_free = [0, 0]
    mask_free = [0, 0]
    qtl_free = [0, 0]
    ld_mask = [0] * NTL

    if st.preload_tables:
        @blk.scalar
        def _(s):
            s.preload_act_table(AF.Exp)

    gidx = 0
    for tl in range(NTL):
        mbuf = tl % 2

        @blk.sync
        def _(y, tl=tl, mbuf=mbuf):
            W(y, st, "mm", mask_free[mbuf])
            W(y, st, "mm", qtl_free[mbuf])
            if tl == 0:
                W(y, st, ("gq0", st.cnt["gq0"]))
                W(y, st, ("gq1", st.cnt["gq1"]))
            DMA(st, y, f"m{mbuf}",
                sb[f"mask{mbuf}"][:, :].rearrange("p (r n) -> p r n", n=512),
                d["maskt"][tl].rearrange("r p n -> p r n"))
            DMA(st, y, f"m{mbuf}",
                sb[f"qtl{mbuf}"][:, :].rearrange("p (h t) -> p h t", t=128),
                d["qh"][:, :, tl * 128:(tl + 1) * 128].rearrange("h p t -> p h t"),
                first=False)
            ld_mask[tl] = (f"m{mbuf}", st.cnt[f"m{mbuf}"])

        nch = 2 * tl + 2
        for j in range(4):
            ab = gidx % 2
            m_sc = [0] * nch
            e_exp = [0] * nch
            m_av = [0] * nch
            p_free0 = list(p_free)

            def emit_lav(c):
                pb = c % 3

                @blk.tensor
                def _(t, j=j, c=c, pb=pb, ab=ab, nch=nch):
                    W(t, st, "ex", e_exp[c])
                    t.matmul(ps[f"pl{ab}"][:], sb["onescol"][:], sb[f"P{pb}"][:],
                             start=(c == 0), stop=(c == nch - 1))
                    last = t.matmul(ps[f"pav{ab}"][:],
                                    sb["v_tm"][:, c * 512 + j * 128: c * 512 + (j + 1) * 128],
                                    sb[f"P{pb}"][:],
                                    start=(c == 0), stop=(c == nch - 1))
                    m_av[c] = INC(st, last, "mm")
                    p_free[pb] = m_av[c]

            ex0 = st.cnt["ex"]
            for c in range(nch):
                e_exp[c] = ex0 + c + 1

            for c in range(nch):
                sbk = c % 2
                pb = c % 3

                @blk.tensor
                def _(t, tl=tl, j=j, c=c, sbk=sbk, pb=pb, ab=ab, mbuf=mbuf, nch=nch):
                    HEAT(st, t)
                    W(t, st, "ex", s_free[sbk])
                    if c >= 3:
                        W(t, st, "mm", m_av[c - 3])
                    else:
                        W(t, st, "mm", p_free[pb])
                        if c == 0:
                            W(t, st, "vv", pav_free[ab])
                            W(t, st, "vv", pl_free[ab])
                    if c >= 2 * tl:
                        W(t, st, ld_mask[tl])
                    if c == 0:
                        W(t, st, ld_mask[tl])
                    lt = sb["k_fm"][:, j * T + c * 128: j * T + (c + 1) * 128]
                    ride = (c >= 2 * tl)
                    last = t.matmul(ps[f"S{sbk}"][:], lt,
                                    sb[f"qtl{mbuf}"][:, j * 512:(j + 1) * 512],
                                    start=True, stop=not ride)
                    if ride:
                        r = c - 2 * tl
                        last = t.matmul(ps[f"S{sbk}"][:], sb["identb"][:],
                                        sb[f"mask{mbuf}"][:, r * 512:(r + 1) * 512],
                                        start=False, stop=True)
                    m_sc[c] = INC(st, last, "mm")
                    if ride and c == nch - 1:
                        mask_free[mbuf] = m_sc[c]
                        if j == 3:
                            qtl_free[mbuf] = m_sc[c]
                s_free[sbk] = e_exp[c]

                if c >= 2:
                    emit_lav(c - 2)
            if nch >= 2:
                emit_lav(nch - 2)
            emit_lav(nch - 1)

            @blk.scalar
            def _(s, nch=nch, ab=ab):
                for c in range(nch):
                    sbk = c % 2
                    pb = c % 3
                    s.wait_ge(st.sem["mm"], m_sc[c])
                    if c >= 3:
                        s.wait_ge(st.sem["mm"], m_av[c - 3])
                    else:
                        if p_free0[pb] > 0:
                            s.wait_ge(st.sem["mm"], p_free0[pb])
                    got = INC(st, s.activation(sb[f"P{pb}"][:], ps[f"S{sbk}"][:],
                                               AF.Exp, scale=SCALE), "ex")
                    assert got == e_exp[c], (got, e_exp[c])

            @blk.vector
            def _(v, ab=ab, nch=nch):
                W(v, st, "mm", m_av[nch - 1])
                W(v, st, "mm", linv_free)
                st.c_li = INC(st, v.reciprocal(sb["linv"][:], ps[f"pl{ab}"][:]), "vv")
                pl_free[ab] = st.c_li

            @blk.tensor
            def _(t):
                W(t, st, "vv", st.c_li)
                W(t, st, "vv", bc_free)
                st.c_bc = INC(st, t.matmul(ps["pbc"][:], sb["onesrowf"][:], sb["linv"][:],
                                           start=True, stop=True), "mm")
            linv_free = st.c_bc

            stg = gidx % 2

            @blk.vector
            def _(v, ab=ab, stg=stg):
                W(v, st, "mm", st.c_bc)
                v.tensor_copy(sb["bc_sb"][:], ps["pbc"][:])
                v.drain()
                W(v, st, stg_free[stg])
                st.c_av = INC(st, v.tensor_mul(sb[f"avst{stg}"][:], ps[f"pav{ab}"][:],
                                               sb["bc_sb"][:]), "vv")
                pav_free[ab] = st.c_av
            bc_free = st.c_av

            @blk.gpsimd
            def _(g, tl=tl, j=j, stg=stg):
                W(g, st, "vv", st.c_av)
                dst = bass.AP(st.dram["avh"],
                              (j * 128 * NOWN) + tl * 128,
                              [[NOWN, 128], [4 * 128 * NOWN, 4], [1, 128]])
                DMA(st, g, f"gav{stg}", dst,
                    sb[f"avst{stg}"][:, :].rearrange("p (g4 t) -> p g4 t", t=128))
                stg_free[stg] = (f"gav{stg}", st.cnt[f"gav{stg}"])
            gidx += 1


class W2Prep:

    def __init__(self, st):
        self.st = st
        self.v_bfly = [0] * 16
        self.g_out = [0] * 16
        self.tpw_free = [0, 0]
        self.ld = [0] * 16
        self.cp_done = [0] * 16
        self.e_w_last = [0] * 16
        self.m_tpw = [[0] * 16 for _ in range(16)]

    def part_load(self, dc):
        st = self.st
        d, sb = st.d, st.sb
        blk = st.block

        @blk.sync
        def _(y, dc=dc):
            if dc >= 1:
                W(y, st, "vv", self.v_bfly[dc - 1])
            DMA(st, y, "wp", sb["wpB"][:], d["w2"][dc * 128:(dc + 1) * 128, :])
            self.ld[dc] = ("wp", st.cnt["wp"])

    def part_tr(self, dc, half):
        st = self.st
        sb, ps = st.sb, st.ps
        blk = st.block
        for grp in range(8 * half, 8 * half + 8):
            b = grp % 2

            @blk.tensor
            def _(t, dc=dc, grp=grp, b=b):
                if grp == 0:
                    W(t, st, self.ld[dc])
                    if dc >= 1:
                        W(t, st, "vv", self.v_bfly[dc - 1])
                W(t, st, "mm", self.tpw_free[b])
                last = None
                for q in range(4):
                    hcx = 4 * grp + q
                    last = t.transpose(ps[f"tpw{b}"][:, q * 128:(q + 1) * 128],
                                       sb["wpB"][:, hcx * 128:(hcx + 1) * 128],
                                       sb["identf"][:])
                self.m_tpw[dc][grp] = INC(st, last, "mm")

            @blk.vector
            def _(v, dc=dc, grp=grp, b=b):
                W(v, st, "mm", self.m_tpw[dc][grp])
                self.tpw_free[b] = ("vv", INC(st, v.tensor_copy(
                    sb["wpA"][:, grp * 512:(grp + 1) * 512], ps[f"tpw{b}"][:]), "vv"))
                if grp == 15:
                    self.cp_done[dc] = ("vv", st.cnt["vv"])

    def part_bf(self, dc, piece):
        st = self.st
        sb = st.sb
        blk = st.block

        @blk.vector
        def _(v, dc=dc, piece=piece):
            if piece == 0:
                W(v, st, self.cp_done[dc])
            bufs = [sb["wpA"], sb["wpB"]]
            for stg in (2 * piece, 2 * piece + 1):
                cur = bufs[stg % 2]
                dstt = sb["w2xb"] if stg == 5 else bufs[(stg + 1) % 2]
                if stg == 5 and dc >= 1:
                    W(v, st, ("ex", self.e_w_last[dc - 1]))
                bit = 1 << stg
                hi = 64 // (2 * bit)
                lo = bit
                cv = cur[:, :].rearrange("p (h t l e) -> p h t l e", h=hi, t=2, l=lo)
                ov = dstt[:, :].rearrange("p (h t l e) -> p h t l e", h=hi, t=2, l=lo)
                v.drain()
                for eh in range(2):
                    e128 = cv.shape[4]
                    eh_sl = slice(eh * e128 // 2, (eh + 1) * e128 // 2)
                    v.tensor_add(ov[:, :, 0, :, eh_sl], cv[:, :, 0, :, eh_sl],
                                 cv[:, :, 1, :, eh_sl])
                    last = v.tensor_sub(ov[:, :, 1, :, eh_sl], cv[:, :, 0, :, eh_sl],
                                        cv[:, :, 1, :, eh_sl])
            if piece == 2:
                self.v_bfly[dc] = INC(st, last, "vv")

    def part_h(self, dc, half):
        st = self.st
        sb, ps = st.sb, st.ps
        blk = st.block
        m_h = [0] * 8
        ex0 = st.cnt["ex"]

        for k in range(8):
            s512 = 8 * half + k
            b = s512 % 2

            @blk.tensor
            def _(t, dc=dc, s512=s512, b=b, k=k):
                if k == 0 and half == 0:
                    W(t, st, "vv", self.v_bfly[dc])
                W(t, st, "mm", self.tpw_free[b])
                m_h[k] = INC(st, t.matmul(ps[f"tpw{b}"][:], sb["h128b"][:],
                                          sb["w2xb"][:, s512 * 512:(s512 + 1) * 512],
                                          start=True, stop=True), "mm")
            self.tpw_free[b] = ("ex", ex0 + k + 1)

        @blk.scalar
        def _(s, dc=dc, half=half):
            for k in range(8):
                s512 = 8 * half + k
                b = s512 % 2
                if k == 0 and half == 0 and dc >= 1:
                    W(s, st, self.g_out[dc - 1])
                s.wait_ge(st.sem["mm"], m_h[k])
                got = INC(st, s.activation(sb["w2xo"][:, s512 * 512:(s512 + 1) * 512],
                                           ps[f"tpw{b}"][:], AF.Copy, scale=RSQN), "ex")
                assert got == ex0 + k + 1
            if half == 1:
                self.e_w_last[dc] = st.cnt["ex"]

    def part_spill(self, dc):
        st = self.st
        blk = st.block

        @blk.gpsimd
        def _(g, dc=dc):
            W(g, st, ("ex", self.e_w_last[dc]))
            DMA(st, g, "gw2", st.d["w2x"][dc], st.sb["w2xo"][:])
            self.g_out[dc] = ("gw2", st.cnt["gw2"])

    def steps_for(self, hc):
        dc = hc // 4
        ph = hc % 4
        if dc >= 16:
            return
        if ph == 0:
            self.part_load(dc)
            self.part_tr(dc, 0)
        elif ph == 1:
            self.part_tr(dc, 1)
            self.part_bf(dc, 0)
        elif ph == 2:
            self.part_bf(dc, 1)
            self.part_bf(dc, 2)
        else:
            self.part_h(dc, 0)
            self.part_h(dc, 1)
            self.part_spill(dc)


def _emit_ph3a(st):
    d, sb, ps = st.d, st.sb, st.ps
    blk = st.block

    @blk.sync
    def _(y):
        DMA(st, y, "pre",
            sb["xnall"][:, :].rearrange("p (dc t) -> p dc t", t=NOWN),
            d["xnh"].rearrange("dc p t -> p dc t"))
    c_xnall = ("pre", st.cnt["pre"])

    if st.preload_tables:
        @blk.scalar
        def _(s):
            s.preload_act_table(AF.Silu)

    ld_w = [0] * 64
    m_g = [[0] * 4 for _ in range(64)]
    m_u = [[0] * 4 for _ in range(64)]
    e_sg = [[0] * 4 for _ in range(64)]
    v_h = [[0] * 4 for _ in range(64)]
    g_spill = [0] * 64
    w2 = st.w2prep

    for hc in range(64):
        i = hc % 2

        @blk.sync
        def _(y, hc=hc, i=i):
            if hc == 0:
                for cw in st.c_wt_done:
                    W(y, st, cw)
            if hc >= 2:
                W(y, st, "mm", m_g[hc - 2][0])
                W(y, st, "mm", m_u[hc - 2][1])
            DMA(st, y, f"wa{i}", sb[f"w1tb{i}"][:], d["w1t"][hc])
            DMA(st, y, f"wa{i}", sb[f"w3tb{i}"][:], d["w3t"][hc], first=False)
            ld_w[hc] = (f"wa{i}", st.cnt[f"wa{i}"])

        ex0 = st.cnt["ex"]
        vv0 = st.cnt["vv"]
        for tsub in range(4):
            e_sg[hc][tsub] = ex0 + tsub + 1
            v_h[hc][tsub] = vv0 + tsub + 1

        @blk.tensor
        def _(t, hc=hc, i=i):
            if hc == 0:
                W(t, st, c_xnall)
            W(t, st, ld_w[hc])
            if hc >= 1:
                W(t, st, "ex", e_sg[hc - 1][3])
            for dc in range(16):
                for tsub in range(4):
                    last = t.matmul(
                        ps[f"pg{tsub}"][:], sb[f"w1tb{i}"][:, dc * 128:(dc + 1) * 128],
                        sb["xnall"][:, dc * NOWN + tsub * 512: dc * NOWN + (tsub + 1) * 512],
                        start=(dc == 0), stop=(dc == 15))
            m_g[hc][0] = INC(st, last, "mm")

        @blk.scalar
        def _(s, hc=hc):
            for tsub in range(4):
                s.wait_ge(st.sem["mm"], m_g[hc][0])
                if hc >= 1:
                    s.wait_ge(st.sem["vv"], v_h[hc - 1][3])
                got = INC(st, s.activation(sb[f"sg{tsub}"][:], ps[f"pg{tsub}"][:],
                                           AF.Silu), "ex")
                assert got == e_sg[hc][tsub], (got, e_sg[hc][tsub])

        for pair in range(2):
            @blk.tensor
            def _(t, hc=hc, i=i, pair=pair):
                if pair == 0:
                    if hc >= 1:
                        W(t, st, "vv", v_h[hc - 1][3])
                else:
                    W(t, st, "vv", v_h[hc][1])
                for dc in range(16):
                    for tsub in (2 * pair, 2 * pair + 1):
                        last = t.matmul(
                            ps[f"pu{tsub % 2}"][:], sb[f"w3tb{i}"][:, dc * 128:(dc + 1) * 128],
                            sb["xnall"][:, dc * NOWN + tsub * 512: dc * NOWN + (tsub + 1) * 512],
                            start=(dc == 0), stop=(dc == 15))
                m_u[hc][pair] = INC(st, last, "mm")

            @blk.vector
            def _(v, hc=hc, i=i, pair=pair):
                for tsub in (2 * pair, 2 * pair + 1):
                    v.wait_ge(st.sem["ex"], e_sg[hc][tsub])
                    v.wait_ge(st.sem["mm"], m_u[hc][pair])
                    if pair == 0 and tsub == 0 and hc >= 2:
                        W(v, st, g_spill[hc - 2])
                    got = INC(st, v.tensor_mul(
                        sb[f"hst{i}"][:, tsub * 512:(tsub + 1) * 512],
                        sb[f"sg{tsub}"][:], ps[f"pu{tsub % 2}"][:]), "vv")
                    assert got == v_h[hc][tsub], (got, v_h[hc][tsub])

        @blk.gpsimd
        def _(g, hc=hc, i=i):
            W(g, st, "vv", v_h[hc][3])
            DMA(st, g, f"gh{i}", d["hidh"][hc], sb[f"hst{i}"][:])
            g_spill[hc] = (f"gh{i}", st.cnt[f"gh{i}"])

        w2.steps_for(hc)


def _emit_ph3b(st):
    d, sb, ps = st.d, st.sb, st.ps
    blk = st.block

    @blk.gpsimd
    def _(g):
        DMA(st, g, "gbo", sb["bo"][:], d["bo"])
    c_bo = ("gbo", st.cnt["gbo"])

    ld_tb = [0] * 4
    ld_wd = [0] * 64
    m_o = [0] * 64
    e_o = [0] * 64
    m_tr = [0] * 64
    v_add = [0] * 64
    g_out = [0] * 4

    for tb in range(4):
        @blk.sync
        def _(y, tb=tb):
            if tb > 0:
                W(y, st, "vv", v_add[tb * 16 - 1])
                W(y, st, g_out[tb - 1])
                W(y, st, "mm", m_o[tb * 16 - 1])
            DMA(st, y, "tbs",
                sb["hidtb"][:, :].rearrange("p (hc t) -> p hc t", t=512),
                d["hidh"][:, :, tb * 512:(tb + 1) * 512].rearrange("hc p t -> p hc t"))
            DMA(st, y, "tbs",
                sb["avtb"][:, :].rearrange("p (hc t) -> p hc t", t=512),
                d["avh"][:, :, tb * 512:(tb + 1) * 512].rearrange("hc p t -> p hc t"),
                first=False)
            for q in range(4):
                tl = 4 * tb + q
                DMA(st, y, "tbs", sb[f"xst{q}"][:],
                    d["xo"][tl * 128:(tl + 1) * 128, :], first=False)
            ld_tb[tb] = ("tbs", st.cnt["tbs"])

        for dc in range(16):
            k = tb * 16 + dc
            i = dc % 2

            @blk.sync
            def _(y, tb=tb, dc=dc, k=k, i=i):
                if k >= 2:
                    W(y, st, "mm", m_o[k - 2])
                if k == 0:
                    W(y, st, ("gw2", st.cnt["gw2"]))
                    W(y, st, st.c_wob)
                DMA(st, y, f"wd{i}", sb[f"w2xd{i}"][:], d["w2x"][dc])
                DMA(st, y, f"wd{i}", sb[f"wod{i}"][:], d["wob"][dc], first=False)
                ld_wd[k] = (f"wd{i}", st.cnt[f"wd{i}"])

            @blk.tensor
            def _(t, tb=tb, dc=dc, k=k, i=i):
                HEAT(st, t)
                W(t, st, ld_tb[tb])
                W(t, st, ld_wd[k])
                if k == 0:
                    W(t, st, c_bo)
                ob = dc % 2
                if k >= 2:
                    W(t, st, e_o[k - 2])
                t.matmul(ps[f"po{ob}"][:], sb["bo"][:, dc * 128:(dc + 1) * 128],
                         sb["ones512"][:], start=True, stop=False)
                for hc in range(16):
                    t.matmul(ps[f"po{ob}"][:], sb[f"wod{i}"][:, hc * 128:(hc + 1) * 128],
                             sb["avtb"][:, hc * 512:(hc + 1) * 512],
                             start=False, stop=False)
                for hc in range(64):
                    last = t.matmul(ps[f"po{ob}"][:], sb[f"w2xd{i}"][:, hc * 128:(hc + 1) * 128],
                                    sb["hidtb"][:, hc * 512:(hc + 1) * 512],
                                    start=False, stop=(hc == 63))
                m_o[k] = INC(st, last, "mm")

            @blk.vector
            def _(v, k=k, dc=dc):
                ob = dc % 2
                W(v, st, "mm", m_o[k])
                if k >= 2:
                    W(v, st, "mm", m_tr[k - 2])
                e_o[k] = ("vv", INC(st, v.tensor_copy(sb[f"pev{ob}"][:], ps[f"po{ob}"][:]), "vv"))

            @blk.tensor
            def _(t, k=k, dc=dc):
                ob = dc % 2
                W(t, st, e_o[k])
                if k >= 2:
                    W(t, st, "vv", v_add[k - 2])
                last = None
                for q in range(4):
                    last = t.transpose(ps[f"ptr{ob}"][:, q * 128:(q + 1) * 128],
                                       sb[f"pev{ob}"][:, q * 128:(q + 1) * 128],
                                       sb["identf"][:])
                m_tr[k] = INC(st, last, "mm")

            @blk.vector
            def _(v, k=k, dc=dc, tb=tb):
                ob = dc % 2
                W(v, st, "mm", m_tr[k])
                if tb > 0 and dc == 0:
                    W(v, st, g_out[tb - 1])
                last = None
                for q in range(4):
                    last = v.tensor_add(sb[f"ost{q}"][:, dc * 128:(dc + 1) * 128],
                                        ps[f"ptr{ob}"][:, q * 128:(q + 1) * 128],
                                        sb[f"xst{q}"][:, dc * 128:(dc + 1) * 128])
                v_add[k] = INC(st, last, "vv")

        @blk.gpsimd
        def _(g, tb=tb):
            W(g, st, "vv", v_add[tb * 16 + 15])
            for q in range(4):
                tl = 4 * tb + q
                DMA(st, g, "gout", d["out"][tl * 128:(tl + 1) * 128, :],
                    sb[f"ost{q}"][:], first=(q == 0))
            g_out[tb] = ("gout", st.cnt["gout"])


def _emit_final(st):
    blk = st.block
    tot = dict(st.cnt)

    @blk.gpsimd
    def _(g):
        for s in DMA_SEMS:
            if s.startswith("g") and tot[s] > 0:
                g.wait_ge(st.sem[s], tot[s])

    @blk.sync
    def _(y):
        for s in DMA_SEMS:
            if tot[s] > 0:
                y.wait_ge(st.sem[s], tot[s])


def build(upto=4, dbg=False, preload_tables=False):
    nc = bass.Bass("TRN2", target_bir_lowering=False)
    st = St()
    st.preload_tables = preload_tables
    st.nc = nc

    dram = {}

    def din(name, shape, dt):
        dram[name] = nc.dram_tensor(name, list(shape), dt, kind="ExternalInput")

    def dint(name, shape, dt):
        dram[name] = nc.dram_tensor(name, list(shape), dt, kind="Internal")

    def do_(name, shape, dt):
        dram[name] = nc.dram_tensor(name, list(shape), dt, kind="ExternalOutput")

    din("x", (T, C), F32)
    din("xo", (NOWN, C), F32)
    din("wq", (C, C), F32)
    din("wk", (C, 512), F32)
    din("wv", (C, 512), F32)
    din("wo", (C, C), F32)
    din("w1", (HID, C), F32)
    din("w2", (C, HID), F32)
    din("w3", (HID, C), F32)
    din("bq", (1, C), F32)
    din("bk", (1, 512), F32)
    din("bv", (1, 512), F32)
    din("bo", (1, C), F32)
    din("identb", (128, 128), BF16)
    din("identf", (128, 128), F32)
    din("h128", (128, 128), BF16)
    din("onesrow", (1, 128), BF16)
    din("ones512", (1, 512), BF16)
    din("onescol", (128, 1), BF16)
    din("onesrowf", (1, 128), F32)
    din("cos4kv", (T, 256), F32)
    din("sin4kv", (T, 256), F32)
    din("cos4own", (NOWN, 256), F32)
    din("sin4own", (NOWN, 256), F32)
    din("maskt", (NTL, 2, 128, 512), BF16)

    dsc = do_ if dbg else dint
    dint("w1b", (HID, C), BF16)
    dint("w3b", (HID, C), BF16)
    dint("wob", (16, 128, 16 * 128), BF16)
    dsc("w1t", (64, 128, 16 * 128), BF16)
    dsc("w3t", (64, 128, 16 * 128), BF16)
    dsc("w2x", (16, 128, 64 * 128), BF16)
    dsc("xnh", (16, 128, NOWN), BF16)
    dsc("qh", (16, 128, NOWN), BF16)
    dsc("avh", (16, 128, NOWN), BF16)
    dsc("hidh", (64, 128, NOWN), BF16)

    do_("out", (NOWN, C), F32)
    if dbg:
        do_("kdump", (128, 4 * T), BF16)
        do_("vdump", (128, NKT * 512), BF16)

    st.dram = dram
    st.d = {k: v.ap() for k, v in dram.items()}

    with ExitStack() as ctx:
        st.ctx = ctx
        st.cnt = {}
        st.sb = {}
        st.ps = {}
        st.sem = {}
        for n in ("mm", "ex", "vv", "gv") + DMA_SEMS:
            st.sem[n] = ctx.enter_context(nc.semaphore(n))
            st.cnt[n] = 0
        st.block = ctx.enter_context(nc.Block())

        uid = [0]

        def SB(stack, name, shape, dt):
            uid[0] += 1
            st.sb[name] = stack.enter_context(
                nc.sbuf_tensor(f"s{uid[0]}_{name}", list(shape), dt))

        def PS(stack, name, shape, dt):
            uid[0] += 1
            st.ps[name] = stack.enter_context(
                nc.psum_tensor(f"p{uid[0]}_{name}", list(shape), dt))

        for nm, shp, dt in [("identb", (128, 128), BF16), ("identf", (128, 128), F32),
                            ("h128b", (128, 128), BF16), ("onesrow", (1, 128), BF16),
                            ("ones512", (1, 512), BF16), ("onescol", (128, 1), BF16),
                            ("onesrowf", (1, 128), F32)]:
            SB(ctx, nm, shp, dt)

        _emit_preload(st)
        st.tsp = TspPipe(st)

        with ExitStack() as ph012:
            for nm, shp, dt in [("k_fm", (128, 4 * T), BF16),
                                ("v_tm", (128, NKT * 512), BF16),
                                ("tsp0", (128, 4096), BF16),
                                ("tsp1", (128, 4096), BF16)]:
                SB(ph012, nm, shp, dt)
            with ExitStack() as p0:
                for nm, shp, dt in [("wkv", (128, 16 * 1024), BF16), ("bkv", (1, 1024), BF16),
                                    ("bn6", (128, 24), F32), ("mv", (128, 2), F32),
                                    ("vt", (128, 1), F32), ("rstd", (128, 1), F32),
                                    ("rt1", (128, 256), F32), ("rt2", (128, 256), F32)]:
                    SB(p0, nm, shp, dt)
                for i in range(2):
                    for nm, shp, dt in [(f"x{i}", (128, C), F32), (f"xntm{i}", (128, C), BF16),
                                        (f"xnf{i}", (128, C), BF16), (f"ktm{i}", (128, 512), BF16),
                                        (f"ckv{i}", (128, 256), F32), (f"skv{i}", (128, 256), F32)]:
                        SB(p0, nm, shp, dt)
                for nm, shp, dt in [("tp0", (128, 512), BF16), ("tp1", (128, 512), BF16),
                                    ("heat", (128, 64), F32),
                                    ("pk0", (128, 512), F32), ("pk1", (128, 512), F32),
                                    ("pv0", (128, 512), F32), ("pv1", (128, 512), F32)]:
                    PS(p0, nm, shp, dt)
                _emit_gp_weights(st)
                _emit_ph0(st)
            if upto >= 1:
                _barrier(st)
                with ExitStack() as p1:
                    for nm, shp, dt in [("wq", (128, 16 * 1024), BF16), ("bq", (1, C), BF16),
                                        ("bn6", (128, 24), F32), ("mv", (128, 2), F32),
                                        ("vt", (128, 1), F32), ("rstd", (128, 1), F32),
                                        ("rt1", (128, 256), F32), ("rt2", (128, 256), F32)]:
                        SB(p1, nm, shp, dt)
                    for i in range(2):
                        for nm, shp, dt in [(f"xo{i}", (128, C), F32), (f"xntm{i}", (128, C), BF16),
                                            (f"xnf{i}", (128, C), BF16), (f"qst{i}", (128, 1024), BF16),
                                            (f"qfst{i}", (128, 1024), BF16),
                                            (f"co{i}", (128, 256), F32), (f"so{i}", (128, 256), F32)]:
                            SB(p1, nm, shp, dt)
                    for nm, shp, dt in [("tq0", (128, 512), BF16), ("tq1", (128, 512), BF16),
                                        ("heat", (128, 64), F32),
                                        ("pq0", (128, 1024), F32), ("pq1", (128, 1024), F32)]:
                        PS(p1, nm, shp, dt)
                    _emit_ph1(st)
            if upto >= 2:
                _barrier(st)
                with ExitStack() as p2:
                    for nm, shp, dt in [("mask0", (128, 1024), BF16), ("mask1", (128, 1024), BF16),
                                        ("qtl0", (128, 2048), BF16), ("qtl1", (128, 2048), BF16),
                                        ("P0", (128, 512), BF16), ("P1", (128, 512), BF16),
                                        ("P2", (128, 512), BF16),
                                        ("linv", (1, 512), F32), ("bc_sb", (128, 512), F32),
                                        ("avst0", (128, 512), BF16), ("avst1", (128, 512), BF16)]:
                        SB(p2, nm, shp, dt)
                    for nm, shp, dt in [("S0", (128, 512), F32), ("S1", (128, 512), F32),
                                        ("heat", (128, 64), F32),
                                        ("pav0", (128, 512), F32), ("pav1", (128, 512), F32),
                                        ("pl0", (1, 512), F32), ("pl1", (1, 512), F32),
                                        ("pbc", (128, 512), F32)]:
                        PS(p2, nm, shp, dt)
                    _emit_ph2(st)
            if dbg:
                _barrier(st)

                @st.block.gpsimd
                def _(g):
                    DMA(st, g, "gout", st.d["kdump"], st.sb["k_fm"][:])
                    DMA(st, g, "gout", st.d["vdump"], st.sb["v_tm"][:], first=False)
        if upto >= 3:
            _barrier(st)
            with ExitStack() as p3:
                for nm, shp, dt in [("xnall", (128, 16 * NOWN), BF16),
                                    ("wpA", (128, HID), F32), ("wpB", (128, HID), F32),
                                    ("w2xb", (128, HID), BF16), ("w2xo", (128, HID), BF16),
                                    ("sg0", (128, 512), BF16), ("sg1", (128, 512), BF16),
                                    ("sg2", (128, 512), BF16), ("sg3", (128, 512), BF16),
                                    ("w1tb0", (128, 2048), BF16), ("w1tb1", (128, 2048), BF16),
                                    ("w3tb0", (128, 2048), BF16), ("w3tb1", (128, 2048), BF16),
                                    ("hst0", (128, 2048), BF16), ("hst1", (128, 2048), BF16)]:
                    SB(p3, nm, shp, dt)
                for nm, shp, dt in [("pg0", (128, 512), F32), ("pg1", (128, 512), F32),
                                    ("pg2", (128, 512), F32), ("pg3", (128, 512), F32),
                                    ("pu0", (128, 512), F32), ("pu1", (128, 512), F32),
                                    ("tpw0", (128, 512), F32), ("tpw1", (128, 512), F32)]:
                    PS(p3, nm, shp, dt)
                st.w2prep = W2Prep(st)
                _emit_ph3a(st)
        if upto >= 4:
            _barrier(st)
            with ExitStack() as p4:
                for nm, shp, dt in [("hidtb", (128, 64 * 512), BF16),
                                    ("avtb", (128, 16 * 512), BF16),
                                    ("w2xd0", (128, 64 * 128), BF16), ("w2xd1", (128, 64 * 128), BF16),
                                    ("wod0", (128, 16 * 128), BF16), ("wod1", (128, 16 * 128), BF16),
                                    ("pev0", (128, 512), F32), ("pev1", (128, 512), F32),
                                    ("bo", (1, C), BF16)]:
                    SB(p4, nm, shp, dt)
                for q in range(4):
                    SB(p4, f"ost{q}", (128, C), F32)
                    SB(p4, f"xst{q}", (128, C), F32)
                for nm, shp, dt in [("po0", (128, 512), F32), ("po1", (128, 512), F32),
                                    ("heat", (128, 64), F32),
                                    ("ptr0", (128, 512), F32), ("ptr1", (128, 512), F32)]:
                    PS(p4, nm, shp, dt)
                _emit_ph3b(st)
        _emit_final(st)

    return nc


def kernel(L0_attn_x, L0_attn_Wq, L0_attn_bq, L0_attn_Wk, L0_attn_bk,
           L0_attn_Wv, L0_attn_bv, L0_attn_Wo, L0_attn_bo,
           L0_ffn_W1, L0_ffn_W2, L0_ffn_W3):
    x = np.ascontiguousarray(np.asarray(L0_attn_x, np.float32))
    wq = np.ascontiguousarray(np.asarray(L0_attn_Wq, np.float32))
    wk = np.ascontiguousarray(np.asarray(L0_attn_Wk, np.float32))
    wv = np.ascontiguousarray(np.asarray(L0_attn_Wv, np.float32))
    wo = np.ascontiguousarray(np.asarray(L0_attn_Wo, np.float32))
    w1 = np.ascontiguousarray(np.asarray(L0_ffn_W1, np.float32))
    w2 = np.ascontiguousarray(np.asarray(L0_ffn_W2, np.float32))
    w3 = np.ascontiguousarray(np.asarray(L0_ffn_W3, np.float32))
    bq = np.ascontiguousarray(np.asarray(L0_attn_bq, np.float32).reshape(1, C))
    bk = np.ascontiguousarray(np.asarray(L0_attn_bk, np.float32).reshape(1, 512))
    bv = np.ascontiguousarray(np.asarray(L0_attn_bv, np.float32).reshape(1, 512))
    bo = np.ascontiguousarray(np.asarray(L0_attn_bo, np.float32).reshape(1, C))

    cos4, sin4 = _rope_tables()
    consts = dict(
        identb=np.eye(128, dtype=bf16),
        identf=np.eye(128, dtype=np.float32),
        h128=_h128(),
        onesrow=np.ones((1, 128), bf16),
        ones512=np.ones((1, 512), bf16),
        onescol=np.ones((128, 1), bf16),
        onesrowf=np.ones((1, 128), np.float32),
        cos4kv=cos4, sin4kv=sin4,
    )
    masks = [_masks(p) for p in range(2)]

    in_maps = []
    for core in range(8):
        s, p = core // 2, core % 2
        rows = _own_rows(p)
        im = dict(x=np.ascontiguousarray(x[s]),
                  xo=np.ascontiguousarray(x[s][rows]),
                  wq=wq, wk=wk, wv=wv, wo=wo, w1=w1, w2=w2, w3=w3,
                  bq=bq, bk=bk, bv=bv, bo=bo,
                  cos4own=np.ascontiguousarray(cos4[rows]),
                  sin4own=np.ascontiguousarray(sin4[rows]),
                  maskt=masks[p], **consts)
        in_maps.append(im)

    nc = build(upto=4)
    res = run_bass_kernel_spmd(nc, in_maps, core_ids=list(range(8)))

    out = np.empty((B, T, C), np.float32)
    for core in range(8):
        s, p = core // 2, core % 2
        o = np.asarray(res.results[core]["out"], np.float32)
        for i in range(NTL):
            tg = 2 * i + p
            out[s, tg * 128:(tg + 1) * 128, :] = o[i * 128:(i + 1) * 128, :]
    return out
